# version 02
# baseline (speedup 1.0000x reference)
import contextlib
import numpy as np
import ml_dtypes
import concourse.bass as bass
import concourse.mybir as mybir
from concourse.bass_utils import run_bass_kernel_spmd

F32 = mybir.dt.float32
BF16 = mybir.dt.bfloat16
U8 = mybir.dt.uint8
U16 = mybir.dt.uint16
AF = mybir.ActivationFunctionType
ALU = mybir.AluOpType
AX = mybir.AxisListType

NCORES = 8
B = 2
S = 2048
TOK = B * S
D = 2048
DC = D // 128
N_SSM = 8
FF = 8192
FFC = FF // 128
NHEAD = 16
DH = 128
LN_EPS = 1e-5
RSQ_DH = float(1.0 / np.sqrt(128.0))

ENGINES = ("tensor", "vector", "scalar", "gpsimd", "sync")



class Buf:
    __slots__ = ("uid", "nsub", "war", "is_psum")
    _next = [0]

    def __init__(self, war=()):
        self.uid = str(Buf._next[0])
        Buf._next[0] += 1
        self.nsub = set()
        self.war = set(war)
        self.is_psum = False

    def k(self, i=None):
        key = self.uid if i is None else f"{self.uid}/{i}"
        self.nsub.add(key)
        return key


class _BufState:
    __slots__ = ("writer", "readers")

    def __init__(self):
        self.writer = None
        self.readers = []


class Arena:
    def __init__(self, bld, nbytes):
        self.bld = bld
        self.t = bld.stack.enter_context(
            bld.nc.sbuf_tensor("arena", [128, nbytes], U8))
        self.free_list = [(0, nbytes, set())]

    def alloc(self, shape, dtype, align=64):
        nbytes = int(np.prod(shape[1:])) * mybir.dt.size(dtype)
        nbytes = (nbytes + align - 1) & ~(align - 1)
        for i, (off, size, ghosts) in enumerate(self.free_list):
            aoff = (off + align - 1) & ~(align - 1)
            pad = aoff - off
            if size >= nbytes + pad:
                del self.free_list[i]
                if pad:
                    self.free_list.insert(i, (off, pad, set(ghosts)))
                if size - pad - nbytes:
                    self.free_list.insert(i + (1 if pad else 0),
                                          (aoff + nbytes,
                                           size - pad - nbytes, set(ghosts)))
                ap = self.t[:, aoff:aoff + nbytes].bitcast(dtype)
                n = int(np.prod(shape[1:]))
                ap = ap[0:shape[0], :n]
                if len(shape) == 3:
                    ap = ap.rearrange("p (a b) -> p a b", a=shape[1])
                elif len(shape) == 4:
                    ap = ap.rearrange("p (a b c) -> p a b c", a=shape[1],
                                      b=shape[2])
                elif len(shape) != 2:
                    raise ValueError(shape)
                buf = Buf(war=ghosts)
                self.bld.bufreg[buf.uid] = buf
                return ArenaAlloc(self, ap, buf, aoff, nbytes)
        raise MemoryError(f"arena alloc {shape} {dtype} ({nbytes}B): "
                          f"free={[(o, s) for o, s, _ in self.free_list]}")


class ArenaAlloc:
    def __init__(self, arena, ap, buf, off, nbytes):
        self.arena = arena
        self.ap = ap
        self.buf = buf
        self.off = off
        self.nbytes = nbytes

    def k(self, i=None):
        return self.buf.k(i)

    def free(self):
        self.arena.free_list.append((self.off, self.nbytes,
                                     set(self.buf.nsub)))
        fl = sorted(self.arena.free_list)
        merged = []
        for off, size, gh in fl:
            if merged and merged[-1][0] + merged[-1][1] == off:
                merged[-1] = (merged[-1][0], merged[-1][1] + size,
                              merged[-1][2] | gh)
            else:
                merged.append((off, size, set(gh)))
        self.arena.free_list = merged


class PsumAlloc:
    def __init__(self, bld, ap, buf, banks):
        self.bld = bld
        self.ap = ap
        self.buf = buf
        self.banks = banks

    def k(self, i=None):
        return self.buf.k(i)

    def free(self):
        self.bld._psum_seq[0] += 1
        for bk in self.banks:
            self.bld._psum_free.append(bk)
            self.bld._psum_ghost[bk] = set(self.buf.nsub)
            self.bld._psum_age[bk] = self.bld._psum_seq[0]


class Bld:
    def __init__(self, fused=True):
        self.nc = bass.Bass("TRN2", target_bir_lowering=False)
        self.fused = fused
        self.stack = contextlib.ExitStack()
        self.ops = []
        self.cnt = {e: 0 for e in ENGINES}
        self.NDMA = 8
        for i in range(self.NDMA):
            self.cnt[f"dma{i}"] = 0
        self.dma_idx = 0
        self.state = {}
        self.last_waited = {e: {} for e in ENGINES}
        self.dram = {}
        self.bufreg = {}
        self._psum_free = list(range(8))
        self._psum_ghost = {}
        self._psum_age = {i: 0 for i in range(8)}
        self._psum_seq = [0]
        self._uniq = [0]
        self._psum_all = self.stack.enter_context(
            self.nc.psum_tensor("psum_all", [128, 8 * 512], F32))
        self.arena = None

    def mk_arena(self, nbytes):
        self.arena = Arena(self, nbytes)

    def din(self, name, shape, dtype):
        ap = self.nc.dram_tensor(name, list(shape), dtype,
                                 kind="ExternalInput").ap()
        self.dram[name] = ap
        return ap

    def dout(self, name, shape, dtype):
        ap = self.nc.dram_tensor(name, list(shape), dtype,
                                 kind="ExternalOutput").ap()
        self.dram[name] = ap
        return ap

    def dscr(self, name, shape, dtype):
        ap = self.nc.dram_tensor(name, list(shape), dtype, kind="Internal").ap()
        self.dram[name] = ap
        return ap

    def _name(self, pfx):
        self._uniq[0] += 1
        return f"{pfx}_{self._uniq[0]}"

    def sb(self, shape, dtype, pfx="sb"):
        h = self.stack.enter_context(
            self.nc.sbuf_tensor(self._name(pfx), list(shape), dtype))
        b = Buf()
        self.bufreg[b.uid] = b
        return h, b

    def psum(self, nbank=1):
        fr = sorted(self._psum_free)
        runs = []
        for i in range(len(fr) - nbank + 1):
            if fr[i + nbank - 1] - fr[i] == nbank - 1:
                runs.append(fr[i:i + nbank])
        assert runs, f"psum alloc {nbank} failed; free={fr}"
        run = min(runs, key=lambda r: max(self._psum_age[b] for b in r))
        for bk in run:
            self._psum_free.remove(bk)
        ap = self._psum_all[:, run[0] * 512:(run[0] + nbank) * 512]
        war = set()
        for bk in run:
            war |= self._psum_ghost.pop(bk, set())
        buf = Buf(war=war)
        buf.is_psum = True
        self.bufreg[buf.uid] = buf
        return PsumAlloc(self, ap, buf, run)

    def _need(self, waits, ev):
        sem, val = ev
        if waits.get(sem, 0) < val:
            waits[sem] = val

    def emit(self, engine, fn, reads=(), writes=(), extra_waits=(), dma=False):
        waits = {}
        my_sem = engine
        for k in reads:
            st = self.state.get(k)
            if st and st.writer:
                self._need(waits, st.writer)
            if st is not None and not dma:
                buf = self.bufreg.get(k.split("/")[0])
                if buf is not None and buf.is_psum:
                    for r in st.readers:
                        if r[0] != my_sem:
                            self._need(waits, r)
        for k in writes:
            st = self.state.get(k)
            if st:
                for r in st.readers:
                    self._need(waits, r)
                if st.writer:
                    self._need(waits, st.writer)
            buf = self.bufreg.get(k.split("/")[0])
            if buf is not None and buf.war:
                for wk in buf.war:
                    wst = self.state.get(wk)
                    if wst:
                        for r in wst.readers:
                            self._need(waits, r)
                        if wst.writer:
                            self._need(waits, wst.writer)
                buf.war = set()
        for ev in extra_waits:
            if ev is not None:
                self._need(waits, ev)
        if dma:
            j = self.dma_idx
            self.dma_idx += 1
            ev_sem = f"dma{j % self.NDMA}"
            occ = j // self.NDMA
            if occ > 0:
                self._need(waits, (ev_sem, 16 * occ))
        else:
            ev_sem = engine
        lw = self.last_waited[engine]
        pruned = {}
        for sem, val in waits.items():
            if lw.get(sem, 0) >= val:
                continue
            pruned[sem] = val
            lw[sem] = val
        self.cnt[ev_sem] += 16 if dma else 1
        ev = (ev_sem, self.cnt[ev_sem])
        self.ops.append((engine, pruned, fn, (ev_sem, 16 if dma else 1)))
        for k in writes:
            st = self.state.setdefault(k, _BufState())
            st.writer = ev
            st.readers = []
        for k in reads:
            st = self.state.setdefault(k, _BufState())
            st.readers.append(ev)
        return ev

    def dma(self, out, in_, reads=(), writes=(), extra_waits=()):
        return self.emit("sync", lambda e: e.dma_start(out=out, in_=in_),
                         reads=reads, writes=writes, extra_waits=extra_waits,
                         dma=True)

    def finalize(self, final_events=()):
        nc = self.nc
        sems = {}
        with contextlib.ExitStack() as es:
            for name in list(self.cnt.keys()):
                sems[name] = es.enter_context(nc.semaphore(f"sem_{name}"))
            blk = es.enter_context(nc.Block())
            per_engine = {e: [] for e in ENGINES}
            for (engine, waits, fn, inc) in self.ops:
                per_engine[engine].append((waits, fn, inc))

            def make(engine):
                def body(e):
                    for waits, fn, inc in per_engine[engine]:
                        for sem, val in waits.items():
                            e.wait_ge(sems[sem], val)
                        inst = fn(e)
                        sem, amt = inc
                        assert inst is not None, f"op on {engine} -> None"
                        inst.then_inc(sems[sem], amt)
                    if engine == "sync":
                        for sem, val in final_events:
                            e.wait_ge(sems[sem], val)
                return body

            for engine in ENGINES:
                if per_engine[engine] or engine == "sync":
                    getattr(blk, engine)(make(engine))
        self.stack.close()
        return nc

    def outputs_done(self):
        return [(f"dma{i}", self.cnt[f"dma{i}"]) for i in range(self.NDMA)
                if self.cnt[f"dma{i}"] > 0]



def _slabify(W, fs=128):
    K, F = W.shape
    KC = K // 128
    assert K % 128 == 0 and F % fs == 0
    return np.ascontiguousarray(
        W.reshape(KC, 128, F // fs, fs).transpose(2, 1, 0, 3))


def _perm24():
    i = np.arange(FF)
    g, j = i // 4, i % 4
    ip = j * (FF // 4) + g
    inv = np.empty(FF, dtype=np.int64)
    inv[ip] = i
    return inv


def _chunkvec(v):
    return np.ascontiguousarray(v.reshape(DC, 128).T.astype(np.float32))


def prep_weights(inp):
    P24 = _perm24()
    asc = np.ascontiguousarray
    w = {}
    xp = np.zeros((D, 384), dtype=np.float32)
    xp[:, :272] = inp["L0_attn_W_xproj"].T
    w["w_xproj_s"] = _slabify(xp)
    w["w_dt_s"] = _slabify(asc(inp["L0_attn_W_dt"].T.astype(np.float32)))
    w["b_dt_c"] = _chunkvec(inp["L0_attn_b_dt"])
    w["a_log_c"] = asc(inp["L0_attn_A_log"].reshape(DC, 128, N_SSM)
                       .transpose(1, 0, 2).reshape(128, DC * N_SSM)
                       .astype(np.float32))
    w["d_param_c"] = _chunkvec(inp["L0_attn_D_param"])
    w["w_gate_s"] = _slabify(asc(inp["L0_ffn_w_gate"][:, P24]))
    w["w_up_s"] = _slabify(asc(inp["L0_ffn_w_up"][:, P24]))
    w["w_down_s"] = _slabify(asc(inp["L0_ffn_w_down"][P24, :]))
    for nm, key in (("wq", "L1_attn_Wq"), ("wk", "L1_attn_Wk"),
                    ("wv", "L1_attn_Wv"), ("wo", "L1_attn_Wo")):
        w[nm + "_s"] = _slabify(asc(inp[key].T))
    w["bq_c"] = _chunkvec(inp["L1_attn_bq"])
    w["bk_c"] = _chunkvec(inp["L1_attn_bk"])
    w["bv_c"] = _chunkvec(inp["L1_attn_bv"])
    w["bo_c"] = _chunkvec(inp["L1_attn_bo"])
    w["sbias_c"] = _chunkvec(inp["L1_ffn_simple_bias"])
    for nm, key in (("iwq", "L1_ffn_wq"), ("iwk", "L1_ffn_wk"),
                    ("iwv", "L1_ffn_wv"), ("iwo", "L1_ffn_wo")):
        w[nm + "_s"] = _slabify(asc(inp[key].T))
    w["wff1_s"] = _slabify(asc(inp["L1_ffn_w_ff1"].T))
    w["wff2_s"] = _slabify(asc(inp["L1_ffn_w_ff2"].T))
    w["ln1s_c"] = _chunkvec(inp["L1_ffn_ln1_scale"])
    w["ln1b_c"] = _chunkvec(inp["L1_ffn_ln1_bias"])
    w["ln2s_c"] = _chunkvec(inp["L1_ffn_ln2_scale"])
    w["ln2b_c"] = _chunkvec(inp["L1_ffn_ln2_bias"])
    rt = np.zeros((D, 128), dtype=np.float32)
    rt[:, 0] = inp["L1_ffn_w_router"][0]
    rt[:, 32] = inp["L1_ffn_w_router"][1]
    rt[:, 64] = inp["L1_ffn_w_router"][2]
    w["w_router_s"] = _slabify(rt)
    lrd = np.zeros((D, 128), dtype=np.float32)
    lrd[:, :32] = inp["L1_ffn_w_down"].T
    w["w_lrdown_s"] = _slabify(lrd)
    lru = np.zeros((128, D), dtype=np.float32)
    lru[:32, :] = inp["L1_ffn_w_up"].T
    w["w_lrup_s"] = _slabify(lru)
    w["ident"] = np.eye(128, dtype=np.float32)
    sel = np.zeros((16, 16 * 128), dtype=np.float32)
    for n in range(16):
        sel[n, n * 128:(n + 1) * 128] = 1.0
    w["sel16"] = asc(sel)
    cm = (np.arange(3072)[None, :] >= (np.arange(128)[:, None] + 1920))
    w["causal_m"] = asc(cm.astype(ml_dtypes.bfloat16))
    return w


WEIGHT_SPECS = {
    "w_xproj_s": ([3, 128, 16, 128], F32),
    "w_dt_s": ([16, 128, 2, 128], F32),
    "b_dt_c": ([128, 16], F32),
    "a_log_c": ([128, 128], F32),
    "d_param_c": ([128, 16], F32),
    "w_gate_s": ([64, 128, 16, 128], F32),
    "w_up_s": ([64, 128, 16, 128], F32),
    "w_down_s": ([16, 128, 64, 128], F32),
    "wq_s": ([16, 128, 16, 128], F32),
    "wk_s": ([16, 128, 16, 128], F32),
    "wv_s": ([16, 128, 16, 128], F32),
    "wo_s": ([16, 128, 16, 128], F32),
    "bq_c": ([128, 16], F32),
    "bk_c": ([128, 16], F32),
    "bv_c": ([128, 16], F32),
    "bo_c": ([128, 16], F32),
    "sbias_c": ([128, 16], F32),
    "iwq_s": ([16, 128, 16, 128], F32),
    "iwk_s": ([16, 128, 16, 128], F32),
    "iwv_s": ([16, 128, 16, 128], F32),
    "iwo_s": ([16, 128, 16, 128], F32),
    "wff1_s": ([64, 128, 16, 128], F32),
    "wff2_s": ([16, 128, 64, 128], F32),
    "ln1s_c": ([128, 16], F32),
    "ln1b_c": ([128, 16], F32),
    "ln2s_c": ([128, 16], F32),
    "ln2b_c": ([128, 16], F32),
    "w_router_s": ([1, 128, 16, 128], F32),
    "w_lrdown_s": ([1, 128, 16, 128], F32),
    "w_lrup_s": ([16, 128, 1, 128], F32),
    "ident": ([128, 128], F32),
    "sel16": ([16, 2048], F32),
    "causal_m": ([128, 3072], BF16),
}

BF_WEIGHTS = {
    "w_xproj_s": (3, 16), "w_dt_s": (16, 2),
    "w_gate_s": (64, 16), "w_up_s": (64, 16), "w_down_s": (16, 64),
    "wq_s": (16, 16), "wk_s": (16, 16), "wv_s": (16, 16), "wo_s": (16, 16),
    "iwq_s": (16, 16), "iwk_s": (16, 16), "iwv_s": (16, 16), "iwo_s": (16, 16),
    "wff1_s": (64, 16), "wff2_s": (16, 64),
    "w_router_s": (1, 16), "w_lrdown_s": (1, 16), "w_lrup_s": (16, 1),
}


def ph_wcast(B_, names):
    stg = B_.arena.alloc([128, 4096], F32)
    ob = B_.arena.alloc([128, 4096], BF16)
    for nm in names:
        nslab, KCW = BF_WEIGHTS[nm]
        src = B_.dram[nm]
        dst = B_.dram["bf_" + nm]
        cols = KCW * 128
        for fi in range(nslab):
            for s0 in range(0, cols, 4096):
                w = min(4096, cols - s0)
                sv = src[fi].rearrange("p a b -> p (a b)")[:, s0:s0 + w]
                dv = dst[fi].rearrange("p a b -> p (a b)")[:, s0:s0 + w]
                B_.dma(stg.ap[:, :w], sv, writes=[stg.k()])
                B_.emit("vector", lambda e, w=w: e.tensor_copy(
                    ob.ap[:, :w], stg.ap[:, :w]),
                    reads=[stg.k()], writes=[ob.k()])
                B_.dma(dv, ob.ap[:, :w], reads=[ob.k()],
                       writes=[f"bfw_{nm}/{fi}"])
    stg.free()
    ob.free()


def declare_weights(B_, bf_names=()):
    for nm, (shape, dtype) in WEIGHT_SPECS.items():
        B_.din(nm, shape, dtype)
    for nm in bf_names:
        shape = WEIGHT_SPECS[nm][0]
        B_.dscr("bf_" + nm, shape, BF16)



class Globals:
    def __init__(self, B_):
        self.ones_bf, self.b_ones = B_.sb([128, 128], BF16, "ones")
        B_.emit("vector", lambda e: e.memset(self.ones_bf[:], 1.0),
                writes=[self.b_ones.k()])
        self.ident, self.b_ident = B_.sb([128, 128], F32, "ident")
        B_.dma(self.ident[:], B_.dram["ident"], writes=[self.b_ident.k()])
        self.identb, self.b_identb = B_.sb([128, 128], BF16, "identb")
        B_.emit("vector", lambda e: e.tensor_copy(self.identb[:], self.ident[:]),
                reads=[self.b_ident.k()], writes=[self.b_identb.k()])
        self.vecs = {}
        for nm in ("b_dt_c", "d_param_c", "bq_c", "bk_c", "bv_c", "bo_c",
                   "sbias_c", "ln1s_c", "ln1b_c", "ln2s_c", "ln2b_c"):
            h, b = B_.sb([128, 16], F32, nm)
            B_.dma(h[:], B_.dram[nm], writes=[b.k()])
            self.vecs[nm] = (h, b)
        self.A, self.b_A = B_.sb([128, 128], F32, "A")
        B_.dma(self.A[:], B_.dram["a_log_c"], writes=[self.b_A.k()])
        B_.emit("scalar", lambda e: e.activation(self.A[:], self.A[:], AF.Exp),
                reads=[self.b_A.k()], writes=[self.b_A.k()])
        B_.emit("vector", lambda e: e.tensor_scalar(
            self.A[:], self.A[:], -1.0, None, ALU.mult),
            reads=[self.b_A.k()], writes=[self.b_A.k()])
        self.sel16, self.b_sel16 = B_.sb([16, 16 * 128], BF16, "sel16")
        self.cpool, self.b_cpool = B_.sb([128, 8], F32, "cpool")
        self._cmap = {}
        self._B = B_

    def cbias(self, value):
        if value not in self._cmap:
            col = len(self._cmap)
            assert col < 8
            self._B.emit("vector",
                         lambda e, col=col, value=value: e.memset(
                             self.cpool[:, col:col + 1], value),
                         writes=[self.b_cpool.k(col)])
            self._cmap[value] = col
        col = self._cmap[value]
        return self.cpool[:, col:col + 1], self.b_cpool.k(col)

    def load_sel16(self, B_):
        tmp = B_.arena.alloc([16, 2048], F32)
        B_.dma(tmp.ap[:], B_.dram["sel16"], writes=[tmp.k()])
        B_.emit("vector", lambda e: e.tensor_copy(self.sel16[:], tmp.ap[:]),
                reads=[tmp.k()], writes=[self.b_sel16.k()])
        tmp.free()

    def vec(self, nm, c):
        h, b = self.vecs[nm]
        return h[:, c:c + 1], b.k()


def load_xn(B_, xn_dram, tag):
    al = B_.arena.alloc([128, DC, TOK], BF16)
    for c in range(DC):
        B_.dma(al.ap[:, c], xn_dram[c], reads=[f"{tag}/{c}"],
               writes=[al.k(c)])
    return al


def proj_resident(B_, wname, nslab, KCW, rhs_ap, rhs_keys, evac, order=None,
                  slab_extra_reads=None):
    wdram = B_.dram["bf_" + wname]
    if order is None:
        order = list(range(nslab))
    wb = [B_.arena.alloc([128, KCW, 128], BF16) for _ in range(2)]
    for idx, fi in enumerate(order):
        w = wb[idx % 2]
        B_.dma(w.ap[:], wdram[fi],
               reads=[f"bfw_{wname}/{fi}"], writes=[w.k()])
        for half in range(2):
            pa = B_.psum(4)

            def mmfn(e, w=w, pa=pa, half=half):
                last = None
                for kc in range(KCW):
                    for t4 in range(4):
                        last = e.matmul(
                            pa.ap[:, t4 * 512:(t4 + 1) * 512], w.ap[:, kc],
                            rhs_ap[:, kc, half * 2048 + t4 * 512:
                                   half * 2048 + (t4 + 1) * 512],
                            start=(kc == 0), stop=(kc == KCW - 1))
                return last
            B_.emit("tensor", mmfn,
                    reads=[w.k()] + list(rhs_keys) +
                          (slab_extra_reads(fi) if slab_extra_reads else []),
                    writes=[pa.k()])
            evac(fi, half, pa, [pa.k()])
            pa.free()
    for w in wb:
        w.free()



def ph_ln(B_, G, z_dram, xn_dram=None, z_is_bf16=False,
          scale_nm=None, bias_nm=None, resid_dram=None, out_dram=None,
          out_tag=None, z_tag=None, resid_tag=None):
    htok = TOK // 2
    stats = B_.arena.alloc([128, 2, TOK], F32)
    zstg = [B_.arena.alloc([128, 2048], BF16 if z_is_bf16 else F32)
            for _ in range(2)]
    zb = None if z_is_bf16 else [B_.arena.alloc([128, 2048], BF16)
                                 for _ in range(2)]
    z2 = [B_.arena.alloc([128, 2048], BF16) for _ in range(2)]
    for half in range(2):
        pa_s = B_.psum(4)
        pa_q = B_.psum(4)
        for c in range(DC):
            par = c % 2
            stg = zstg[par]
            B_.dma(stg.ap[:], z_dram[c, :, half * htok:(half + 1) * htok],
                   reads=[f"{z_tag}/{c}"] if z_tag else [],
                   writes=[stg.k()])
            if z_is_bf16:
                zb_ap, zb_k = stg.ap, stg.k()
            else:
                B_.emit("vector", lambda e, stg=stg, par=par: e.tensor_copy(
                    zb[par].ap[:], stg.ap[:]),
                    reads=[stg.k()], writes=[zb[par].k()])
                zb_ap, zb_k = zb[par].ap, zb[par].k()
            B_.emit("scalar", lambda e, stg=stg, par=par: e.activation(
                z2[par].ap[:], stg.ap[:], AF.Square),
                reads=[stg.k()], writes=[z2[par].k()])

            def mk_mm(src_ap, pa, c):
                def mms(e):
                    last = None
                    for t4 in range(4):
                        last = e.matmul(
                            pa.ap[:, t4 * 512:(t4 + 1) * 512], G.ones_bf[:],
                            src_ap[:, t4 * 512:(t4 + 1) * 512],
                            start=(c == 0), stop=(c == DC - 1))
                    return last
                return mms
            B_.emit("tensor", mk_mm(zb_ap, pa_s, c),
                    reads=[zb_k, G.b_ones.k()],
                    writes=[pa_s.k()] if c in (0, DC - 1) else [])
            B_.emit("tensor", mk_mm(z2[par].ap, pa_q, c),
                    reads=[z2[par].k(), G.b_ones.k()],
                    writes=[pa_q.k()] if c in (0, DC - 1) else [])
        hsl = slice(half * htok, (half + 1) * htok)
        m_b = B_.arena.alloc([128, 2048], F32)
        msq = B_.arena.alloc([128, 2048], F32)
        vpe = B_.arena.alloc([128, 2048], F32)
        B_.emit("vector", lambda e, m_b=m_b, pa=pa_s: e.tensor_scalar(
            m_b.ap[:], pa.ap[:, 0:2048], 1.0 / D, None, ALU.mult),
            reads=[pa_s.k()], writes=[m_b.k()])
        B_.emit("vector", lambda e, m_b=m_b, msq=msq: e.tensor_mul(
            msq.ap[:], m_b.ap[:], m_b.ap[:]),
            reads=[m_b.k()], writes=[msq.k()])
        B_.emit("vector", lambda e, vpe=vpe, msq=msq, pa=pa_q:
                e.scalar_tensor_tensor(vpe.ap[:], pa.ap[:, 0:2048], 1.0 / D,
                                       msq.ap[:], ALU.mult, ALU.subtract),
                reads=[pa_q.k(), msq.k()], writes=[vpe.k()])
        eps_ap, eps_k = G.cbias(LN_EPS)
        B_.emit("scalar", lambda e, vpe=vpe, eps_ap=eps_ap: e.activation(
            vpe.ap[:], vpe.ap[:], AF.Ln, bias=eps_ap),
            reads=[vpe.k(), eps_k], writes=[vpe.k()])
        B_.emit("scalar", lambda e, vpe=vpe, hsl=hsl: e.activation(
            stats.ap[:, 0, hsl], vpe.ap[:], AF.Exp, scale=-0.5),
            reads=[vpe.k()], writes=[stats.k(f"r{half}")])
        B_.emit("vector", lambda e, m_b=m_b, hsl=hsl: e.tensor_mul(
            stats.ap[:, 1, hsl], m_b.ap[:], stats.ap[:, 0, hsl]),
            reads=[m_b.k(), stats.k(f"r{half}")],
            writes=[stats.k(f"m{half}")])
        pa_s.free()
        pa_q.free()
        m_b.free()
        msq.free()
        vpe.free()
    for a in zstg + (zb or []) + z2:
        a.free()

    stat_reads = [stats.k("r0"), stats.k("r1"), stats.k("m0"), stats.k("m1")]
    zfull = [B_.arena.alloc([128, TOK], BF16 if z_is_bf16 else F32)
             for _ in range(2)]
    t1 = [B_.arena.alloc([128, TOK], F32) for _ in range(2)]
    xn_t = [B_.arena.alloc([128, TOK], BF16) for _ in range(2)] \
        if xn_dram is not None else None
    xm = [B_.arena.alloc([128, TOK], F32) for _ in range(2)] \
        if scale_nm is not None else None
    xs = [B_.arena.alloc([128, TOK], F32) for _ in range(2)] \
        if resid_dram is not None else None
    xo = [B_.arena.alloc([128, TOK], F32) for _ in range(2)] \
        if out_dram is not None else None
    for c in range(DC):
        par = c % 2
        stg = zfull[par]
        B_.dma(stg.ap[:], z_dram[c], reads=[f"{z_tag}/{c}"] if z_tag else [],
               writes=[stg.k()])
        eng = "vector" if (c % 3) != 2 else "gpsimd"
        B_.emit(eng, lambda e, stg=stg, par=par: e.tensor_mul(
            t1[par].ap[:], stg.ap[:], stats.ap[:, 0]),
            reads=[stg.k()] + stat_reads, writes=[t1[par].k()])
        if xn_dram is not None:
            if scale_nm is None:
                B_.emit(eng, lambda e, par=par: e.tensor_sub(
                    xn_t[par].ap[:], t1[par].ap[:], stats.ap[:, 1]),
                    reads=[t1[par].k()] + stat_reads, writes=[xn_t[par].k()])
            else:
                B_.emit(eng, lambda e, par=par: e.tensor_sub(
                    xm[par].ap[:], t1[par].ap[:], stats.ap[:, 1]),
                    reads=[t1[par].k()] + stat_reads, writes=[xm[par].k()])
                sap, sk = G.vec(scale_nm, c)
                bap, bk = G.vec(bias_nm, c)
                B_.emit("vector", lambda e, par=par, sap=sap, bap=bap:
                        e.tensor_scalar(xn_t[par].ap[:], xm[par].ap[:], sap,
                                        bap, ALU.mult, ALU.add),
                        reads=[xm[par].k(), sk, bk], writes=[xn_t[par].k()])
            B_.dma(xn_dram[c], xn_t[par].ap[:], reads=[xn_t[par].k()],
                   writes=[f"{out_tag}/{c}"])
        else:
            B_.dma(xs[par].ap[:], resid_dram[c],
                   reads=[f"{resid_tag}/{c}"] if resid_tag else [],
                   writes=[xs[par].k()])
            B_.emit(eng, lambda e, par=par: e.tensor_sub(
                xo[par].ap[:], t1[par].ap[:], stats.ap[:, 1]),
                reads=[t1[par].k()] + stat_reads, writes=[xo[par].k()])
            B_.emit(eng, lambda e, par=par: e.tensor_add(
                xo[par].ap[:], xo[par].ap[:], xs[par].ap[:]),
                reads=[xo[par].k(), xs[par].k()], writes=[xo[par].k()])
            B_.dma(out_dram[c], xo[par].ap[:], reads=[xo[par].k()],
                   writes=[f"{out_tag}/{c}"])
    for a in zfull + t1 + (xn_t or []) + (xm or []) + (xs or []) + (xo or []):
        a.free()
    stats.free()



def ph_entry(B_, G, x0, xres):
    GRP = 4
    NG = (TOK // 128) // GRP
    tin = B_.arena.alloc([128, GRP, D], F32)
    tout = B_.arena.alloc([128, DC, GRP * 128], F32)
    for g in range(NG):
        for r in range(GRP):
            rc = g * GRP + r
            B_.dma(tin.ap[:, r], x0[rc * 128:(rc + 1) * 128, :],
                   writes=[tin.k(r)])
        for c in range(DC):
            pa = B_.psum(1)

            def tfn(e, c=c, pa=pa):
                last = None
                for r in range(GRP):
                    last = e.transpose(pa.ap[:, r * 128:(r + 1) * 128],
                                       tin.ap[:, r, c * 128:(c + 1) * 128],
                                       G.ident[:])
                return last
            B_.emit("tensor", tfn,
                    reads=[tin.k(r) for r in range(GRP)] + [G.b_ident.k()],
                    writes=[pa.k()])
            B_.emit("vector", lambda e, c=c, pa=pa: e.tensor_copy(
                tout.ap[:, c], pa.ap[:, 0:GRP * 128]),
                reads=[pa.k()], writes=[tout.k(c)])
            pa.free()
        for c in range(DC):
            B_.dma(xres[c, :, g * GRP * 128:(g + 1) * GRP * 128],
                   tout.ap[:, c], reads=[tout.k(c)], writes=[f"xres0/{c}"])
    tin.free()
    tout.free()


def ph_exit(B_, G, xres, xout, z_tag):
    GRP = 4
    NG = DC // GRP
    tin = B_.arena.alloc([128, GRP, TOK], F32)
    tout = B_.arena.alloc([128, TOK // 128, GRP * 128], F32)
    for g in range(NG):
        for r in range(GRP):
            c = g * GRP + r
            B_.dma(tin.ap[:, r], xres[c], reads=[f"{z_tag}/{c}"],
                   writes=[tin.k(r)])
        for rc in range(TOK // 128):
            pa = B_.psum(1)

            def tfn(e, rc=rc, pa=pa):
                last = None
                for r in range(GRP):
                    last = e.transpose(pa.ap[:, r * 128:(r + 1) * 128],
                                       tin.ap[:, r, rc * 128:(rc + 1) * 128],
                                       G.ident[:])
                return last
            B_.emit("tensor", tfn,
                    reads=[tin.k(r) for r in range(GRP)] + [G.b_ident.k()],
                    writes=[pa.k()])
            B_.emit("vector", lambda e, rc=rc, pa=pa: e.tensor_copy(
                tout.ap[:, rc], pa.ap[:, 0:GRP * 128]),
                reads=[pa.k()], writes=[tout.k(rc)])
            pa.free()
        for rc in range(TOK // 128):
            B_.dma(xout[rc * 128:(rc + 1) * 128,
                        g * GRP * 128:(g + 1) * GRP * 128],
                   tout.ap[:, rc], reads=[tout.k(rc)], writes=["xout_done"])
    tin.free()
    tout.free()



def ph_l0_mixer(B_, G, xn_dram, y_dram, xn_tag):
    G.load_sel16(B_)
    sub = MIXER_SUB[0]
    xn = load_xn(B_, xn_dram, xn_tag)
    xn_keys = [xn.k(c) for c in range(DC)]
    dtr = B_.arena.alloc([128, 2, TOK], BF16)
    bcsb = B_.arena.alloc([16, TOK], BF16)

    def evac_xp(fi, half, pa, reads):
        hsl = slice(half * 2048, (half + 1) * 2048)
        if fi < 2:
            B_.emit("vector", lambda e, fi=fi, hsl=hsl, pa=pa: e.tensor_copy(
                dtr.ap[:, fi, hsl], pa.ap[:, 0:2048]),
                reads=reads, writes=[dtr.k(fi)])
        else:
            B_.emit("vector", lambda e, hsl=hsl, pa=pa: e.tensor_copy(
                bcsb.ap[:, hsl], pa.ap[0:16, 0:2048]),
                reads=reads, writes=[bcsb.k(half)])
    if sub != 24:
        proj_resident(B_, "w_xproj_s", 3, DC, xn.ap, xn_keys, evac_xp)
    if sub <= 1:
        zt = B_.arena.alloc([128, TOK], F32)
        for c in range(DC):
            if c < 2:
                B_.emit("vector", lambda e, c=c: e.tensor_copy(
                    zt.ap[:], dtr.ap[:, c]), reads=[dtr.k(c)], writes=[zt.k()])
            else:
                B_.emit("vector", lambda e: e.memset(zt.ap[:], 0.0),
                        writes=[zt.k()])
            B_.dma(y_dram[c], zt.ap[:], reads=[zt.k()], writes=[f"l0_y/{c}"])
        zt.free()
        xn.free()
        dtr.free()
        bcsb.free()
        return

    delta_dram = B_.dscr("l0_delta", [DC, 128, TOK], BF16)
    u_dram = B_.dscr("l0_u", [DC, 128, TOK], BF16)
    dl_t = B_.arena.alloc([128, TOK], BF16)
    u_t = B_.arena.alloc([128, TOK], BF16)
    ab_t = B_.arena.alloc([128, 2048], F32)
    lg_t = B_.arena.alloc([128, 2048], F32)
    mz_t = B_.arena.alloc([128, 2048], F32)

    def evac_dt(fi, half, pa, reads):
        c = fi
        hsl = slice(half * 2048, (half + 1) * 2048)
        bap, bk = G.vec("b_dt_c", c)
        if MIXER_SUB[0] in (21, 24):
            B_.emit("vector", lambda e, pa=pa, hsl=hsl: e.tensor_copy(
                dl_t.ap[:, hsl], pa.ap[:, 0:2048]),
                reads=reads, writes=[dl_t.k()])
            if half == 1:
                B_.dma(delta_dram[c], dl_t.ap[:], reads=[dl_t.k()],
                       writes=[f"l0_delta/{c}"])
                B_.dma(u_dram[c], dl_t.ap[:], reads=[dl_t.k()],
                       writes=[f"l0_u/{c}"])
            return
        if MIXER_SUB[0] == 22:
            B_.emit("scalar", lambda e, pa=pa, bap=bap: e.activation(
                ab_t.ap[:], pa.ap[:, 0:2048], AF.Abs, bias=bap),
                reads=reads + [bk], writes=[ab_t.k()])
            B_.emit("scalar", lambda e: e.activation(
                ab_t.ap[:], ab_t.ap[:], AF.Exp, scale=-1.0),
                reads=[ab_t.k()], writes=[ab_t.k()])
            B_.emit("scalar", lambda e: e.activation(
                lg_t.ap[:], ab_t.ap[:], AF.Ln, bias=1.0),
                reads=[ab_t.k()], writes=[lg_t.k()])
            B_.emit("vector", lambda e, pa=pa, bap=bap: e.tensor_scalar(
                mz_t.ap[:], pa.ap[:, 0:2048], bap, 0.0, ALU.add, ALU.max),
                reads=reads + [bk], writes=[mz_t.k()])
            B_.emit("vector", lambda e, hsl=hsl: e.tensor_add(
                dl_t.ap[:, hsl], mz_t.ap[:], lg_t.ap[:]),
                reads=[mz_t.k(), lg_t.k()], writes=[dl_t.k()])
            if half == 1:
                B_.dma(delta_dram[c], dl_t.ap[:], reads=[dl_t.k()],
                       writes=[f"l0_delta/{c}"])
                B_.dma(u_dram[c], dl_t.ap[:], reads=[dl_t.k()],
                       writes=[f"l0_u/{c}"])
            return
        ev_abs = B_.emit("scalar", lambda e, pa=pa, bap=bap: e.activation(
            ab_t.ap[:], pa.ap[:, 0:2048], AF.Abs, bias=bap),
            reads=reads + [bk], writes=[ab_t.k()])
        B_.emit("scalar", lambda e: e.activation(
            ab_t.ap[:], ab_t.ap[:], AF.Exp, scale=-1.0),
            reads=[ab_t.k()], writes=[ab_t.k()])
        B_.emit("scalar", lambda e: e.activation(
            lg_t.ap[:], ab_t.ap[:], AF.Ln, bias=1.0),
            reads=[ab_t.k()], writes=[lg_t.k()])
        B_.emit("vector", lambda e, pa=pa, bap=bap: e.tensor_scalar(
            mz_t.ap[:], pa.ap[:, 0:2048], bap, 0.0, ALU.add, ALU.max),
            reads=reads + [bk], writes=[mz_t.k()], extra_waits=[ev_abs])
        B_.emit("vector", lambda e, hsl=hsl: e.tensor_add(
            dl_t.ap[:, hsl], mz_t.ap[:], lg_t.ap[:]),
            reads=[mz_t.k(), lg_t.k()], writes=[dl_t.k()])
        B_.emit("vector", lambda e, hsl=hsl, c=c: e.tensor_mul(
            u_t.ap[:, hsl], dl_t.ap[:, hsl], xn.ap[:, c, hsl]),
            reads=[dl_t.k(), xn.k(c)], writes=[u_t.k()])
        if half == 1:
            B_.dma(delta_dram[c], dl_t.ap[:], reads=[dl_t.k()],
                   writes=[f"l0_delta/{c}"])
            B_.dma(u_dram[c], u_t.ap[:], reads=[u_t.k()],
                   writes=[f"l0_u/{c}"])
    if MIXER_SUB[0] in (23, 24):
        proj_resident(B_, "w_dt_s", DC, 2, xn.ap, [xn.k(0), xn.k(1)], evac_dt)
    else:
        proj_resident(B_, "w_dt_s", DC, 2, dtr.ap, [dtr.k(0), dtr.k(1)], evac_dt)
    for a in (dl_t, u_t, ab_t, lg_t, mz_t):
        a.free()
    xn.free()
    dtr.free()
    if sub <= 2:
        zt = B_.arena.alloc([128, TOK], F32)
        zb2 = B_.arena.alloc([128, TOK], BF16)
        for c in range(DC):
            B_.dma(zb2.ap[:], delta_dram[c], reads=[f"l0_delta/{c}"],
                   writes=[zb2.k()])
            B_.emit("vector", lambda e: e.tensor_copy(zt.ap[:], zb2.ap[:]),
                    reads=[zb2.k()], writes=[zt.k()])
            B_.dma(y_dram[c], zt.ap[:], reads=[zt.k()], writes=[f"l0_y/{c}"])
        zt.free()
        zb2.free()
        bcsb.free()
        return

    dsl = B_.arena.alloc([128, TOK], BF16)
    usl = B_.arena.alloc([128, TOK], BF16)
    xsl = B_.arena.alloc([128, TOK], BF16)
    ysl = B_.arena.alloc([128, TOK], F32)
    dA = B_.arena.alloc([128, 2048], F32)
    dBu = B_.arena.alloc([128, 2048], F32)
    hsc = B_.arena.alloc([128, 2048], F32)
    tmul = B_.arena.alloc([128, 2048], F32)
    for sl in range(DC):
        if sl >= SCAN_SLABS[0]:
            B_.emit("vector", lambda e: e.memset(ysl.ap[:], 0.0),
                    writes=[ysl.k()])
            B_.dma(y_dram[sl], ysl.ap[:], reads=[ysl.k()],
                   writes=[f"l0_y/{sl}"])
            continue
        B_.dma(dsl.ap[:], delta_dram[sl], reads=[f"l0_delta/{sl}"],
               writes=[dsl.k()])
        B_.dma(usl.ap[:], u_dram[sl], reads=[f"l0_u/{sl}"], writes=[usl.k()])
        B_.dma(xsl.ap[:], xn_dram[sl], reads=[f"{xn_tag}/{sl}"],
               writes=[xsl.k()])
        dap, dk = G.vec("d_param_c", sl)
        B_.emit("vector", lambda e, dap=dap: e.tensor_scalar(
            ysl.ap[:], xsl.ap[:], dap, None, ALU.mult),
            reads=[xsl.k(), dk], writes=[ysl.k()])
        for b in range(B):
            bsl = slice(b * 2048, (b + 1) * 2048)
            for n in range(N_SSM):
                paB = B_.psum(4)
                paC = B_.psum(4)

                def bcfn(e, b=b, n=n, paB=paB, paC=paC):
                    last = None
                    for q in range(4):
                        qs = slice(b * 2048 + q * 512,
                                   b * 2048 + (q + 1) * 512)
                        last = e.matmul(paB.ap[:, q * 512:(q + 1) * 512],
                                        G.sel16[:, n * 128:(n + 1) * 128],
                                        bcsb.ap[:, qs], start=True, stop=True)
                        last = e.matmul(paC.ap[:, q * 512:(q + 1) * 512],
                                        G.sel16[:, (8 + n) * 128:(9 + n) * 128],
                                        bcsb.ap[:, qs], start=True, stop=True)
                    return last
                B_.emit("tensor", bcfn,
                        reads=[bcsb.k(0), bcsb.k(1), G.b_sel16.k()],
                        writes=[paB.k(), paC.k()])
                B_.emit("scalar", lambda e, sl=sl, n=n, bsl=bsl: e.activation(
                    dA.ap[:], dsl.ap[:, bsl], AF.Exp,
                    scale=G.A[:, sl * 8 + n:sl * 8 + n + 1]),
                    reads=[dsl.k(), G.b_A.k()], writes=[dA.k()])
                B_.emit("vector", lambda e, bsl=bsl, paB=paB: e.tensor_mul(
                    dBu.ap[:], usl.ap[:, bsl], paB.ap[:, 0:2048]),
                    reads=[usl.k(), paB.k()], writes=[dBu.k()])
                B_.emit("vector", lambda e: e.tensor_tensor_scan(
                    hsc.ap[:], dA.ap[:], dBu.ap[:], 0.0, ALU.mult, ALU.add),
                    reads=[dA.k(), dBu.k()], writes=[hsc.k()])
                B_.emit("vector", lambda e, paC=paC: e.tensor_mul(
                    tmul.ap[:], hsc.ap[:], paC.ap[:, 0:2048]),
                    reads=[hsc.k(), paC.k()], writes=[tmul.k()])
                B_.emit("gpsimd", lambda e, bsl=bsl: e.tensor_add(
                    ysl.ap[:, bsl], ysl.ap[:, bsl], tmul.ap[:]),
                    reads=[tmul.k(), ysl.k()], writes=[ysl.k()])
                paB.free()
                paC.free()
        B_.dma(y_dram[sl], ysl.ap[:], reads=[ysl.k()], writes=[f"l0_y/{sl}"])
    for a in (dsl, usl, xsl, ysl, dA, dBu, hsc, tmul, bcsb):
        a.free()



def ph_ffn24(B_, G, xn_dram, y_dram, xn_tag):
    xn = load_xn(B_, xn_dram, xn_tag)
    xn_keys = [xn.k(c) for c in range(DC)]
    hsp_dram = B_.dscr("l0_hsp", [FFC, 128, TOK], BF16)
    wgd = B_.dram["bf_w_gate_s"]
    wud = B_.dram["bf_w_up_s"]
    hbuf = [B_.arena.alloc([128, 2048], BF16) for _ in range(4)]
    absb = [B_.arena.alloc([128, 2048], BF16) for _ in range(4)]
    q_t = B_.arena.alloc([128, 2048], BF16)
    s_t = B_.arena.alloc([128, 2048], BF16)
    msk = B_.arena.alloc([128, 2048], BF16)
    wgb = [B_.arena.alloc([128, DC, 128], BF16) for _ in range(2)]
    wub = [B_.arena.alloc([128, DC, 128], BF16) for _ in range(2)]

    def one_mm(wap, pa, rhs_off):
        def mmfn(e):
            last = None
            for kc in range(DC):
                for t4 in range(4):
                    last = e.matmul(
                        pa.ap[:, t4 * 512:(t4 + 1) * 512], wap[:, kc],
                        xn.ap[:, kc, rhs_off + t4 * 512:
                              rhs_off + (t4 + 1) * 512],
                        start=(kc == 0), stop=(kc == DC - 1))
            return last
        return mmfn

    for gc in range(16):
        for half in range(2):
            hsl = slice(half * 2048, (half + 1) * 2048)
            for j in range(4):
                cp = 16 * j + gc
                par = j % 2
                wg, wu = wgb[par], wub[par]
                B_.dma(wg.ap[:], wgd[cp], reads=[f"bfw_w_gate_s/{cp}"],
                       writes=[wg.k()])
                B_.dma(wu.ap[:], wud[cp], reads=[f"bfw_w_up_s/{cp}"],
                       writes=[wu.k()])
                pa_g = B_.psum(4)
                B_.emit("tensor", one_mm(wg.ap, pa_g, half * 2048),
                        reads=[wg.k()] + xn_keys, writes=[pa_g.k()])
                B_.emit("scalar", lambda e, pa=pa_g, j=j:
                        e.activation(hbuf[j].ap[:], pa.ap[:, 0:2048],
                                     AF.Silu),
                        reads=[pa_g.k()], writes=[hbuf[j].k()])
                pa_g.free()
                pa_u = B_.psum(4)
                B_.emit("tensor", one_mm(wu.ap, pa_u, half * 2048),
                        reads=[wu.k()] + xn_keys, writes=[pa_u.k()])
                B_.emit("vector", lambda e, pa=pa_u, j=j:
                        e.tensor_mul(hbuf[j].ap[:], hbuf[j].ap[:],
                                     pa.ap[:, 0:2048]),
                        reads=[pa_u.k(), hbuf[j].k()], writes=[hbuf[j].k()])
                pa_u.free()
            for j in range(4):
                B_.emit("vector", lambda e, j=j: e.tensor_scalar(
                    absb[j].ap[:].bitcast(U16), hbuf[j].ap[:].bitcast(U16),
                    0x7FFF, None, ALU.bitwise_and),
                    reads=[hbuf[j].k()], writes=[absb[j].k()])
            B_.emit("vector", lambda e: e.tensor_tensor(
                q_t.ap[:], absb[0].ap[:], absb[1].ap[:], ALU.min),
                reads=[absb[0].k(), absb[1].k()], writes=[q_t.k()])
            B_.emit("vector", lambda e: e.tensor_tensor(
                s_t.ap[:], absb[2].ap[:], absb[3].ap[:], ALU.min),
                reads=[absb[2].k(), absb[3].k()], writes=[s_t.k()])
            B_.emit("vector", lambda e: e.tensor_max(
                absb[0].ap[:], absb[0].ap[:], absb[1].ap[:]),
                reads=[absb[0].k(), absb[1].k()], writes=[absb[0].k()])
            B_.emit("vector", lambda e: e.tensor_max(
                absb[2].ap[:], absb[2].ap[:], absb[3].ap[:]),
                reads=[absb[2].k(), absb[3].k()], writes=[absb[2].k()])
            B_.emit("vector", lambda e: e.tensor_tensor(
                absb[0].ap[:], absb[0].ap[:], absb[2].ap[:], ALU.min),
                reads=[absb[0].k(), absb[2].k()], writes=[absb[0].k()])
            B_.emit("vector", lambda e: e.tensor_max(
                q_t.ap[:], q_t.ap[:], s_t.ap[:]),
                reads=[q_t.k(), s_t.k()], writes=[q_t.k()])
            B_.emit("vector", lambda e: e.tensor_max(
                q_t.ap[:], absb[0].ap[:], q_t.ap[:]),
                reads=[absb[0].k(), q_t.k()], writes=[q_t.k()])
            for j in range(4):
                cp = 16 * j + gc
                B_.emit("vector", lambda e, j=j: e.tensor_scalar(
                    msk.ap[:].bitcast(U16), hbuf[j].ap[:].bitcast(U16),
                    0x7FFF, None, ALU.bitwise_and),
                    reads=[hbuf[j].k()], writes=[msk.k()])
                B_.emit("vector", lambda e: e.tensor_tensor(
                    msk.ap[:], msk.ap[:], q_t.ap[:], ALU.is_ge),
                    reads=[msk.k(), q_t.k()], writes=[msk.k()])
                B_.emit("gpsimd", lambda e, j=j: e.tensor_mul(
                    hbuf[j].ap[:], hbuf[j].ap[:], msk.ap[:]),
                    reads=[msk.k(), hbuf[j].k()], writes=[hbuf[j].k()])
                B_.dma(hsp_dram[cp, :, hsl], hbuf[j].ap[:],
                       reads=[hbuf[j].k()], writes=[f"l0_hsp/{cp}"])
    xn.free()
    for a in hbuf + absb + [q_t, s_t, msk] + wgb + wub:
        a.free()

    wdd = B_.dram["bf_w_down_s"]
    wdh = [B_.arena.alloc([128, FFC, 128], BF16) for _ in range(8)]
    hsp_t = [B_.arena.alloc([128, 512], BF16) for _ in range(2)]
    yev = [B_.arena.alloc([128, 512], F32) for _ in range(2)]
    for dh2 in range(2):
        for i in range(8):
            dc = dh2 * 8 + i
            B_.dma(wdh[i].ap[:], wdd[dc], reads=[f"bfw_w_down_s/{dc}"],
                   writes=[wdh[i].k()])
        for ttn in range(8):
            pa = B_.psum(8)
            for kc in range(FFC):
                par = kc % 2
                ht = hsp_t[par]
                B_.dma(ht.ap[:], hsp_dram[kc, :, ttn * 512:(ttn + 1) * 512],
                       reads=[f"l0_hsp/{kc}"], writes=[ht.k()])

                def mmfn(e, kc=kc, ht=ht, pa=pa):
                    last = None
                    for i in range(8):
                        last = e.matmul(pa.ap[:, i * 512:(i + 1) * 512],
                                        wdh[i].ap[:, kc], ht.ap[:],
                                        start=(kc == 0), stop=(kc == FFC - 1))
                    return last
                B_.emit("tensor", mmfn,
                        reads=[ht.k()] + [wdh[i].k() for i in range(8)],
                        writes=[pa.k()] if kc in (0, FFC - 1) else [])
            for i in range(8):
                dc = dh2 * 8 + i
                par = i % 2
                B_.emit("vector", lambda e, i=i, pa=pa, par=par: e.tensor_copy(
                    yev[par].ap[:], pa.ap[:, i * 512:(i + 1) * 512]),
                    reads=[pa.k()], writes=[yev[par].k()])
                B_.dma(y_dram[dc, :, ttn * 512:(ttn + 1) * 512],
                       yev[par].ap[:], reads=[yev[par].k()],
                       writes=[f"l0ffn_y/{dc}"])
            pa.free()
    for a in wdh + hsp_t + yev:
        a.free()



def ph_mha(B_, G, xn_dram, y_dram, pfx, wnames, bias_names=None, causal=False,
           inner_resid=False, xn_tag="xn", y_tag=None, causal_sb=None):
    xn = load_xn(B_, xn_dram, xn_tag)
    xn_keys = [xn.k(c) for c in range(DC)]
    qd = B_.dscr(pfx + "_q", [DC, 128, TOK], BF16)
    kd = B_.dscr(pfx + "_k", [DC, 128, TOK], BF16)
    vd = B_.dscr(pfx + "_v", [DC, 128, TOK], BF16)
    negm = B_.arena.alloc([128, 96], F32)
    sq_t = B_.arena.alloc([128, 2048], BF16)
    red_t = B_.arena.alloc([128, 1], F32)
    out_t = [B_.arena.alloc([128, 2048], BF16) for _ in range(2)]
    B_.emit("vector", lambda e: e.memset(negm.ap[:], 0.0),
            writes=[negm.k(i) for i in range(96)])

    wq, wk, wv, wo = wnames
    bq = bk_ = bv = bo = None
    if bias_names:
        bq, bk_, bv, bo = bias_names

    def evac_qkv(which, out_dram, bias_nm, base_col):
        def evac(fi, half, pa, reads):
            c = fi
            par = half
            hsl = slice(half * 2048, (half + 1) * 2048)
            ot = out_t[par]
            if bias_nm is not None:
                bap, bk2 = G.vec(bias_nm, c)
                B_.emit("scalar", lambda e, pa=pa, bap=bap, ot=ot:
                        e.activation(ot.ap[:], pa.ap[:, 0:2048], AF.Identity,
                                     bias=bap),
                        reads=reads + [bk2], writes=[ot.k()])
            else:
                B_.emit("scalar", lambda e, pa=pa, ot=ot: e.activation(
                    ot.ap[:], pa.ap[:, 0:2048], AF.Copy),
                    reads=reads, writes=[ot.k()])
            B_.dma(out_dram[c, :, hsl], ot.ap[:], reads=[ot.k()],
                   writes=[f"{pfx}_{which}/{c}"])
            if base_col >= 0:
                B_.emit("scalar", lambda e, ot=ot: e.activation(
                    sq_t.ap[:], ot.ap[:], AF.Square),
                    reads=[ot.k()], writes=[sq_t.k()])
                pn = B_.psum(4)

                def nfn(e, pn=pn):
                    last = None
                    for t4 in range(4):
                        last = e.matmul(pn.ap[:, t4 * 512:(t4 + 1) * 512],
                                        G.ones_bf[:],
                                        sq_t.ap[:, t4 * 512:(t4 + 1) * 512],
                                        start=True, stop=True)
                    return last
                B_.emit("tensor", nfn, reads=[sq_t.k(), G.b_ones.k()],
                        writes=[pn.k()])
                col = base_col + half * 16 + c
                B_.emit("vector", lambda e, pn=pn: e.reduce_max(
                    red_t.ap[:], pn.ap[:, 0:2048], axis=AX.X),
                    reads=[pn.k()], writes=[red_t.k()])
                B_.emit("vector", lambda e, col=col: e.tensor_max(
                    negm.ap[:, col:col + 1], negm.ap[:, col:col + 1],
                    red_t.ap[:]),
                    reads=[red_t.k(), negm.k(col)], writes=[negm.k(col)])
                pn.free()
        return evac

    proj_resident(B_, wq, DC, DC, xn.ap, xn_keys, evac_qkv("q", qd, bq, 0))
    proj_resident(B_, wk, DC, DC, xn.ap, xn_keys, evac_qkv("k", kd, bk_, 32))
    proj_resident(B_, wv, DC, DC, xn.ap, xn_keys, evac_qkv("v", vd, bv, -1))
    sq_t.free()
    red_t.free()
    for a in out_t:
        a.free()
    xn.free()

    for b in range(B):
        for h in range(NHEAD):
            i = b * 16 + h
            cq, ck, co = i, 32 + i, 64 + i
            B_.emit("vector", lambda e, cq=cq, ck=ck, co=co: e.tensor_mul(
                negm.ap[:, co:co + 1], negm.ap[:, cq:cq + 1],
                negm.ap[:, ck:ck + 1]),
                reads=[negm.k(cq), negm.k(ck)], writes=[negm.k(co)])
            t30_ap, t30_k = G.cbias(1e-30)
            B_.emit("scalar", lambda e, co=co, t30_ap=t30_ap: e.activation(
                negm.ap[:, co:co + 1], negm.ap[:, co:co + 1], AF.Ln,
                bias=t30_ap),
                reads=[negm.k(co), t30_k], writes=[negm.k(co)])
            B_.emit("scalar", lambda e, co=co: e.activation(
                negm.ap[:, co:co + 1], negm.ap[:, co:co + 1], AF.Exp,
                scale=0.5),
                reads=[negm.k(co)], writes=[negm.k(co)])
            B_.emit("vector", lambda e, co=co: e.tensor_scalar(
                negm.ap[:, co:co + 1], negm.ap[:, co:co + 1], -RSQ_DH, None,
                ALU.mult),
                reads=[negm.k(co)], writes=[negm.k(co)])

    ctx = B_.arena.alloc([128, DC, TOK], BF16)
    qh_t = B_.arena.alloc([128, 2048], BF16)
    kh_t = B_.arena.alloc([128, 2048], BF16)
    vh_t = B_.arena.alloc([128, 2048], BF16)
    vtm = B_.arena.alloc([128, 16, 128], BF16)
    pt = [B_.arena.alloc([128, 1024], BF16) for _ in range(2)]
    rec_t = B_.arena.alloc([128, 1024], F32)
    for b in range(B):
        bsl = slice(b * 2048, (b + 1) * 2048)
        for h in range(NHEAD):
            co = 64 + b * 16 + h
            B_.dma(qh_t.ap[:], qd[h, :, bsl], reads=[f"{pfx}_q/{h}"],
                   writes=[qh_t.k()])
            B_.dma(kh_t.ap[:], kd[h, :, bsl], reads=[f"{pfx}_k/{h}"],
                   writes=[kh_t.k()])
            B_.dma(vh_t.ap[:], vd[h, :, bsl], reads=[f"{pfx}_v/{h}"],
                   writes=[vh_t.k()])
            pv = B_.psum(2)

            def tv(e, pv=pv):
                last = None
                for kc in range(16):
                    last = e.transpose(
                        pv.ap.bitcast(BF16)[:, kc * 128:(kc + 1) * 128],
                        vh_t.ap[:, kc * 128:(kc + 1) * 128], G.identb[:])
                return last
            B_.emit("tensor", tv, reads=[vh_t.k(), G.b_identb.k()],
                    writes=[pv.k()])
            B_.emit("vector", lambda e, pv=pv: e.tensor_copy(
                vtm.ap[:], pv.ap.bitcast(BF16)[:, 0:2048]),
                reads=[pv.k()], writes=[vtm.k()])
            pv.free()
            for qh2 in range(2):
                Q0 = qh2 * 1024
                pa_rs = B_.psum(2)
                pa_av = B_.psum(2)
                kcl = list(range(16)) if not causal else \
                    list(range((Q0 + 1024 + 127) // 128))
                for ki, kc in enumerate(kcl):
                    K0 = kc * 128
                    par = kc % 2
                    first = (ki == 0)
                    last_k = (ki == len(kcl) - 1)
                    pa_st = B_.psum(2)

                    def stfn(e, kc=kc, pa=pa_st, qh2=qh2):
                        last = None
                        for qq in range(2):
                            last = e.matmul(
                                pa.ap[:, qq * 512:(qq + 1) * 512],
                                kh_t.ap[:, kc * 128:(kc + 1) * 128],
                                qh_t.ap[:, qh2 * 1024 + qq * 512:
                                        qh2 * 1024 + (qq + 1) * 512],
                                start=True, stop=True)
                        return last
                    B_.emit("tensor", stfn, reads=[kh_t.k(), qh_t.k()],
                            writes=[pa_st.k()])
                    B_.emit("scalar", lambda e, pa=pa_st, par=par, co=co:
                            e.activation(pt[par].ap[:], pa.ap[:, 0:1024],
                                         AF.Exp, bias=negm.ap[:, co:co + 1],
                                         scale=RSQ_DH),
                            reads=[pa_st.k(), negm.k(co)],
                            writes=[pt[par].k()])
                    pa_st.free()
                    if causal and K0 + 127 > Q0:
                        off = 1920 - (K0 - Q0)
                        B_.emit("vector", lambda e, par=par, off=off:
                                e.tensor_mul(pt[par].ap[:], pt[par].ap[:],
                                             causal_sb.ap[:, off:off + 1024]),
                                reads=[pt[par].k(), causal_sb.k()],
                                writes=[pt[par].k()])

                    def rsfn(e, par=par, pa=pa_rs, first=first, last_k=last_k):
                        last = None
                        for qq in range(2):
                            last = e.matmul(pa.ap[:, qq * 512:(qq + 1) * 512],
                                            G.ones_bf[:],
                                            pt[par].ap[:, qq * 512:(qq + 1) * 512],
                                            start=first, stop=last_k)
                        return last
                    B_.emit("tensor", rsfn, reads=[pt[par].k(), G.b_ones.k()],
                            writes=[pa_rs.k()] if (first or last_k) else [])

                    def avfn(e, par=par, pa=pa_av, kc=kc, first=first,
                             last_k=last_k):
                        last = None
                        for qq in range(2):
                            last = e.matmul(pa.ap[:, qq * 512:(qq + 1) * 512],
                                            vtm.ap[:, kc],
                                            pt[par].ap[:, qq * 512:(qq + 1) * 512],
                                            start=first, stop=last_k)
                        return last
                    B_.emit("tensor", avfn, reads=[pt[par].k(), vtm.k()],
                            writes=[pa_av.k()] if (first or last_k) else [])
                B_.emit("vector", lambda e, pa=pa_rs: e.reciprocal(
                    rec_t.ap[:], pa.ap[:, 0:1024]),
                    reads=[pa_rs.k()], writes=[rec_t.k()])
                B_.emit("vector", lambda e, pa=pa_av, h=h, b=b, Q0=Q0:
                        e.tensor_mul(
                            ctx.ap[:, h, b * 2048 + Q0:b * 2048 + Q0 + 1024],
                            pa.ap[:, 0:1024], rec_t.ap[:]),
                        reads=[pa_av.k(), rec_t.k()], writes=[ctx.k(h)])
                pa_rs.free()
                pa_av.free()
    for a in (qh_t, kh_t, vh_t, vtm, rec_t):
        a.free()
    for a in pt:
        a.free()
    negm.free()

    yev2 = [B_.arena.alloc([128, 2048], F32) for _ in range(2)]
    rin = [B_.arena.alloc([128, 2048], BF16) for _ in range(2)] \
        if inner_resid else None
    ctx_keys = [ctx.k(h) for h in range(DC)]

    def evac_o(fi, half, pa, reads):
        c = fi
        par = half
        hsl = slice(half * 2048, (half + 1) * 2048)
        ye = yev2[par]
        if inner_resid:
            B_.dma(rin[par].ap[:], xn_dram[c, :, hsl],
                   reads=[f"{xn_tag}/{c}"], writes=[rin[par].k()])
            B_.emit("vector", lambda e, pa=pa, par=par, ye=ye: e.tensor_add(
                ye.ap[:], pa.ap[:, 0:2048], rin[par].ap[:]),
                reads=reads + [rin[par].k()], writes=[ye.k()])
        elif bo is not None:
            bap, bk2 = G.vec(bo, c)
            B_.emit("scalar", lambda e, pa=pa, bap=bap, ye=ye: e.activation(
                ye.ap[:], pa.ap[:, 0:2048], AF.Identity, bias=bap),
                reads=reads + [bk2], writes=[ye.k()])
        else:
            B_.emit("vector", lambda e, pa=pa, ye=ye: e.tensor_copy(
                ye.ap[:], pa.ap[:, 0:2048]),
                reads=reads, writes=[ye.k()])
        B_.dma(y_dram[c, :, hsl], ye.ap[:], reads=[ye.k()],
               writes=[f"{y_tag}/{c}"])
    proj_resident(B_, wo, DC, DC, ctx.ap, ctx_keys, evac_o)
    for a in yev2 + (rin or []):
        a.free()
    ctx.free()



STOP_AFTER = [99]
MIXER_SUB = [99]
SCAN_SLABS = [99]


def emit_block0(B_, G, x0):
    xres0 = B_.dscr("xres0", [DC, 128, TOK], F32)
    xn1 = B_.dscr("xn1", [DC, 128, TOK], BF16)
    y1 = B_.dscr("y1", [DC, 128, TOK], F32)
    x1 = B_.dram["x1"]
    ph_entry(B_, G, x0, xres0)
    if STOP_AFTER[0] <= 1:
        for c in range(DC):
            st = B_.arena.alloc([128, TOK], F32)
            B_.dma(st.ap[:], xres0[c], reads=[f"xres0/{c}"], writes=[st.k()])
            B_.dma(x1[c], st.ap[:], reads=[st.k()], writes=[f"x1/{c}"])
            st.free()
        return
    ph_ln(B_, G, xres0, xn_dram=xn1, z_tag="xres0", out_tag="xn1")
    if STOP_AFTER[0] <= 2:
        cv = B_.arena.alloc([128, TOK], BF16)
        cf = B_.arena.alloc([128, TOK], F32)
        for c in range(DC):
            B_.dma(cv.ap[:], xn1[c], reads=[f"xn1/{c}"], writes=[cv.k()])
            B_.emit("vector", lambda e: e.tensor_copy(cf.ap[:], cv.ap[:]),
                    reads=[cv.k()], writes=[cf.k()])
            B_.dma(x1[c], cf.ap[:], reads=[cf.k()], writes=[f"x1/{c}"])
        cv.free()
        cf.free()
        return
    ph_l0_mixer(B_, G, xn1, y1, "xn1")
    if STOP_AFTER[0] <= 3:
        for c in range(DC):
            st = B_.arena.alloc([128, TOK], F32)
            B_.dma(st.ap[:], y1[c], reads=[f"l0_y/{c}"], writes=[st.k()])
            B_.dma(x1[c], st.ap[:], reads=[st.k()], writes=[f"x1/{c}"])
            st.free()
        return
    ph_ln(B_, G, y1, resid_dram=xres0, out_dram=x1, z_tag="l0_y",
          out_tag="x1", resid_tag="xres0")


def emit_block1(B_, G):
    x1 = B_.dram["x1"]
    xn2 = B_.dscr("xn2", [DC, 128, TOK], BF16)
    y2 = B_.dscr("y2", [DC, 128, TOK], F32)
    x2 = B_.dram["x2"]
    ph_ln(B_, G, x1, xn_dram=xn2, z_tag="x1", out_tag="xn2")
    ph_ffn24(B_, G, xn2, y2, "xn2")
    ph_ln(B_, G, y2, resid_dram=x1, out_dram=x2, z_tag="l0ffn_y",
          out_tag="x2", resid_tag="x1")


def emit_block2(B_, G):
    x2 = B_.dram["x2"]
    xn3 = B_.dscr("xn3", [DC, 128, TOK], BF16)
    y3 = B_.dscr("y3", [DC, 128, TOK], F32)
    x3 = B_.dram["x3"]
    ph_ln(B_, G, x2, xn_dram=xn3, z_tag="x2", out_tag="xn3")
    ph_mha(B_, G, xn3, y3, "mha3", ("wq_s", "wk_s", "wv_s", "wo_s"),
           bias_names=("bq_c", "bk_c", "bv_c", "bo_c"), causal=False,
           xn_tag="xn3", y_tag="mha3_y")
    ph_ln(B_, G, y3, resid_dram=x2, out_dram=x3, z_tag="mha3_y",
          out_tag="x3", resid_tag="x2")


def emit_block3(B_, G):
    x3 = B_.dram["x3"]
    xn4 = B_.dscr("xn4", [DC, 128, TOK], BF16)
    x4 = B_.dram["x4"]
    ph_ln(B_, G, x3, xn_dram=xn4, z_tag="x3", out_tag="xn4")

    causal_sb = B_.arena.alloc([128, 3072], BF16)
    B_.dma(causal_sb.ap[:], B_.dram["causal_m"], writes=[causal_sb.k()])

    ixn1 = B_.dscr("ixn1", [DC, 128, TOK], BF16)
    x2a = B_.dscr("x2a", [DC, 128, TOK], BF16)
    x2a_f = B_.dscr("x2a_f", [DC, 128, TOK], F32)
    ixn2 = B_.dscr("ixn2", [DC, 128, TOK], BF16)
    hg = B_.dscr("hg", [FFC, 128, TOK], BF16)
    out2 = B_.dscr("out2", [DC, 128, TOK], F32)

    ph_ln(B_, G, xn4, xn_dram=ixn1, z_is_bf16=True, scale_nm="ln1s_c",
          bias_nm="ln1b_c", z_tag="xn4", out_tag="ixn1")
    ph_mha(B_, G, ixn1, x2a_f, "imha", ("iwq_s", "iwk_s", "iwv_s", "iwo_s"),
           bias_names=None, causal=True, inner_resid=False,
           xn_tag="ixn1", y_tag="ictx_y", causal_sb=causal_sb)
    ra = B_.arena.alloc([128, TOK], F32)
    rb = B_.arena.alloc([128, TOK], BF16)
    for c in range(DC):
        B_.dma(ra.ap[:], x2a_f[c], reads=[f"ictx_y/{c}"], writes=[ra.k()])
        B_.dma(rb.ap[:], xn4[c], reads=[f"xn4/{c}"], writes=[rb.k()])
        B_.emit("vector", lambda e: e.tensor_add(ra.ap[:], ra.ap[:], rb.ap[:]),
                reads=[ra.k(), rb.k()], writes=[ra.k()])
        B_.dma(x2a_f[c], ra.ap[:], reads=[ra.k()], writes=[f"x2a/{c}"])
    ra.free()
    rb.free()
    ph_ln(B_, G, x2a_f, xn_dram=ixn2, scale_nm="ln2s_c", bias_nm="ln2b_c",
          z_tag="x2a", out_tag="ixn2")

    ixn2_sb = load_xn(B_, ixn2, "ixn2")
    ixn2_keys = [ixn2_sb.k(c) for c in range(DC)]
    gel = [B_.arena.alloc([128, 2048], BF16) for _ in range(2)]

    def evac_ff1(fi, half, pa, reads):
        par = half
        hsl = slice(half * 2048, (half + 1) * 2048)
        B_.emit("scalar", lambda e, pa=pa, par=par: e.activation(
            gel[par].ap[:], pa.ap[:, 0:2048], AF.Gelu),
            reads=reads, writes=[gel[par].k()])
        B_.dma(hg[fi, :, hsl], gel[par].ap[:], reads=[gel[par].k()],
               writes=[f"hg/{fi}"])
    proj_resident(B_, "wff1_s", FFC, DC, ixn2_sb.ap, ixn2_keys, evac_ff1)
    for a in gel:
        a.free()
    ixn2_sb.free()

    wfd = B_.dram["bf_wff2_s"]
    wfh = [B_.arena.alloc([128, FFC, 128], BF16) for _ in range(8)]
    hg_t = [B_.arena.alloc([128, 512], BF16) for _ in range(2)]
    o2ev = [B_.arena.alloc([128, 512], F32) for _ in range(2)]
    x2a_t = [B_.arena.alloc([128, 512], F32) for _ in range(2)]
    for dh2 in range(2):
        for i in range(8):
            dc = dh2 * 8 + i
            B_.dma(wfh[i].ap[:], wfd[dc], reads=[f"bfw_wff2_s/{dc}"],
                   writes=[wfh[i].k()])
        for ttn in range(8):
            pa = B_.psum(8)
            for kc in range(FFC):
                par = kc % 2
                ht = hg_t[par]
                B_.dma(ht.ap[:], hg[kc, :, ttn * 512:(ttn + 1) * 512],
                       reads=[f"hg/{kc}"], writes=[ht.k()])

                def mmfn(e, kc=kc, ht=ht, pa=pa):
                    last = None
                    for i in range(8):
                        last = e.matmul(pa.ap[:, i * 512:(i + 1) * 512],
                                        wfh[i].ap[:, kc], ht.ap[:],
                                        start=(kc == 0), stop=(kc == FFC - 1))
                    return last
                B_.emit("tensor", mmfn,
                        reads=[ht.k()] + [wfh[i].k() for i in range(8)],
                        writes=[pa.k()] if kc in (0, FFC - 1) else [])
            for i in range(8):
                dc = dh2 * 8 + i
                par = i % 2
                B_.dma(x2a_t[par].ap[:],
                       x2a_f[dc, :, ttn * 512:(ttn + 1) * 512],
                       reads=[f"x2a/{dc}"], writes=[x2a_t[par].k()])
                B_.emit("vector", lambda e, i=i, pa=pa, par=par: e.tensor_add(
                    o2ev[par].ap[:], pa.ap[:, i * 512:(i + 1) * 512],
                    x2a_t[par].ap[:]),
                    reads=[pa.k(), x2a_t[par].k()], writes=[o2ev[par].k()])
                B_.dma(out2[dc, :, ttn * 512:(ttn + 1) * 512],
                       o2ev[par].ap[:], reads=[o2ev[par].k()],
                       writes=[f"out2/{dc}"])
            pa.free()
    for a in wfh + hg_t + o2ev + x2a_t:
        a.free()
    causal_sb.free()

    xn4_sb = load_xn(B_, xn4, "xn4")
    xn4_keys = [xn4_sb.k(c) for c in range(DC)]
    y4 = B_.dscr("y4", [DC, 128, TOK], F32)
    lg = B_.arena.alloc([128, TOK], BF16)

    def evac_rt(fi, half, pa, reads):
        hsl = slice(half * 2048, (half + 1) * 2048)
        def cpf(e, pa=pa, hsl=hsl):
            last = None
            for prow in (0, 32, 64):
                last = e.tensor_copy(lg.ap[prow:prow + 1, hsl],
                                     pa.ap[prow:prow + 1, 0:2048])
            return last
        B_.emit("vector", cpf, reads=reads, writes=[lg.k()])
    proj_resident(B_, "w_router_s", 1, DC, xn4_sb.ap, xn4_keys, evac_rt)
    lr1 = B_.arena.alloc([128, 1, TOK], BF16)

    def evac_lr1(fi, half, pa, reads):
        hsl = slice(half * 2048, (half + 1) * 2048)
        B_.emit("vector", lambda e, pa=pa, hsl=hsl: e.tensor_copy(
            lr1.ap[:, 0, hsl], pa.ap[:, 0:2048]),
            reads=reads, writes=[lr1.k()])
    proj_resident(B_, "w_lrdown_s", 1, DC, xn4_sb.ap, xn4_keys, evac_lr1)
    xn4_sb.free()

    lf = [B_.arena.alloc([128, TOK], F32) for _ in range(3)]
    for r in range(3):
        base = 32 * r
        for half in range(2):
            pb = B_.psum(4)

            def bfn(e, base=base, half=half, pb=pb):
                last = None
                for t4 in range(4):
                    c0 = half * 2048 + t4 * 512
                    last = e.matmul(pb.ap[:, t4 * 512:(t4 + 1) * 512],
                                    G.ones_bf[base:base + 1, :],
                                    lg.ap[base:base + 1, c0:c0 + 512],
                                    start=True, stop=True)
                return last
            B_.emit("tensor", bfn, reads=[lg.k(), G.b_ones.k()],
                    writes=[pb.k()])
            B_.emit("vector", lambda e, pb=pb, r=r, half=half: e.tensor_copy(
                lf[r].ap[:, half * 2048:(half + 1) * 2048], pb.ap[:, 0:2048]),
                reads=[pb.k()], writes=[lf[r].k()])
            pb.free()

    m1b = B_.arena.alloc([128, TOK], U8)
    m2b = B_.arena.alloc([128, TOK], U8)
    g1 = B_.arena.alloc([128, TOK], BF16)
    g2 = B_.arena.alloc([128, TOK], BF16)
    B_.emit("vector", lambda e: e.tensor_tensor(
        m1b.ap[:], lf[1].ap[:], lf[0].ap[:], ALU.is_gt),
        reads=[lf[1].k(), lf[0].k()], writes=[m1b.k()])
    B_.emit("vector", lambda e: e.tensor_tensor(
        g1.ap[:], lf[2].ap[:], lf[0].ap[:], ALU.is_gt),
        reads=[lf[2].k(), lf[0].k()], writes=[g1.k()])
    B_.emit("vector", lambda e: e.tensor_tensor(
        g2.ap[:], lf[2].ap[:], lf[1].ap[:], ALU.is_gt),
        reads=[lf[2].k(), lf[1].k()], writes=[g2.k()])
    B_.emit("vector", lambda e: e.tensor_mul(
        m2b.ap[:], g1.ap[:], g2.ap[:]),
        reads=[g1.k(), g2.k()], writes=[m2b.k()])
    for a in lf:
        a.free()
    g1.free()
    g2.free()

    osel = [B_.arena.alloc([128, 2048], F32) for _ in range(2)]
    o2t = [B_.arena.alloc([128, 2048], F32) for _ in range(2)]
    o1t = [B_.arena.alloc([128, 2048], F32) for _ in range(2)]
    xn4_t = [B_.arena.alloc([128, 2048], BF16) for _ in range(2)]

    def evac_sel(fi, half, pa, reads):
        c = fi
        par = half
        hsl = slice(half * 2048, (half + 1) * 2048)
        sap, sk = G.vec("sbias_c", c)
        B_.dma(xn4_t[par].ap[:], xn4[c, :, hsl], reads=[f"xn4/{c}"],
               writes=[xn4_t[par].k()])
        B_.emit("vector", lambda e, par=par, sap=sap:
                e.tensor_scalar(osel[par].ap[:], xn4_t[par].ap[:], sap,
                                None, ALU.add),
                reads=[xn4_t[par].k(), sk], writes=[osel[par].k()])
        B_.emit("vector", lambda e, par=par, pa=pa:
                e.tensor_add(o1t[par].ap[:], pa.ap[:, 0:2048],
                             xn4_t[par].ap[:]),
                reads=reads + [xn4_t[par].k()], writes=[o1t[par].k()])
        B_.dma(o2t[par].ap[:], out2[c, :, hsl], reads=[f"out2/{c}"],
               writes=[o2t[par].k()])
        B_.emit("vector", lambda e, par=par, hsl=hsl: e.copy_predicated(
            osel[par].ap[:], m1b.ap[:, hsl], o1t[par].ap[:]),
            reads=[m1b.k(), o1t[par].k(), osel[par].k()],
            writes=[osel[par].k()])
        B_.emit("vector", lambda e, par=par, hsl=hsl: e.copy_predicated(
            osel[par].ap[:], m2b.ap[:, hsl], o2t[par].ap[:]),
            reads=[m2b.k(), o2t[par].k(), osel[par].k()],
            writes=[osel[par].k()])
        B_.dma(y4[c, :, hsl], osel[par].ap[:], reads=[osel[par].k()],
               writes=[f"y4/{c}"])
    proj_resident(B_, "w_lrup_s", DC, 1, lr1.ap, [lr1.k()], evac_sel)
    for a in osel + o2t + o1t + xn4_t:
        a.free()
    lr1.free()
    m1b.free()
    m2b.free()
    lg.free()

    ph_ln(B_, G, y4, resid_dram=x3, out_dram=x4, z_tag="y4",
          out_tag="x4", resid_tag="x3")



ALL_BF = list(BF_WEIGHTS.keys())
BLOCK_BF = {
    0: ["w_xproj_s", "w_dt_s"],
    1: ["w_gate_s", "w_up_s", "w_down_s"],
    2: ["wq_s", "wk_s", "wv_s", "wo_s"],
    3: ["iwq_s", "iwk_s", "iwv_s", "iwo_s", "wff1_s", "wff2_s",
        "w_router_s", "w_lrdown_s", "w_lrup_s"],
}


def build_program(blocks, fused_all=False):
    B_ = Bld()
    B_.mk_arena(200 * 1024)
    first, last = blocks[0], blocks[-1]
    bf_names = []
    for b in blocks:
        bf_names += BLOCK_BF[b]
    declare_weights(B_, bf_names)
    if first == 0:
        x0 = B_.din("x0", [TOK, D], F32)
    res_names = {0: "x1", 1: "x2", 2: "x3", 3: "x4"}
    for b in blocks:
        nm = res_names[b]
        if b == last and last != 3:
            B_.dout(nm, [DC, 128, TOK], F32)
        else:
            B_.dscr(nm, [DC, 128, TOK], F32)
    if first != 0:
        prev = res_names[first - 1]
        B_.din(prev, [DC, 128, TOK], F32)
    if last == 3:
        xout = B_.dout("xout", [TOK, D], F32)
    G = Globals(B_)
    ph_wcast(B_, bf_names)
    for b in blocks:
        if b == 0:
            emit_block0(B_, G, B_.dram["x0"])
        elif b == 1:
            emit_block1(B_, G)
        elif b == 2:
            emit_block2(B_, G)
        elif b == 3:
            emit_block3(B_, G)
    if last == 3:
        ph_exit(B_, G, B_.dram["x4"], xout, "x4")
    nc = B_.finalize(B_.outputs_done())
    return nc


_PROG_CACHE = {}


def _get_prog(key, blocks):
    if key not in _PROG_CACHE:
        _PROG_CACHE[key] = build_program(blocks)
    return _PROG_CACHE[key]


def kernel(**inputs):
    w = prep_weights(inputs)
    x_full = inputs["L0_attn_x"]
    launches = [[0, 1, 2, 3]]
    core_in = []
    for core in range(NCORES):
        xc = np.ascontiguousarray(
            x_full[core * B:(core + 1) * B].reshape(TOK, D))
        core_in.append(xc)
    state = [dict() for _ in range(NCORES)]
    res_names = {0: "x1", 1: "x2", 2: "x3", 3: "x4"}
    out = None
    for li, blocks in enumerate(launches):
        nc = _get_prog(tuple(blocks), blocks)
        in_maps = []
        for core in range(NCORES):
            m = dict(w)
            if blocks[0] == 0:
                m["x0"] = core_in[core]
            else:
                prev = res_names[blocks[0] - 1]
                m[prev] = state[core][prev]
            in_maps.append(m)
        res = run_bass_kernel_spmd(nc, in_maps, core_ids=list(range(NCORES)))
        for core in range(NCORES):
            r = res.results[core]
            for k, v in r.items():
                state[core][k] = np.asarray(v)
    out = np.empty((NCORES * B, S, D), dtype=np.float32)
    for core in range(NCORES):
        out[core * B:(core + 1) * B] = state[core]["xout"].reshape(B, S, D)
    return out


# version 03
# speedup vs baseline: 1.1294x; 1.1294x over previous
import contextlib
import numpy as np
import ml_dtypes
import concourse.bass as bass
import concourse.mybir as mybir
from concourse.bass_utils import run_bass_kernel_spmd

F32 = mybir.dt.float32
BF16 = mybir.dt.bfloat16
U8 = mybir.dt.uint8
U16 = mybir.dt.uint16
AF = mybir.ActivationFunctionType
ALU = mybir.AluOpType
AX = mybir.AxisListType

NCORES = 8
B = 2
S = 2048
TOK = B * S
D = 2048
DC = D // 128
N_SSM = 8
FF = 8192
FFC = FF // 128
NHEAD = 16
DH = 128
LN_EPS = 1e-5
RSQ_DH = float(1.0 / np.sqrt(128.0))

ENGINES = ("tensor", "vector", "scalar", "gpsimd", "sync")



class Buf:
    __slots__ = ("uid", "nsub", "war", "is_psum")
    _next = [0]

    def __init__(self, war=()):
        self.uid = str(Buf._next[0])
        Buf._next[0] += 1
        self.nsub = set()
        self.war = set(war)
        self.is_psum = False

    def k(self, i=None):
        key = self.uid if i is None else f"{self.uid}/{i}"
        self.nsub.add(key)
        return key


class _BufState:
    __slots__ = ("writer", "readers")

    def __init__(self):
        self.writer = None
        self.readers = []


class Arena:
    def __init__(self, bld, nbytes):
        self.bld = bld
        self.t = bld.stack.enter_context(
            bld.nc.sbuf_tensor("arena", [128, nbytes], U8))
        self.free_list = [(0, nbytes, set())]

    def alloc(self, shape, dtype, align=64):
        nbytes = int(np.prod(shape[1:])) * mybir.dt.size(dtype)
        nbytes = (nbytes + align - 1) & ~(align - 1)
        for i, (off, size, ghosts) in enumerate(self.free_list):
            aoff = (off + align - 1) & ~(align - 1)
            pad = aoff - off
            if size >= nbytes + pad:
                del self.free_list[i]
                if pad:
                    self.free_list.insert(i, (off, pad, set(ghosts)))
                if size - pad - nbytes:
                    self.free_list.insert(i + (1 if pad else 0),
                                          (aoff + nbytes,
                                           size - pad - nbytes, set(ghosts)))
                ap = self.t[:, aoff:aoff + nbytes].bitcast(dtype)
                n = int(np.prod(shape[1:]))
                ap = ap[0:shape[0], :n]
                if len(shape) == 3:
                    ap = ap.rearrange("p (a b) -> p a b", a=shape[1])
                elif len(shape) == 4:
                    ap = ap.rearrange("p (a b c) -> p a b c", a=shape[1],
                                      b=shape[2])
                elif len(shape) != 2:
                    raise ValueError(shape)
                buf = Buf(war=ghosts)
                self.bld.bufreg[buf.uid] = buf
                return ArenaAlloc(self, ap, buf, aoff, nbytes)
        raise MemoryError(f"arena alloc {shape} {dtype} ({nbytes}B): "
                          f"free={[(o, s) for o, s, _ in self.free_list]}")


class ArenaAlloc:
    def __init__(self, arena, ap, buf, off, nbytes):
        self.arena = arena
        self.ap = ap
        self.buf = buf
        self.off = off
        self.nbytes = nbytes

    def k(self, i=None):
        return self.buf.k(i)

    def free(self):
        self.arena.free_list.append((self.off, self.nbytes,
                                     set(self.buf.nsub)))
        fl = sorted(self.arena.free_list)
        merged = []
        for off, size, gh in fl:
            if merged and merged[-1][0] + merged[-1][1] == off:
                merged[-1] = (merged[-1][0], merged[-1][1] + size,
                              merged[-1][2] | gh)
            else:
                merged.append((off, size, set(gh)))
        self.arena.free_list = merged


class PsumAlloc:
    def __init__(self, bld, ap, buf, banks):
        self.bld = bld
        self.ap = ap
        self.buf = buf
        self.banks = banks

    def k(self, i=None):
        return self.buf.k(i)

    def free(self):
        self.bld._psum_seq[0] += 1
        for bk in self.banks:
            self.bld._psum_free.append(bk)
            self.bld._psum_ghost[bk] = set(self.buf.nsub)
            self.bld._psum_age[bk] = self.bld._psum_seq[0]


class Bld:
    def __init__(self, fused=True):
        self.nc = bass.Bass("TRN2", target_bir_lowering=False)
        self.fused = fused
        self.stack = contextlib.ExitStack()
        self.ops = []
        self.cnt = {e: 0 for e in ENGINES}
        self.NDMA = 8
        for q in ("sync", "scalar"):
            for i in range(self.NDMA):
                self.cnt[f"dma_{q}{i}"] = 0
        self.dma_idx = {"sync": 0, "scalar": 0, "gpsimd": 0, "vector": 0}
        self.state = {}
        self.last_waited = {e: {} for e in ENGINES}
        self.dram = {}
        self.bufreg = {}
        self._psum_free = list(range(8))
        self._psum_ghost = {}
        self._psum_age = {i: 0 for i in range(8)}
        self._psum_seq = [0]
        self._uniq = [0]
        self._psum_all = self.stack.enter_context(
            self.nc.psum_tensor("psum_all", [128, 8 * 512], F32))
        self.arena = None

    def mk_arena(self, nbytes):
        self.arena = Arena(self, nbytes)

    def din(self, name, shape, dtype):
        ap = self.nc.dram_tensor(name, list(shape), dtype,
                                 kind="ExternalInput").ap()
        self.dram[name] = ap
        return ap

    def dout(self, name, shape, dtype):
        ap = self.nc.dram_tensor(name, list(shape), dtype,
                                 kind="ExternalOutput").ap()
        self.dram[name] = ap
        return ap

    def dscr(self, name, shape, dtype):
        ap = self.nc.dram_tensor(name, list(shape), dtype, kind="Internal").ap()
        self.dram[name] = ap
        return ap

    def _name(self, pfx):
        self._uniq[0] += 1
        return f"{pfx}_{self._uniq[0]}"

    def sb(self, shape, dtype, pfx="sb"):
        h = self.stack.enter_context(
            self.nc.sbuf_tensor(self._name(pfx), list(shape), dtype))
        b = Buf()
        self.bufreg[b.uid] = b
        return h, b

    def psum(self, nbank=1):
        fr = sorted(self._psum_free)
        runs = []
        for i in range(len(fr) - nbank + 1):
            if fr[i + nbank - 1] - fr[i] == nbank - 1:
                runs.append(fr[i:i + nbank])
        assert runs, f"psum alloc {nbank} failed; free={fr}"
        run = min(runs, key=lambda r: max(self._psum_age[b] for b in r))
        for bk in run:
            self._psum_free.remove(bk)
        ap = self._psum_all[:, run[0] * 512:(run[0] + nbank) * 512]
        war = set()
        for bk in run:
            war |= self._psum_ghost.pop(bk, set())
        buf = Buf(war=war)
        buf.is_psum = True
        self.bufreg[buf.uid] = buf
        return PsumAlloc(self, ap, buf, run)

    def _need(self, waits, ev):
        sem, val = ev
        if waits.get(sem, 0) < val:
            waits[sem] = val

    def emit(self, engine, fn, reads=(), writes=(), extra_waits=(), dma=False):
        waits = {}
        my_sem = engine
        for k in reads:
            st = self.state.get(k)
            if st and st.writer:
                self._need(waits, st.writer)
            if st is not None and not dma:
                buf = self.bufreg.get(k.split("/")[0])
                if buf is not None and buf.is_psum:
                    for r in st.readers:
                        if r[0] != my_sem:
                            self._need(waits, r)
        for k in writes:
            st = self.state.get(k)
            if st:
                for r in st.readers:
                    self._need(waits, r)
                if st.writer:
                    self._need(waits, st.writer)
            buf = self.bufreg.get(k.split("/")[0])
            if buf is not None and buf.war:
                for wk in buf.war:
                    wst = self.state.get(wk)
                    if wst:
                        for r in wst.readers:
                            self._need(waits, r)
                        if wst.writer:
                            self._need(waits, wst.writer)
                buf.war = set()
        for ev in extra_waits:
            if ev is not None:
                self._need(waits, ev)
        if dma:
            j = self.dma_idx[engine]
            self.dma_idx[engine] = j + 1
            ev_sem = f"dma_{engine}{j % self.NDMA}"
            if ev_sem not in self.cnt:
                self.cnt[ev_sem] = 0
            occ = j // self.NDMA
            if occ > 0:
                self._need(waits, (ev_sem, 16 * occ))
        else:
            ev_sem = engine
        lw = self.last_waited[engine]
        pruned = {}
        for sem, val in waits.items():
            if lw.get(sem, 0) >= val:
                continue
            pruned[sem] = val
            lw[sem] = val
        self.cnt[ev_sem] += 16 if dma else 1
        ev = (ev_sem, self.cnt[ev_sem])
        self.ops.append((engine, pruned, fn, (ev_sem, 16 if dma else 1)))
        for k in writes:
            st = self.state.setdefault(k, _BufState())
            st.writer = ev
            st.readers = []
        for k in reads:
            st = self.state.setdefault(k, _BufState())
            st.readers.append(ev)
        return ev

    def dma(self, out, in_, reads=(), writes=(), extra_waits=(), queue="sync"):
        return self.emit(queue, lambda e: e.dma_start(out=out, in_=in_),
                         reads=reads, writes=writes, extra_waits=extra_waits,
                         dma=True)

    def finalize(self, final_events=()):
        nc = self.nc
        sems = {}
        with contextlib.ExitStack() as es:
            for name in list(self.cnt.keys()):
                sems[name] = es.enter_context(nc.semaphore(f"sem_{name}"))
            blk = es.enter_context(nc.Block())
            per_engine = {e: [] for e in ENGINES}
            for (engine, waits, fn, inc) in self.ops:
                per_engine[engine].append((waits, fn, inc))

            def make(engine):
                def body(e):
                    for waits, fn, inc in per_engine[engine]:
                        for sem, val in waits.items():
                            e.wait_ge(sems[sem], val)
                        inst = fn(e)
                        sem, amt = inc
                        assert inst is not None, f"op on {engine} -> None"
                        inst.then_inc(sems[sem], amt)
                    if engine == "sync":
                        for sem, val in final_events:
                            e.wait_ge(sems[sem], val)
                return body

            for engine in ENGINES:
                if per_engine[engine] or engine == "sync":
                    getattr(blk, engine)(make(engine))
        self.stack.close()
        return nc

    def outputs_done(self):
        return [(s, v) for s, v in self.cnt.items()
                if s.startswith("dma_") and v > 0]



def _slabify(W, fs=128):
    K, F = W.shape
    KC = K // 128
    assert K % 128 == 0 and F % fs == 0
    return np.ascontiguousarray(
        W.reshape(KC, 128, F // fs, fs).transpose(2, 1, 0, 3))


def _perm24():
    i = np.arange(FF)
    g, j = i // 4, i % 4
    ip = j * (FF // 4) + g
    inv = np.empty(FF, dtype=np.int64)
    inv[ip] = i
    return inv


def _chunkvec(v):
    return np.ascontiguousarray(v.reshape(DC, 128).T.astype(np.float32))


def prep_weights(inp):
    P24 = _perm24()
    asc = np.ascontiguousarray
    w = {}
    xp = np.zeros((D, 384), dtype=np.float32)
    xp[:, :272] = inp["L0_attn_W_xproj"].T
    w["w_xproj_s"] = _slabify(xp)
    w["w_dt_s"] = _slabify(asc(inp["L0_attn_W_dt"].T.astype(np.float32)))
    w["b_dt_c"] = _chunkvec(inp["L0_attn_b_dt"])
    w["a_log_c"] = asc(inp["L0_attn_A_log"].reshape(DC, 128, N_SSM)
                       .transpose(1, 0, 2).reshape(128, DC * N_SSM)
                       .astype(np.float32))
    w["d_param_c"] = _chunkvec(inp["L0_attn_D_param"])
    w["w_gate_s"] = _slabify(asc(inp["L0_ffn_w_gate"][:, P24]))
    w["w_up_s"] = _slabify(asc(inp["L0_ffn_w_up"][:, P24]))
    w["w_down_s"] = _slabify(asc(inp["L0_ffn_w_down"][P24, :]))
    for nm, key in (("wq", "L1_attn_Wq"), ("wk", "L1_attn_Wk"),
                    ("wv", "L1_attn_Wv"), ("wo", "L1_attn_Wo")):
        w[nm + "_s"] = _slabify(asc(inp[key].T))
    w["bq_c"] = _chunkvec(inp["L1_attn_bq"])
    w["bk_c"] = _chunkvec(inp["L1_attn_bk"])
    w["bv_c"] = _chunkvec(inp["L1_attn_bv"])
    w["bo_c"] = _chunkvec(inp["L1_attn_bo"])
    w["sbias_c"] = _chunkvec(inp["L1_ffn_simple_bias"])
    for nm, key in (("iwq", "L1_ffn_wq"), ("iwk", "L1_ffn_wk"),
                    ("iwv", "L1_ffn_wv"), ("iwo", "L1_ffn_wo")):
        w[nm + "_s"] = _slabify(asc(inp[key].T))
    w["wff1_s"] = _slabify(asc(inp["L1_ffn_w_ff1"].T))
    w["wff2_s"] = _slabify(asc(inp["L1_ffn_w_ff2"].T))
    w["ln1s_c"] = _chunkvec(inp["L1_ffn_ln1_scale"])
    w["ln1b_c"] = _chunkvec(inp["L1_ffn_ln1_bias"])
    w["ln2s_c"] = _chunkvec(inp["L1_ffn_ln2_scale"])
    w["ln2b_c"] = _chunkvec(inp["L1_ffn_ln2_bias"])
    rt = np.zeros((D, 128), dtype=np.float32)
    rt[:, 0] = inp["L1_ffn_w_router"][0]
    rt[:, 32] = inp["L1_ffn_w_router"][1]
    rt[:, 64] = inp["L1_ffn_w_router"][2]
    w["w_router_s"] = _slabify(rt)
    lrd = np.zeros((D, 128), dtype=np.float32)
    lrd[:, :32] = inp["L1_ffn_w_down"].T
    w["w_lrdown_s"] = _slabify(lrd)
    lru = np.zeros((128, D), dtype=np.float32)
    lru[:32, :] = inp["L1_ffn_w_up"].T
    w["w_lrup_s"] = _slabify(lru)
    w["ident"] = np.eye(128, dtype=np.float32)
    sel = np.zeros((16, 16 * 128), dtype=np.float32)
    for n in range(16):
        sel[n, n * 128:(n + 1) * 128] = 1.0
    w["sel16"] = asc(sel)
    cm = (np.arange(3072)[None, :] >= (np.arange(128)[:, None] + 1920))
    w["causal_m"] = asc(cm.astype(ml_dtypes.bfloat16))
    return w


WEIGHT_SPECS = {
    "w_xproj_s": ([3, 128, 16, 128], F32),
    "w_dt_s": ([16, 128, 2, 128], F32),
    "b_dt_c": ([128, 16], F32),
    "a_log_c": ([128, 128], F32),
    "d_param_c": ([128, 16], F32),
    "w_gate_s": ([64, 128, 16, 128], F32),
    "w_up_s": ([64, 128, 16, 128], F32),
    "w_down_s": ([16, 128, 64, 128], F32),
    "wq_s": ([16, 128, 16, 128], F32),
    "wk_s": ([16, 128, 16, 128], F32),
    "wv_s": ([16, 128, 16, 128], F32),
    "wo_s": ([16, 128, 16, 128], F32),
    "bq_c": ([128, 16], F32),
    "bk_c": ([128, 16], F32),
    "bv_c": ([128, 16], F32),
    "bo_c": ([128, 16], F32),
    "sbias_c": ([128, 16], F32),
    "iwq_s": ([16, 128, 16, 128], F32),
    "iwk_s": ([16, 128, 16, 128], F32),
    "iwv_s": ([16, 128, 16, 128], F32),
    "iwo_s": ([16, 128, 16, 128], F32),
    "wff1_s": ([64, 128, 16, 128], F32),
    "wff2_s": ([16, 128, 64, 128], F32),
    "ln1s_c": ([128, 16], F32),
    "ln1b_c": ([128, 16], F32),
    "ln2s_c": ([128, 16], F32),
    "ln2b_c": ([128, 16], F32),
    "w_router_s": ([1, 128, 16, 128], F32),
    "w_lrdown_s": ([1, 128, 16, 128], F32),
    "w_lrup_s": ([16, 128, 1, 128], F32),
    "ident": ([128, 128], F32),
    "sel16": ([16, 2048], F32),
    "causal_m": ([128, 3072], BF16),
}

BF_WEIGHTS = {
    "w_xproj_s": (3, 16), "w_dt_s": (16, 2),
    "w_gate_s": (64, 16), "w_up_s": (64, 16), "w_down_s": (16, 64),
    "wq_s": (16, 16), "wk_s": (16, 16), "wv_s": (16, 16), "wo_s": (16, 16),
    "iwq_s": (16, 16), "iwk_s": (16, 16), "iwv_s": (16, 16), "iwo_s": (16, 16),
    "wff1_s": (64, 16), "wff2_s": (16, 64),
    "w_router_s": (1, 16), "w_lrdown_s": (1, 16), "w_lrup_s": (16, 1),
}


def ph_wcast(B_, names):
    NB = 4
    stg = [B_.arena.alloc([128, 4096], F32) for _ in range(NB)]
    ob = [B_.arena.alloc([128, 4096], BF16) for _ in range(NB)]
    i = 0
    for nm in names:
        nslab, KCW = BF_WEIGHTS[nm]
        src = B_.dram[nm]
        dst = B_.dram["bf_" + nm]
        cols = KCW * 128
        for fi in range(nslab):
            for s0 in range(0, cols, 4096):
                w = min(4096, cols - s0)
                sv = src[fi].rearrange("p a b -> p (a b)")[:, s0:s0 + w]
                dv = dst[fi].rearrange("p a b -> p (a b)")[:, s0:s0 + w]
                par = i % NB
                i += 1
                B_.dma(stg[par].ap[:, :w], sv, writes=[stg[par].k()],
                       queue="scalar")
                B_.emit("gpsimd", lambda e, w=w, par=par: e.tensor_copy(
                    ob[par].ap[:, :w], stg[par].ap[:, :w]),
                    reads=[stg[par].k()], writes=[ob[par].k()])
                B_.dma(dv, ob[par].ap[:, :w], reads=[ob[par].k()],
                       writes=[f"bfw_{nm}/{fi}"], queue="scalar")
    for a in stg + ob:
        a.free()


def declare_weights(B_, bf_names=()):
    for nm, (shape, dtype) in WEIGHT_SPECS.items():
        B_.din(nm, shape, dtype)
    for nm in bf_names:
        shape = WEIGHT_SPECS[nm][0]
        B_.dscr("bf_" + nm, shape, BF16)



class Globals:
    def __init__(self, B_):
        self.ones_bf, self.b_ones = B_.sb([128, 128], BF16, "ones")
        B_.emit("vector", lambda e: e.memset(self.ones_bf[:], 1.0),
                writes=[self.b_ones.k()])
        self.ident, self.b_ident = B_.sb([128, 128], F32, "ident")
        B_.dma(self.ident[:], B_.dram["ident"], writes=[self.b_ident.k()])
        self.identb, self.b_identb = B_.sb([128, 128], BF16, "identb")
        B_.emit("vector", lambda e: e.tensor_copy(self.identb[:], self.ident[:]),
                reads=[self.b_ident.k()], writes=[self.b_identb.k()])
        self.vecs = {}
        for nm in ("b_dt_c", "d_param_c", "bq_c", "bk_c", "bv_c", "bo_c",
                   "sbias_c", "ln1s_c", "ln1b_c", "ln2s_c", "ln2b_c"):
            h, b = B_.sb([128, 16], F32, nm)
            B_.dma(h[:], B_.dram[nm], writes=[b.k()])
            self.vecs[nm] = (h, b)
        self.A, self.b_A = B_.sb([128, 128], F32, "A")
        B_.dma(self.A[:], B_.dram["a_log_c"], writes=[self.b_A.k()])
        B_.emit("scalar", lambda e: e.activation(self.A[:], self.A[:], AF.Exp),
                reads=[self.b_A.k()], writes=[self.b_A.k()])
        B_.emit("vector", lambda e: e.tensor_scalar(
            self.A[:], self.A[:], -1.0, None, ALU.mult),
            reads=[self.b_A.k()], writes=[self.b_A.k()])
        self.sel16, self.b_sel16 = B_.sb([16, 16 * 128], BF16, "sel16")
        self.cpool, self.b_cpool = B_.sb([128, 8], F32, "cpool")
        self._cmap = {}
        self._B = B_

    def cbias(self, value):
        if value not in self._cmap:
            col = len(self._cmap)
            assert col < 8
            self._B.emit("vector",
                         lambda e, col=col, value=value: e.memset(
                             self.cpool[:, col:col + 1], value),
                         writes=[self.b_cpool.k(col)])
            self._cmap[value] = col
        col = self._cmap[value]
        return self.cpool[:, col:col + 1], self.b_cpool.k(col)

    def load_sel16(self, B_):
        tmp = B_.arena.alloc([16, 2048], F32)
        B_.dma(tmp.ap[:], B_.dram["sel16"], writes=[tmp.k()])
        B_.emit("vector", lambda e: e.tensor_copy(self.sel16[:], tmp.ap[:]),
                reads=[tmp.k()], writes=[self.b_sel16.k()])
        tmp.free()

    def vec(self, nm, c):
        h, b = self.vecs[nm]
        return h[:, c:c + 1], b.k()


def load_xn(B_, xn_dram, tag):
    al = B_.arena.alloc([128, DC, TOK], BF16)
    for c in range(DC):
        B_.dma(al.ap[:, c], xn_dram[c], reads=[f"{tag}/{c}"],
               writes=[al.k(c)])
    return al


def proj_resident(B_, wname, nslab, KCW, rhs_ap, rhs_keys, evac, order=None,
                  slab_extra_reads=None):
    wdram = B_.dram["bf_" + wname]
    if order is None:
        order = list(range(nslab))
    wb = [B_.arena.alloc([128, KCW, 128], BF16) for _ in range(2)]
    for idx, fi in enumerate(order):
        w = wb[idx % 2]
        B_.dma(w.ap[:], wdram[fi],
               reads=[f"bfw_{wname}/{fi}"], writes=[w.k()], queue="scalar")
        for half in range(2):
            pa = B_.psum(4)

            def mmfn(e, w=w, pa=pa, half=half):
                last = None
                for kc in range(KCW):
                    for t4 in range(4):
                        last = e.matmul(
                            pa.ap[:, t4 * 512:(t4 + 1) * 512], w.ap[:, kc],
                            rhs_ap[:, kc, half * 2048 + t4 * 512:
                                   half * 2048 + (t4 + 1) * 512],
                            start=(kc == 0), stop=(kc == KCW - 1))
                return last
            B_.emit("tensor", mmfn,
                    reads=[w.k()] + list(rhs_keys) +
                          (slab_extra_reads(fi) if slab_extra_reads else []),
                    writes=[pa.k()])
            evac(fi, half, pa, [pa.k()])
            pa.free()
    for w in wb:
        w.free()



def ph_ln(B_, G, z_dram, xn_dram=None, z_is_bf16=False,
          scale_nm=None, bias_nm=None, resid_dram=None, out_dram=None,
          out_tag=None, z_tag=None, resid_tag=None):
    htok = TOK // 2
    stats = B_.arena.alloc([128, 2, TOK], F32)
    zstg = [B_.arena.alloc([128, 2048], BF16 if z_is_bf16 else F32)
            for _ in range(2)]
    zb = None if z_is_bf16 else [B_.arena.alloc([128, 2048], BF16)
                                 for _ in range(2)]
    z2 = [B_.arena.alloc([128, 2048], BF16) for _ in range(2)]
    for half in range(2):
        pa_s = B_.psum(4)
        pa_q = B_.psum(4)
        for c in range(DC):
            par = c % 2
            stg = zstg[par]
            B_.dma(stg.ap[:], z_dram[c, :, half * htok:(half + 1) * htok],
                   reads=[f"{z_tag}/{c}"] if z_tag else [],
                   writes=[stg.k()])
            if z_is_bf16:
                zb_ap, zb_k = stg.ap, stg.k()
            else:
                B_.emit("vector", lambda e, stg=stg, par=par: e.tensor_copy(
                    zb[par].ap[:], stg.ap[:]),
                    reads=[stg.k()], writes=[zb[par].k()])
                zb_ap, zb_k = zb[par].ap, zb[par].k()
            B_.emit("scalar", lambda e, stg=stg, par=par: e.activation(
                z2[par].ap[:], stg.ap[:], AF.Square),
                reads=[stg.k()], writes=[z2[par].k()])

            def mk_mm(src_ap, pa, c):
                def mms(e):
                    last = None
                    for t4 in range(4):
                        last = e.matmul(
                            pa.ap[:, t4 * 512:(t4 + 1) * 512], G.ones_bf[:],
                            src_ap[:, t4 * 512:(t4 + 1) * 512],
                            start=(c == 0), stop=(c == DC - 1))
                    return last
                return mms
            B_.emit("tensor", mk_mm(zb_ap, pa_s, c),
                    reads=[zb_k, G.b_ones.k()],
                    writes=[pa_s.k()] if c in (0, DC - 1) else [])
            B_.emit("tensor", mk_mm(z2[par].ap, pa_q, c),
                    reads=[z2[par].k(), G.b_ones.k()],
                    writes=[pa_q.k()] if c in (0, DC - 1) else [])
        hsl = slice(half * htok, (half + 1) * htok)
        m_b = B_.arena.alloc([128, 2048], F32)
        msq = B_.arena.alloc([128, 2048], F32)
        vpe = B_.arena.alloc([128, 2048], F32)
        B_.emit("vector", lambda e, m_b=m_b, pa=pa_s: e.tensor_scalar(
            m_b.ap[:], pa.ap[:, 0:2048], 1.0 / D, None, ALU.mult),
            reads=[pa_s.k()], writes=[m_b.k()])
        B_.emit("vector", lambda e, m_b=m_b, msq=msq: e.tensor_mul(
            msq.ap[:], m_b.ap[:], m_b.ap[:]),
            reads=[m_b.k()], writes=[msq.k()])
        B_.emit("vector", lambda e, vpe=vpe, msq=msq, pa=pa_q:
                e.scalar_tensor_tensor(vpe.ap[:], pa.ap[:, 0:2048], 1.0 / D,
                                       msq.ap[:], ALU.mult, ALU.subtract),
                reads=[pa_q.k(), msq.k()], writes=[vpe.k()])
        eps_ap, eps_k = G.cbias(LN_EPS)
        B_.emit("scalar", lambda e, vpe=vpe, eps_ap=eps_ap: e.activation(
            vpe.ap[:], vpe.ap[:], AF.Ln, bias=eps_ap),
            reads=[vpe.k(), eps_k], writes=[vpe.k()])
        B_.emit("scalar", lambda e, vpe=vpe, hsl=hsl: e.activation(
            stats.ap[:, 0, hsl], vpe.ap[:], AF.Exp, scale=-0.5),
            reads=[vpe.k()], writes=[stats.k(f"r{half}")])
        B_.emit("vector", lambda e, m_b=m_b, hsl=hsl: e.tensor_mul(
            stats.ap[:, 1, hsl], m_b.ap[:], stats.ap[:, 0, hsl]),
            reads=[m_b.k(), stats.k(f"r{half}")],
            writes=[stats.k(f"m{half}")])
        pa_s.free()
        pa_q.free()
        m_b.free()
        msq.free()
        vpe.free()
    for a in zstg + (zb or []) + z2:
        a.free()

    stat_reads = [stats.k("r0"), stats.k("r1"), stats.k("m0"), stats.k("m1")]
    zfull = [B_.arena.alloc([128, TOK], BF16 if z_is_bf16 else F32)
             for _ in range(2)]
    t1 = [B_.arena.alloc([128, TOK], F32) for _ in range(2)]
    xn_t = [B_.arena.alloc([128, TOK], BF16) for _ in range(2)] \
        if xn_dram is not None else None
    xm = [B_.arena.alloc([128, TOK], F32) for _ in range(2)] \
        if scale_nm is not None else None
    xs = [B_.arena.alloc([128, TOK], F32) for _ in range(2)] \
        if resid_dram is not None else None
    xo = [B_.arena.alloc([128, TOK], F32) for _ in range(2)] \
        if out_dram is not None else None
    for c in range(DC):
        par = c % 2
        stg = zfull[par]
        B_.dma(stg.ap[:], z_dram[c], reads=[f"{z_tag}/{c}"] if z_tag else [],
               writes=[stg.k()])
        eng = "vector" if (c % 3) != 2 else "gpsimd"
        B_.emit(eng, lambda e, stg=stg, par=par: e.tensor_mul(
            t1[par].ap[:], stg.ap[:], stats.ap[:, 0]),
            reads=[stg.k()] + stat_reads, writes=[t1[par].k()])
        if xn_dram is not None:
            if scale_nm is None:
                B_.emit(eng, lambda e, par=par: e.tensor_sub(
                    xn_t[par].ap[:], t1[par].ap[:], stats.ap[:, 1]),
                    reads=[t1[par].k()] + stat_reads, writes=[xn_t[par].k()])
            else:
                B_.emit(eng, lambda e, par=par: e.tensor_sub(
                    xm[par].ap[:], t1[par].ap[:], stats.ap[:, 1]),
                    reads=[t1[par].k()] + stat_reads, writes=[xm[par].k()])
                sap, sk = G.vec(scale_nm, c)
                bap, bk = G.vec(bias_nm, c)
                B_.emit("vector", lambda e, par=par, sap=sap, bap=bap:
                        e.tensor_scalar(xn_t[par].ap[:], xm[par].ap[:], sap,
                                        bap, ALU.mult, ALU.add),
                        reads=[xm[par].k(), sk, bk], writes=[xn_t[par].k()])
            B_.dma(xn_dram[c], xn_t[par].ap[:], reads=[xn_t[par].k()],
                   writes=[f"{out_tag}/{c}"])
        else:
            B_.dma(xs[par].ap[:], resid_dram[c],
                   reads=[f"{resid_tag}/{c}"] if resid_tag else [],
                   writes=[xs[par].k()])
            B_.emit(eng, lambda e, par=par: e.tensor_sub(
                xo[par].ap[:], t1[par].ap[:], stats.ap[:, 1]),
                reads=[t1[par].k()] + stat_reads, writes=[xo[par].k()])
            B_.emit(eng, lambda e, par=par: e.tensor_add(
                xo[par].ap[:], xo[par].ap[:], xs[par].ap[:]),
                reads=[xo[par].k(), xs[par].k()], writes=[xo[par].k()])
            B_.dma(out_dram[c], xo[par].ap[:], reads=[xo[par].k()],
                   writes=[f"{out_tag}/{c}"])
    for a in zfull + t1 + (xn_t or []) + (xm or []) + (xs or []) + (xo or []):
        a.free()
    stats.free()



def ph_entry(B_, G, x0, xres):
    GRP = 4
    NG = (TOK // 128) // GRP
    tin = B_.arena.alloc([128, GRP, D], F32)
    tout = B_.arena.alloc([128, DC, GRP * 128], F32)
    for g in range(NG):
        for r in range(GRP):
            rc = g * GRP + r
            B_.dma(tin.ap[:, r], x0[rc * 128:(rc + 1) * 128, :],
                   writes=[tin.k(r)])
        for c in range(DC):
            pa = B_.psum(1)

            def tfn(e, c=c, pa=pa):
                last = None
                for r in range(GRP):
                    last = e.transpose(pa.ap[:, r * 128:(r + 1) * 128],
                                       tin.ap[:, r, c * 128:(c + 1) * 128],
                                       G.ident[:])
                return last
            B_.emit("tensor", tfn,
                    reads=[tin.k(r) for r in range(GRP)] + [G.b_ident.k()],
                    writes=[pa.k()])
            B_.emit("vector", lambda e, c=c, pa=pa: e.tensor_copy(
                tout.ap[:, c], pa.ap[:, 0:GRP * 128]),
                reads=[pa.k()], writes=[tout.k(c)])
            pa.free()
        for c in range(DC):
            B_.dma(xres[c, :, g * GRP * 128:(g + 1) * GRP * 128],
                   tout.ap[:, c], reads=[tout.k(c)], writes=[f"xres0/{c}"])
    tin.free()
    tout.free()


def ph_exit(B_, G, xres, xout, z_tag):
    GRP = 4
    NG = DC // GRP
    tin = B_.arena.alloc([128, GRP, TOK], F32)
    tout = B_.arena.alloc([128, TOK // 128, GRP * 128], F32)
    for g in range(NG):
        for r in range(GRP):
            c = g * GRP + r
            B_.dma(tin.ap[:, r], xres[c], reads=[f"{z_tag}/{c}"],
                   writes=[tin.k(r)])
        for rc in range(TOK // 128):
            pa = B_.psum(1)

            def tfn(e, rc=rc, pa=pa):
                last = None
                for r in range(GRP):
                    last = e.transpose(pa.ap[:, r * 128:(r + 1) * 128],
                                       tin.ap[:, r, rc * 128:(rc + 1) * 128],
                                       G.ident[:])
                return last
            B_.emit("tensor", tfn,
                    reads=[tin.k(r) for r in range(GRP)] + [G.b_ident.k()],
                    writes=[pa.k()])
            B_.emit("vector", lambda e, rc=rc, pa=pa: e.tensor_copy(
                tout.ap[:, rc], pa.ap[:, 0:GRP * 128]),
                reads=[pa.k()], writes=[tout.k(rc)])
            pa.free()
        for rc in range(TOK // 128):
            B_.dma(xout[rc * 128:(rc + 1) * 128,
                        g * GRP * 128:(g + 1) * GRP * 128],
                   tout.ap[:, rc], reads=[tout.k(rc)], writes=["xout_done"])
    tin.free()
    tout.free()



def ph_l0_mixer(B_, G, xn_dram, y_dram, xn_tag):
    G.load_sel16(B_)
    sub = MIXER_SUB[0]
    xn = load_xn(B_, xn_dram, xn_tag)
    xn_keys = [xn.k(c) for c in range(DC)]
    dtr = B_.arena.alloc([128, 2, TOK], BF16)
    bcsb = B_.arena.alloc([16, TOK], BF16)

    def evac_xp(fi, half, pa, reads):
        hsl = slice(half * 2048, (half + 1) * 2048)
        if fi < 2:
            B_.emit("vector", lambda e, fi=fi, hsl=hsl, pa=pa: e.tensor_copy(
                dtr.ap[:, fi, hsl], pa.ap[:, 0:2048]),
                reads=reads, writes=[dtr.k(fi)])
        else:
            B_.emit("vector", lambda e, hsl=hsl, pa=pa: e.tensor_copy(
                bcsb.ap[:, hsl], pa.ap[0:16, 0:2048]),
                reads=reads, writes=[bcsb.k(half)])
    if sub != 24:
        proj_resident(B_, "w_xproj_s", 3, DC, xn.ap, xn_keys, evac_xp)
    if sub <= 1:
        zt = B_.arena.alloc([128, TOK], F32)
        for c in range(DC):
            if c < 2:
                B_.emit("vector", lambda e, c=c: e.tensor_copy(
                    zt.ap[:], dtr.ap[:, c]), reads=[dtr.k(c)], writes=[zt.k()])
            else:
                B_.emit("vector", lambda e: e.memset(zt.ap[:], 0.0),
                        writes=[zt.k()])
            B_.dma(y_dram[c], zt.ap[:], reads=[zt.k()], writes=[f"l0_y/{c}"])
        zt.free()
        xn.free()
        dtr.free()
        bcsb.free()
        return

    delta_dram = B_.dscr("l0_delta", [DC, 128, TOK], BF16)
    u_dram = B_.dscr("l0_u", [DC, 128, TOK], BF16)
    dl_t = B_.arena.alloc([128, TOK], BF16)
    u_t = B_.arena.alloc([128, TOK], BF16)
    ab_t = B_.arena.alloc([128, 2048], F32)
    lg_t = B_.arena.alloc([128, 2048], F32)
    mz_t = B_.arena.alloc([128, 2048], F32)

    def evac_dt(fi, half, pa, reads):
        c = fi
        hsl = slice(half * 2048, (half + 1) * 2048)
        bap, bk = G.vec("b_dt_c", c)
        if MIXER_SUB[0] in (21, 24):
            B_.emit("vector", lambda e, pa=pa, hsl=hsl: e.tensor_copy(
                dl_t.ap[:, hsl], pa.ap[:, 0:2048]),
                reads=reads, writes=[dl_t.k()])
            if half == 1:
                B_.dma(delta_dram[c], dl_t.ap[:], reads=[dl_t.k()],
                       writes=[f"l0_delta/{c}"])
                B_.dma(u_dram[c], dl_t.ap[:], reads=[dl_t.k()],
                       writes=[f"l0_u/{c}"])
            return
        if MIXER_SUB[0] == 22:
            B_.emit("scalar", lambda e, pa=pa, bap=bap: e.activation(
                ab_t.ap[:], pa.ap[:, 0:2048], AF.Abs, bias=bap),
                reads=reads + [bk], writes=[ab_t.k()])
            B_.emit("scalar", lambda e: e.activation(
                ab_t.ap[:], ab_t.ap[:], AF.Exp, scale=-1.0),
                reads=[ab_t.k()], writes=[ab_t.k()])
            B_.emit("scalar", lambda e: e.activation(
                lg_t.ap[:], ab_t.ap[:], AF.Ln, bias=1.0),
                reads=[ab_t.k()], writes=[lg_t.k()])
            B_.emit("vector", lambda e, pa=pa, bap=bap: e.tensor_scalar(
                mz_t.ap[:], pa.ap[:, 0:2048], bap, 0.0, ALU.add, ALU.max),
                reads=reads + [bk], writes=[mz_t.k()])
            B_.emit("vector", lambda e, hsl=hsl: e.tensor_add(
                dl_t.ap[:, hsl], mz_t.ap[:], lg_t.ap[:]),
                reads=[mz_t.k(), lg_t.k()], writes=[dl_t.k()])
            if half == 1:
                B_.dma(delta_dram[c], dl_t.ap[:], reads=[dl_t.k()],
                       writes=[f"l0_delta/{c}"])
                B_.dma(u_dram[c], dl_t.ap[:], reads=[dl_t.k()],
                       writes=[f"l0_u/{c}"])
            return
        ev_abs = B_.emit("scalar", lambda e, pa=pa, bap=bap: e.activation(
            ab_t.ap[:], pa.ap[:, 0:2048], AF.Abs, bias=bap),
            reads=reads + [bk], writes=[ab_t.k()])
        B_.emit("scalar", lambda e: e.activation(
            ab_t.ap[:], ab_t.ap[:], AF.Exp, scale=-1.0),
            reads=[ab_t.k()], writes=[ab_t.k()])
        B_.emit("scalar", lambda e: e.activation(
            lg_t.ap[:], ab_t.ap[:], AF.Ln, bias=1.0),
            reads=[ab_t.k()], writes=[lg_t.k()])
        B_.emit("vector", lambda e, pa=pa, bap=bap: e.tensor_scalar(
            mz_t.ap[:], pa.ap[:, 0:2048], bap, 0.0, ALU.add, ALU.max),
            reads=reads + [bk], writes=[mz_t.k()], extra_waits=[ev_abs])
        B_.emit("vector", lambda e, hsl=hsl: e.tensor_add(
            dl_t.ap[:, hsl], mz_t.ap[:], lg_t.ap[:]),
            reads=[mz_t.k(), lg_t.k()], writes=[dl_t.k()])
        B_.emit("vector", lambda e, hsl=hsl, c=c: e.tensor_mul(
            u_t.ap[:, hsl], dl_t.ap[:, hsl], xn.ap[:, c, hsl]),
            reads=[dl_t.k(), xn.k(c)], writes=[u_t.k()])
        if half == 1:
            B_.dma(delta_dram[c], dl_t.ap[:], reads=[dl_t.k()],
                   writes=[f"l0_delta/{c}"])
            B_.dma(u_dram[c], u_t.ap[:], reads=[u_t.k()],
                   writes=[f"l0_u/{c}"])
    if MIXER_SUB[0] in (23, 24):
        proj_resident(B_, "w_dt_s", DC, 2, xn.ap, [xn.k(0), xn.k(1)], evac_dt)
    else:
        proj_resident(B_, "w_dt_s", DC, 2, dtr.ap, [dtr.k(0), dtr.k(1)], evac_dt)
    for a in (dl_t, u_t, ab_t, lg_t, mz_t):
        a.free()
    xn.free()
    dtr.free()
    if sub <= 2:
        zt = B_.arena.alloc([128, TOK], F32)
        zb2 = B_.arena.alloc([128, TOK], BF16)
        for c in range(DC):
            B_.dma(zb2.ap[:], delta_dram[c], reads=[f"l0_delta/{c}"],
                   writes=[zb2.k()])
            B_.emit("vector", lambda e: e.tensor_copy(zt.ap[:], zb2.ap[:]),
                    reads=[zb2.k()], writes=[zt.k()])
            B_.dma(y_dram[c], zt.ap[:], reads=[zt.k()], writes=[f"l0_y/{c}"])
        zt.free()
        zb2.free()
        bcsb.free()
        return

    dsl = B_.arena.alloc([128, TOK], BF16)
    usl = B_.arena.alloc([128, TOK], BF16)
    xsl = B_.arena.alloc([128, TOK], BF16)
    ysl = B_.arena.alloc([128, TOK], F32)
    dA = B_.arena.alloc([128, 2048], F32)
    dBu = B_.arena.alloc([128, 2048], F32)
    hsc = B_.arena.alloc([128, 2048], F32)
    tmul = B_.arena.alloc([128, 2048], F32)
    for sl in range(DC):
        if sl >= SCAN_SLABS[0]:
            B_.emit("vector", lambda e: e.memset(ysl.ap[:], 0.0),
                    writes=[ysl.k()])
            B_.dma(y_dram[sl], ysl.ap[:], reads=[ysl.k()],
                   writes=[f"l0_y/{sl}"])
            continue
        B_.dma(dsl.ap[:], delta_dram[sl], reads=[f"l0_delta/{sl}"],
               writes=[dsl.k()])
        B_.dma(usl.ap[:], u_dram[sl], reads=[f"l0_u/{sl}"], writes=[usl.k()])
        B_.dma(xsl.ap[:], xn_dram[sl], reads=[f"{xn_tag}/{sl}"],
               writes=[xsl.k()])
        dap, dk = G.vec("d_param_c", sl)
        B_.emit("vector", lambda e, dap=dap: e.tensor_scalar(
            ysl.ap[:], xsl.ap[:], dap, None, ALU.mult),
            reads=[xsl.k(), dk], writes=[ysl.k()])
        for b in range(B):
            bsl = slice(b * 2048, (b + 1) * 2048)
            for n in range(N_SSM):
                paB = B_.psum(4)
                paC = B_.psum(4)

                def bcfn(e, b=b, n=n, paB=paB, paC=paC):
                    last = None
                    for q in range(4):
                        qs = slice(b * 2048 + q * 512,
                                   b * 2048 + (q + 1) * 512)
                        last = e.matmul(paB.ap[:, q * 512:(q + 1) * 512],
                                        G.sel16[:, n * 128:(n + 1) * 128],
                                        bcsb.ap[:, qs], start=True, stop=True)
                        last = e.matmul(paC.ap[:, q * 512:(q + 1) * 512],
                                        G.sel16[:, (8 + n) * 128:(9 + n) * 128],
                                        bcsb.ap[:, qs], start=True, stop=True)
                    return last
                B_.emit("tensor", bcfn,
                        reads=[bcsb.k(0), bcsb.k(1), G.b_sel16.k()],
                        writes=[paB.k(), paC.k()])
                B_.emit("scalar", lambda e, sl=sl, n=n, bsl=bsl: e.activation(
                    dA.ap[:], dsl.ap[:, bsl], AF.Exp,
                    scale=G.A[:, sl * 8 + n:sl * 8 + n + 1]),
                    reads=[dsl.k(), G.b_A.k()], writes=[dA.k()])
                B_.emit("vector", lambda e, bsl=bsl, paB=paB: e.tensor_mul(
                    dBu.ap[:], usl.ap[:, bsl], paB.ap[:, 0:2048]),
                    reads=[usl.k(), paB.k()], writes=[dBu.k()])
                B_.emit("vector", lambda e: e.tensor_tensor_scan(
                    hsc.ap[:], dA.ap[:], dBu.ap[:], 0.0, ALU.mult, ALU.add),
                    reads=[dA.k(), dBu.k()], writes=[hsc.k()])
                B_.emit("vector", lambda e, paC=paC: e.tensor_mul(
                    tmul.ap[:], hsc.ap[:], paC.ap[:, 0:2048]),
                    reads=[hsc.k(), paC.k()], writes=[tmul.k()])
                B_.emit("gpsimd", lambda e, bsl=bsl: e.tensor_add(
                    ysl.ap[:, bsl], ysl.ap[:, bsl], tmul.ap[:]),
                    reads=[tmul.k(), ysl.k()], writes=[ysl.k()])
                paB.free()
                paC.free()
        B_.dma(y_dram[sl], ysl.ap[:], reads=[ysl.k()], writes=[f"l0_y/{sl}"])
    for a in (dsl, usl, xsl, ysl, dA, dBu, hsc, tmul, bcsb):
        a.free()



def ph_ffn24(B_, G, xn_dram, y_dram, xn_tag):
    xn = load_xn(B_, xn_dram, xn_tag)
    xn_keys = [xn.k(c) for c in range(DC)]
    hsp_dram = B_.dscr("l0_hsp", [FFC, 128, TOK], BF16)
    wgd = B_.dram["bf_w_gate_s"]
    wud = B_.dram["bf_w_up_s"]
    hbuf = [B_.arena.alloc([128, 2048], BF16) for _ in range(4)]
    absb = [B_.arena.alloc([128, 2048], BF16) for _ in range(4)]
    q_t = B_.arena.alloc([128, 2048], BF16)
    s_t = B_.arena.alloc([128, 2048], BF16)
    msk = B_.arena.alloc([128, 2048], BF16)
    wgb = [B_.arena.alloc([128, DC, 128], BF16) for _ in range(2)]
    wub = [B_.arena.alloc([128, DC, 128], BF16) for _ in range(2)]

    def one_mm(wap, pa, rhs_off):
        def mmfn(e):
            last = None
            for kc in range(DC):
                for t4 in range(4):
                    last = e.matmul(
                        pa.ap[:, t4 * 512:(t4 + 1) * 512], wap[:, kc],
                        xn.ap[:, kc, rhs_off + t4 * 512:
                              rhs_off + (t4 + 1) * 512],
                        start=(kc == 0), stop=(kc == DC - 1))
            return last
        return mmfn

    for gc in range(16):
        for half in range(2):
            hsl = slice(half * 2048, (half + 1) * 2048)
            for j in range(4):
                cp = 16 * j + gc
                par = j % 2
                wg, wu = wgb[par], wub[par]
                B_.dma(wg.ap[:], wgd[cp], reads=[f"bfw_w_gate_s/{cp}"],
                       writes=[wg.k()], queue="scalar")
                B_.dma(wu.ap[:], wud[cp], reads=[f"bfw_w_up_s/{cp}"],
                       writes=[wu.k()], queue="scalar")
                pa_g = B_.psum(4)
                B_.emit("tensor", one_mm(wg.ap, pa_g, half * 2048),
                        reads=[wg.k()] + xn_keys, writes=[pa_g.k()])
                B_.emit("scalar", lambda e, pa=pa_g, j=j:
                        e.activation(hbuf[j].ap[:], pa.ap[:, 0:2048],
                                     AF.Silu),
                        reads=[pa_g.k()], writes=[hbuf[j].k()])
                pa_g.free()
                pa_u = B_.psum(4)
                B_.emit("tensor", one_mm(wu.ap, pa_u, half * 2048),
                        reads=[wu.k()] + xn_keys, writes=[pa_u.k()])
                B_.emit("vector", lambda e, pa=pa_u, j=j:
                        e.tensor_mul(hbuf[j].ap[:], hbuf[j].ap[:],
                                     pa.ap[:, 0:2048]),
                        reads=[pa_u.k(), hbuf[j].k()], writes=[hbuf[j].k()])
                pa_u.free()
            for j in range(4):
                B_.emit("vector", lambda e, j=j: e.tensor_scalar(
                    absb[j].ap[:].bitcast(U16), hbuf[j].ap[:].bitcast(U16),
                    0x7FFF, None, ALU.bitwise_and),
                    reads=[hbuf[j].k()], writes=[absb[j].k()])
            B_.emit("vector", lambda e: e.tensor_tensor(
                q_t.ap[:], absb[0].ap[:], absb[1].ap[:], ALU.min),
                reads=[absb[0].k(), absb[1].k()], writes=[q_t.k()])
            B_.emit("vector", lambda e: e.tensor_tensor(
                s_t.ap[:], absb[2].ap[:], absb[3].ap[:], ALU.min),
                reads=[absb[2].k(), absb[3].k()], writes=[s_t.k()])
            B_.emit("vector", lambda e: e.tensor_max(
                absb[0].ap[:], absb[0].ap[:], absb[1].ap[:]),
                reads=[absb[0].k(), absb[1].k()], writes=[absb[0].k()])
            B_.emit("vector", lambda e: e.tensor_max(
                absb[2].ap[:], absb[2].ap[:], absb[3].ap[:]),
                reads=[absb[2].k(), absb[3].k()], writes=[absb[2].k()])
            B_.emit("vector", lambda e: e.tensor_tensor(
                absb[0].ap[:], absb[0].ap[:], absb[2].ap[:], ALU.min),
                reads=[absb[0].k(), absb[2].k()], writes=[absb[0].k()])
            B_.emit("vector", lambda e: e.tensor_max(
                q_t.ap[:], q_t.ap[:], s_t.ap[:]),
                reads=[q_t.k(), s_t.k()], writes=[q_t.k()])
            B_.emit("vector", lambda e: e.tensor_max(
                q_t.ap[:], absb[0].ap[:], q_t.ap[:]),
                reads=[absb[0].k(), q_t.k()], writes=[q_t.k()])
            for j in range(4):
                cp = 16 * j + gc
                B_.emit("vector", lambda e, j=j: e.tensor_scalar(
                    msk.ap[:].bitcast(U16), hbuf[j].ap[:].bitcast(U16),
                    0x7FFF, None, ALU.bitwise_and),
                    reads=[hbuf[j].k()], writes=[msk.k()])
                B_.emit("vector", lambda e: e.tensor_tensor(
                    msk.ap[:], msk.ap[:], q_t.ap[:], ALU.is_ge),
                    reads=[msk.k(), q_t.k()], writes=[msk.k()])
                B_.emit("gpsimd", lambda e, j=j: e.tensor_mul(
                    hbuf[j].ap[:], hbuf[j].ap[:], msk.ap[:]),
                    reads=[msk.k(), hbuf[j].k()], writes=[hbuf[j].k()])
                B_.dma(hsp_dram[cp, :, hsl], hbuf[j].ap[:],
                       reads=[hbuf[j].k()], writes=[f"l0_hsp/{cp}"])
    xn.free()
    for a in hbuf + absb + [q_t, s_t, msk] + wgb + wub:
        a.free()

    wdd = B_.dram["bf_w_down_s"]
    wdh = [B_.arena.alloc([128, FFC, 128], BF16) for _ in range(8)]
    hsp_t = [B_.arena.alloc([128, 8, 512], BF16) for _ in range(2)]
    yev = [B_.arena.alloc([128, 512], F32) for _ in range(2)]
    for dh2 in range(2):
        for i in range(8):
            dc = dh2 * 8 + i
            B_.dma(wdh[i].ap[:], wdd[dc], reads=[f"bfw_w_down_s/{dc}"],
                   writes=[wdh[i].k()], queue="scalar")
        for ttn in range(8):
            pa = B_.psum(8)
            for kg in range(8):
                par = kg % 2
                ht = hsp_t[par]
                B_.dma(ht.ap[:],
                       hsp_dram[kg * 8:(kg + 1) * 8, :,
                                ttn * 512:(ttn + 1) * 512]
                       .rearrange("a p t -> p a t"),
                       reads=[f"l0_hsp/{kg * 8 + i2}" for i2 in range(8)],
                       writes=[ht.k()])

                def mmfn(e, kg=kg, ht=ht, pa=pa):
                    last = None
                    for i2 in range(8):
                        kc = kg * 8 + i2
                        for i in range(8):
                            last = e.matmul(pa.ap[:, i * 512:(i + 1) * 512],
                                            wdh[i].ap[:, kc], ht.ap[:, i2],
                                            start=(kc == 0),
                                            stop=(kc == FFC - 1))
                    return last
                B_.emit("tensor", mmfn,
                        reads=[ht.k()] + [wdh[i].k() for i in range(8)],
                        writes=[pa.k()] if kg in (0, 7) else [])
            for i in range(8):
                dc = dh2 * 8 + i
                par = i % 2
                B_.emit("vector", lambda e, i=i, pa=pa, par=par: e.tensor_copy(
                    yev[par].ap[:], pa.ap[:, i * 512:(i + 1) * 512]),
                    reads=[pa.k()], writes=[yev[par].k()])
                B_.dma(y_dram[dc, :, ttn * 512:(ttn + 1) * 512],
                       yev[par].ap[:], reads=[yev[par].k()],
                       writes=[f"l0ffn_y/{dc}"])
            pa.free()
    for a in wdh + hsp_t + yev:
        a.free()



def ph_mha(B_, G, xn_dram, y_dram, pfx, wnames, bias_names=None, causal=False,
           inner_resid=False, xn_tag="xn", y_tag=None, causal_sb=None):
    xn = load_xn(B_, xn_dram, xn_tag)
    xn_keys = [xn.k(c) for c in range(DC)]
    qd = B_.dscr(pfx + "_q", [DC, 128, TOK], BF16)
    kd = B_.dscr(pfx + "_k", [DC, 128, TOK], BF16)
    vd = B_.dscr(pfx + "_v", [DC, 128, TOK], BF16)
    negm = B_.arena.alloc([128, 96], F32)
    sq_t = B_.arena.alloc([128, 2048], BF16)
    red_t = B_.arena.alloc([128, 1], F32)
    out_t = [B_.arena.alloc([128, 2048], BF16) for _ in range(2)]
    B_.emit("vector", lambda e: e.memset(negm.ap[:], 0.0),
            writes=[negm.k(i) for i in range(96)])

    wq, wk, wv, wo = wnames
    bq = bk_ = bv = bo = None
    if bias_names:
        bq, bk_, bv, bo = bias_names

    def evac_qkv(which, out_dram, bias_nm, base_col):
        def evac(fi, half, pa, reads):
            c = fi
            par = half
            hsl = slice(half * 2048, (half + 1) * 2048)
            ot = out_t[par]
            if bias_nm is not None:
                bap, bk2 = G.vec(bias_nm, c)
                B_.emit("scalar", lambda e, pa=pa, bap=bap, ot=ot:
                        e.activation(ot.ap[:], pa.ap[:, 0:2048], AF.Identity,
                                     bias=bap),
                        reads=reads + [bk2], writes=[ot.k()])
            else:
                B_.emit("scalar", lambda e, pa=pa, ot=ot: e.activation(
                    ot.ap[:], pa.ap[:, 0:2048], AF.Copy),
                    reads=reads, writes=[ot.k()])
            B_.dma(out_dram[c, :, hsl], ot.ap[:], reads=[ot.k()],
                   writes=[f"{pfx}_{which}/{c}"])
            if base_col >= 0:
                B_.emit("scalar", lambda e, ot=ot: e.activation(
                    sq_t.ap[:], ot.ap[:], AF.Square),
                    reads=[ot.k()], writes=[sq_t.k()])
                pn = B_.psum(4)

                def nfn(e, pn=pn):
                    last = None
                    for t4 in range(4):
                        last = e.matmul(pn.ap[:, t4 * 512:(t4 + 1) * 512],
                                        G.ones_bf[:],
                                        sq_t.ap[:, t4 * 512:(t4 + 1) * 512],
                                        start=True, stop=True)
                    return last
                B_.emit("tensor", nfn, reads=[sq_t.k(), G.b_ones.k()],
                        writes=[pn.k()])
                col = base_col + half * 16 + c
                B_.emit("vector", lambda e, pn=pn: e.reduce_max(
                    red_t.ap[:], pn.ap[:, 0:2048], axis=AX.X),
                    reads=[pn.k()], writes=[red_t.k()])
                B_.emit("vector", lambda e, col=col: e.tensor_max(
                    negm.ap[:, col:col + 1], negm.ap[:, col:col + 1],
                    red_t.ap[:]),
                    reads=[red_t.k(), negm.k(col)], writes=[negm.k(col)])
                pn.free()
        return evac

    proj_resident(B_, wq, DC, DC, xn.ap, xn_keys, evac_qkv("q", qd, bq, 0))
    proj_resident(B_, wk, DC, DC, xn.ap, xn_keys, evac_qkv("k", kd, bk_, 32))
    proj_resident(B_, wv, DC, DC, xn.ap, xn_keys, evac_qkv("v", vd, bv, -1))
    sq_t.free()
    red_t.free()
    for a in out_t:
        a.free()
    xn.free()

    for b in range(B):
        for h in range(NHEAD):
            i = b * 16 + h
            cq, ck, co = i, 32 + i, 64 + i
            B_.emit("vector", lambda e, cq=cq, ck=ck, co=co: e.tensor_mul(
                negm.ap[:, co:co + 1], negm.ap[:, cq:cq + 1],
                negm.ap[:, ck:ck + 1]),
                reads=[negm.k(cq), negm.k(ck)], writes=[negm.k(co)])
            t30_ap, t30_k = G.cbias(1e-30)
            B_.emit("scalar", lambda e, co=co, t30_ap=t30_ap: e.activation(
                negm.ap[:, co:co + 1], negm.ap[:, co:co + 1], AF.Ln,
                bias=t30_ap),
                reads=[negm.k(co), t30_k], writes=[negm.k(co)])
            B_.emit("scalar", lambda e, co=co: e.activation(
                negm.ap[:, co:co + 1], negm.ap[:, co:co + 1], AF.Exp,
                scale=0.5),
                reads=[negm.k(co)], writes=[negm.k(co)])
            B_.emit("vector", lambda e, co=co: e.tensor_scalar(
                negm.ap[:, co:co + 1], negm.ap[:, co:co + 1], -RSQ_DH, None,
                ALU.mult),
                reads=[negm.k(co)], writes=[negm.k(co)])

    ctx = B_.arena.alloc([128, DC, TOK], BF16)
    qh_t = [B_.arena.alloc([128, 2048], BF16) for _ in range(2)]
    kh_t = [B_.arena.alloc([128, 2048], BF16) for _ in range(2)]
    vh_t = [B_.arena.alloc([128, 2048], BF16) for _ in range(2)]
    vtm = [B_.arena.alloc([128, 16, 128], BF16) for _ in range(2)]
    pt = [B_.arena.alloc([128, 1024], BF16) for _ in range(2)]
    rec_t = B_.arena.alloc([128, 1024], F32)
    for bh in range(B * NHEAD):
        b, h = bh // NHEAD, bh % NHEAD
        hp = bh % 2
        bsl = slice(b * 2048, (b + 1) * 2048)
        co = 64 + b * 16 + h
        qh, kh, vh, vt = qh_t[hp], kh_t[hp], vh_t[hp], vtm[hp]
        B_.dma(qh.ap[:], qd[h, :, bsl], reads=[f"{pfx}_q/{h}"],
               writes=[qh.k()])
        B_.dma(kh.ap[:], kd[h, :, bsl], reads=[f"{pfx}_k/{h}"],
               writes=[kh.k()])
        B_.dma(vh.ap[:], vd[h, :, bsl], reads=[f"{pfx}_v/{h}"],
               writes=[vh.k()])
        pv = B_.psum(2)

        def tv(e, pv=pv, vh=vh):
            last = None
            for kc in range(16):
                last = e.transpose(
                    pv.ap.bitcast(BF16)[:, kc * 128:(kc + 1) * 128],
                    vh.ap[:, kc * 128:(kc + 1) * 128], G.identb[:])
            return last
        B_.emit("tensor", tv, reads=[vh.k(), G.b_identb.k()],
                writes=[pv.k()])
        B_.emit("vector", lambda e, pv=pv, vt=vt: e.tensor_copy(
            vt.ap[:], pv.ap.bitcast(BF16)[:, 0:2048]),
            reads=[pv.k()], writes=[vt.k()])
        pv.free()
        for qh2 in range(2):
            Q0 = qh2 * 1024
            pa_rs = B_.psum(2)
            pa_av = B_.psum(2)
            kcl = list(range(16)) if not causal else \
                list(range((Q0 + 1024 + 127) // 128))
            pst = [B_.psum(2), B_.psum(2)]
            nk = len(kcl)

            def emit_st(ki):
                kc = kcl[ki]
                par = ki % 2
                pa = pst[par]

                def stfn(e, kc=kc, pa=pa, qh2=qh2, qh=qh, kh=kh):
                    last = None
                    for qq in range(2):
                        last = e.matmul(
                            pa.ap[:, qq * 512:(qq + 1) * 512],
                            kh.ap[:, kc * 128:(kc + 1) * 128],
                            qh.ap[:, qh2 * 1024 + qq * 512:
                                  qh2 * 1024 + (qq + 1) * 512],
                            start=True, stop=True)
                    return last
                B_.emit("tensor", stfn, reads=[kh.k(), qh.k()],
                        writes=[pa.k()])
                B_.emit("scalar", lambda e, pa=pa, par=par, co=co:
                        e.activation(pt[par].ap[:], pa.ap[:, 0:1024],
                                     AF.Exp, bias=negm.ap[:, co:co + 1],
                                     scale=RSQ_DH),
                        reads=[pa.k(), negm.k(co)],
                        writes=[pt[par].k()])
                K0 = kc * 128
                if causal and K0 + 127 > Q0:
                    off = 1920 - (K0 - Q0)
                    B_.emit("vector", lambda e, par=par, off=off:
                            e.tensor_mul(pt[par].ap[:], pt[par].ap[:],
                                         causal_sb.ap[:, off:off + 1024]),
                            reads=[pt[par].k(), causal_sb.k()],
                            writes=[pt[par].k()])

            def emit_rs_av(ki):
                kc = kcl[ki]
                par = ki % 2
                first = (ki == 0)
                last_k = (ki == nk - 1)

                def rsfn(e, par=par, pa=pa_rs, first=first, last_k=last_k):
                    last = None
                    for qq in range(2):
                        last = e.matmul(pa.ap[:, qq * 512:(qq + 1) * 512],
                                        G.ones_bf[:],
                                        pt[par].ap[:, qq * 512:(qq + 1) * 512],
                                        start=first, stop=last_k)
                    return last
                B_.emit("tensor", rsfn, reads=[pt[par].k(), G.b_ones.k()],
                        writes=[pa_rs.k()] if (first or last_k) else [])

                def avfn(e, par=par, pa=pa_av, kc=kc, first=first,
                         last_k=last_k, vt=vt):
                    last = None
                    for qq in range(2):
                        last = e.matmul(pa.ap[:, qq * 512:(qq + 1) * 512],
                                        vt.ap[:, kc],
                                        pt[par].ap[:, qq * 512:(qq + 1) * 512],
                                        start=first, stop=last_k)
                    return last
                B_.emit("tensor", avfn, reads=[pt[par].k(), vt.k()],
                        writes=[pa_av.k()] if (first or last_k) else [])

            emit_st(0)
            for ki in range(nk):
                if ki + 1 < nk:
                    emit_st(ki + 1)
                emit_rs_av(ki)
            B_.emit("vector", lambda e, pa=pa_rs: e.reciprocal(
                rec_t.ap[:], pa.ap[:, 0:1024]),
                reads=[pa_rs.k()], writes=[rec_t.k()])
            B_.emit("vector", lambda e, pa=pa_av, h=h, b=b, Q0=Q0:
                    e.tensor_mul(
                        ctx.ap[:, h, b * 2048 + Q0:b * 2048 + Q0 + 1024],
                        pa.ap[:, 0:1024], rec_t.ap[:]),
                    reads=[pa_av.k(), rec_t.k()], writes=[ctx.k(h)])
            pa_rs.free()
            pa_av.free()
            pst[0].free()
            pst[1].free()
    for a in qh_t + kh_t + vh_t + vtm + pt + [rec_t]:
        a.free()
    negm.free()

    yev2 = [B_.arena.alloc([128, 2048], F32) for _ in range(2)]
    rin = [B_.arena.alloc([128, 2048], BF16) for _ in range(2)] \
        if inner_resid else None
    ctx_keys = [ctx.k(h) for h in range(DC)]

    def evac_o(fi, half, pa, reads):
        c = fi
        par = half
        hsl = slice(half * 2048, (half + 1) * 2048)
        ye = yev2[par]
        if inner_resid:
            B_.dma(rin[par].ap[:], xn_dram[c, :, hsl],
                   reads=[f"{xn_tag}/{c}"], writes=[rin[par].k()])
            B_.emit("vector", lambda e, pa=pa, par=par, ye=ye: e.tensor_add(
                ye.ap[:], pa.ap[:, 0:2048], rin[par].ap[:]),
                reads=reads + [rin[par].k()], writes=[ye.k()])
        elif bo is not None:
            bap, bk2 = G.vec(bo, c)
            B_.emit("scalar", lambda e, pa=pa, bap=bap, ye=ye: e.activation(
                ye.ap[:], pa.ap[:, 0:2048], AF.Identity, bias=bap),
                reads=reads + [bk2], writes=[ye.k()])
        else:
            B_.emit("vector", lambda e, pa=pa, ye=ye: e.tensor_copy(
                ye.ap[:], pa.ap[:, 0:2048]),
                reads=reads, writes=[ye.k()])
        B_.dma(y_dram[c, :, hsl], ye.ap[:], reads=[ye.k()],
               writes=[f"{y_tag}/{c}"])
    proj_resident(B_, wo, DC, DC, ctx.ap, ctx_keys, evac_o)
    for a in yev2 + (rin or []):
        a.free()
    ctx.free()



STOP_AFTER = [99]
MIXER_SUB = [99]
SCAN_SLABS = [99]


def emit_block0(B_, G, x0):
    xres0 = B_.dscr("xres0", [DC, 128, TOK], F32)
    xn1 = B_.dscr("xn1", [DC, 128, TOK], BF16)
    y1 = B_.dscr("y1", [DC, 128, TOK], F32)
    x1 = B_.dram["x1"]
    ph_entry(B_, G, x0, xres0)
    if STOP_AFTER[0] <= 1:
        for c in range(DC):
            st = B_.arena.alloc([128, TOK], F32)
            B_.dma(st.ap[:], xres0[c], reads=[f"xres0/{c}"], writes=[st.k()])
            B_.dma(x1[c], st.ap[:], reads=[st.k()], writes=[f"x1/{c}"])
            st.free()
        return
    ph_ln(B_, G, xres0, xn_dram=xn1, z_tag="xres0", out_tag="xn1")
    if STOP_AFTER[0] <= 2:
        cv = B_.arena.alloc([128, TOK], BF16)
        cf = B_.arena.alloc([128, TOK], F32)
        for c in range(DC):
            B_.dma(cv.ap[:], xn1[c], reads=[f"xn1/{c}"], writes=[cv.k()])
            B_.emit("vector", lambda e: e.tensor_copy(cf.ap[:], cv.ap[:]),
                    reads=[cv.k()], writes=[cf.k()])
            B_.dma(x1[c], cf.ap[:], reads=[cf.k()], writes=[f"x1/{c}"])
        cv.free()
        cf.free()
        return
    ph_l0_mixer(B_, G, xn1, y1, "xn1")
    if STOP_AFTER[0] <= 3:
        for c in range(DC):
            st = B_.arena.alloc([128, TOK], F32)
            B_.dma(st.ap[:], y1[c], reads=[f"l0_y/{c}"], writes=[st.k()])
            B_.dma(x1[c], st.ap[:], reads=[st.k()], writes=[f"x1/{c}"])
            st.free()
        return
    ph_ln(B_, G, y1, resid_dram=xres0, out_dram=x1, z_tag="l0_y",
          out_tag="x1", resid_tag="xres0")


def emit_block1(B_, G):
    x1 = B_.dram["x1"]
    xn2 = B_.dscr("xn2", [DC, 128, TOK], BF16)
    y2 = B_.dscr("y2", [DC, 128, TOK], F32)
    x2 = B_.dram["x2"]
    ph_ln(B_, G, x1, xn_dram=xn2, z_tag="x1", out_tag="xn2")
    ph_ffn24(B_, G, xn2, y2, "xn2")
    ph_ln(B_, G, y2, resid_dram=x1, out_dram=x2, z_tag="l0ffn_y",
          out_tag="x2", resid_tag="x1")


def emit_block2(B_, G):
    x2 = B_.dram["x2"]
    xn3 = B_.dscr("xn3", [DC, 128, TOK], BF16)
    y3 = B_.dscr("y3", [DC, 128, TOK], F32)
    x3 = B_.dram["x3"]
    ph_ln(B_, G, x2, xn_dram=xn3, z_tag="x2", out_tag="xn3")
    ph_mha(B_, G, xn3, y3, "mha3", ("wq_s", "wk_s", "wv_s", "wo_s"),
           bias_names=("bq_c", "bk_c", "bv_c", "bo_c"), causal=False,
           xn_tag="xn3", y_tag="mha3_y")
    ph_ln(B_, G, y3, resid_dram=x2, out_dram=x3, z_tag="mha3_y",
          out_tag="x3", resid_tag="x2")


def emit_block3(B_, G):
    x3 = B_.dram["x3"]
    xn4 = B_.dscr("xn4", [DC, 128, TOK], BF16)
    x4 = B_.dram["x4"]
    ph_ln(B_, G, x3, xn_dram=xn4, z_tag="x3", out_tag="xn4")

    causal_sb = B_.arena.alloc([128, 3072], BF16)
    B_.dma(causal_sb.ap[:], B_.dram["causal_m"], writes=[causal_sb.k()])

    ixn1 = B_.dscr("ixn1", [DC, 128, TOK], BF16)
    x2a = B_.dscr("x2a", [DC, 128, TOK], BF16)
    x2a_f = B_.dscr("x2a_f", [DC, 128, TOK], F32)
    ixn2 = B_.dscr("ixn2", [DC, 128, TOK], BF16)
    hg = B_.dscr("hg", [FFC, 128, TOK], BF16)
    out2 = B_.dscr("out2", [DC, 128, TOK], F32)

    ph_ln(B_, G, xn4, xn_dram=ixn1, z_is_bf16=True, scale_nm="ln1s_c",
          bias_nm="ln1b_c", z_tag="xn4", out_tag="ixn1")
    ph_mha(B_, G, ixn1, x2a_f, "imha", ("iwq_s", "iwk_s", "iwv_s", "iwo_s"),
           bias_names=None, causal=True, inner_resid=False,
           xn_tag="ixn1", y_tag="ictx_y", causal_sb=causal_sb)
    ra = B_.arena.alloc([128, TOK], F32)
    rb = B_.arena.alloc([128, TOK], BF16)
    for c in range(DC):
        B_.dma(ra.ap[:], x2a_f[c], reads=[f"ictx_y/{c}"], writes=[ra.k()])
        B_.dma(rb.ap[:], xn4[c], reads=[f"xn4/{c}"], writes=[rb.k()])
        B_.emit("vector", lambda e: e.tensor_add(ra.ap[:], ra.ap[:], rb.ap[:]),
                reads=[ra.k(), rb.k()], writes=[ra.k()])
        B_.dma(x2a_f[c], ra.ap[:], reads=[ra.k()], writes=[f"x2a/{c}"])
    ra.free()
    rb.free()
    ph_ln(B_, G, x2a_f, xn_dram=ixn2, scale_nm="ln2s_c", bias_nm="ln2b_c",
          z_tag="x2a", out_tag="ixn2")

    ixn2_sb = load_xn(B_, ixn2, "ixn2")
    ixn2_keys = [ixn2_sb.k(c) for c in range(DC)]
    gel = [B_.arena.alloc([128, 2048], BF16) for _ in range(2)]

    def evac_ff1(fi, half, pa, reads):
        par = half
        hsl = slice(half * 2048, (half + 1) * 2048)
        B_.emit("scalar", lambda e, pa=pa, par=par: e.activation(
            gel[par].ap[:], pa.ap[:, 0:2048], AF.Gelu),
            reads=reads, writes=[gel[par].k()])
        B_.dma(hg[fi, :, hsl], gel[par].ap[:], reads=[gel[par].k()],
               writes=[f"hg/{fi}"])
    proj_resident(B_, "wff1_s", FFC, DC, ixn2_sb.ap, ixn2_keys, evac_ff1)
    for a in gel:
        a.free()
    ixn2_sb.free()

    wfd = B_.dram["bf_wff2_s"]
    wfh = [B_.arena.alloc([128, FFC, 128], BF16) for _ in range(8)]
    hg_t = [B_.arena.alloc([128, 8, 512], BF16) for _ in range(2)]
    o2ev = [B_.arena.alloc([128, 512], F32) for _ in range(2)]
    x2a_t = [B_.arena.alloc([128, 512], F32) for _ in range(2)]
    for dh2 in range(2):
        for i in range(8):
            dc = dh2 * 8 + i
            B_.dma(wfh[i].ap[:], wfd[dc], reads=[f"bfw_wff2_s/{dc}"],
                   writes=[wfh[i].k()], queue="scalar")
        for ttn in range(8):
            pa = B_.psum(8)
            for kg in range(8):
                par = kg % 2
                ht = hg_t[par]
                B_.dma(ht.ap[:],
                       hg[kg * 8:(kg + 1) * 8, :, ttn * 512:(ttn + 1) * 512]
                       .rearrange("a p t -> p a t"),
                       reads=[f"hg/{kg * 8 + i2}" for i2 in range(8)],
                       writes=[ht.k()])

                def mmfn(e, kg=kg, ht=ht, pa=pa):
                    last = None
                    for i2 in range(8):
                        kc = kg * 8 + i2
                        for i in range(8):
                            last = e.matmul(pa.ap[:, i * 512:(i + 1) * 512],
                                            wfh[i].ap[:, kc], ht.ap[:, i2],
                                            start=(kc == 0),
                                            stop=(kc == FFC - 1))
                    return last
                B_.emit("tensor", mmfn,
                        reads=[ht.k()] + [wfh[i].k() for i in range(8)],
                        writes=[pa.k()] if kg in (0, 7) else [])
            for i in range(8):
                dc = dh2 * 8 + i
                par = i % 2
                B_.dma(x2a_t[par].ap[:],
                       x2a_f[dc, :, ttn * 512:(ttn + 1) * 512],
                       reads=[f"x2a/{dc}"], writes=[x2a_t[par].k()])
                B_.emit("vector", lambda e, i=i, pa=pa, par=par: e.tensor_add(
                    o2ev[par].ap[:], pa.ap[:, i * 512:(i + 1) * 512],
                    x2a_t[par].ap[:]),
                    reads=[pa.k(), x2a_t[par].k()], writes=[o2ev[par].k()])
                B_.dma(out2[dc, :, ttn * 512:(ttn + 1) * 512],
                       o2ev[par].ap[:], reads=[o2ev[par].k()],
                       writes=[f"out2/{dc}"])
            pa.free()
    for a in wfh + hg_t + o2ev + x2a_t:
        a.free()
    causal_sb.free()

    xn4_sb = load_xn(B_, xn4, "xn4")
    xn4_keys = [xn4_sb.k(c) for c in range(DC)]
    y4 = B_.dscr("y4", [DC, 128, TOK], F32)
    lg = B_.arena.alloc([128, TOK], BF16)

    def evac_rt(fi, half, pa, reads):
        hsl = slice(half * 2048, (half + 1) * 2048)
        def cpf(e, pa=pa, hsl=hsl):
            last = None
            for prow in (0, 32, 64):
                last = e.tensor_copy(lg.ap[prow:prow + 1, hsl],
                                     pa.ap[prow:prow + 1, 0:2048])
            return last
        B_.emit("vector", cpf, reads=reads, writes=[lg.k()])
    proj_resident(B_, "w_router_s", 1, DC, xn4_sb.ap, xn4_keys, evac_rt)
    lr1 = B_.arena.alloc([128, 1, TOK], BF16)

    def evac_lr1(fi, half, pa, reads):
        hsl = slice(half * 2048, (half + 1) * 2048)
        B_.emit("vector", lambda e, pa=pa, hsl=hsl: e.tensor_copy(
            lr1.ap[:, 0, hsl], pa.ap[:, 0:2048]),
            reads=reads, writes=[lr1.k()])
    proj_resident(B_, "w_lrdown_s", 1, DC, xn4_sb.ap, xn4_keys, evac_lr1)
    xn4_sb.free()

    lf = [B_.arena.alloc([128, TOK], F32) for _ in range(3)]
    for r in range(3):
        base = 32 * r
        for half in range(2):
            pb = B_.psum(4)

            def bfn(e, base=base, half=half, pb=pb):
                last = None
                for t4 in range(4):
                    c0 = half * 2048 + t4 * 512
                    last = e.matmul(pb.ap[:, t4 * 512:(t4 + 1) * 512],
                                    G.ones_bf[base:base + 1, :],
                                    lg.ap[base:base + 1, c0:c0 + 512],
                                    start=True, stop=True)
                return last
            B_.emit("tensor", bfn, reads=[lg.k(), G.b_ones.k()],
                    writes=[pb.k()])
            B_.emit("vector", lambda e, pb=pb, r=r, half=half: e.tensor_copy(
                lf[r].ap[:, half * 2048:(half + 1) * 2048], pb.ap[:, 0:2048]),
                reads=[pb.k()], writes=[lf[r].k()])
            pb.free()

    m1b = B_.arena.alloc([128, TOK], U8)
    m2b = B_.arena.alloc([128, TOK], U8)
    g1 = B_.arena.alloc([128, TOK], BF16)
    g2 = B_.arena.alloc([128, TOK], BF16)
    B_.emit("vector", lambda e: e.tensor_tensor(
        m1b.ap[:], lf[1].ap[:], lf[0].ap[:], ALU.is_gt),
        reads=[lf[1].k(), lf[0].k()], writes=[m1b.k()])
    B_.emit("vector", lambda e: e.tensor_tensor(
        g1.ap[:], lf[2].ap[:], lf[0].ap[:], ALU.is_gt),
        reads=[lf[2].k(), lf[0].k()], writes=[g1.k()])
    B_.emit("vector", lambda e: e.tensor_tensor(
        g2.ap[:], lf[2].ap[:], lf[1].ap[:], ALU.is_gt),
        reads=[lf[2].k(), lf[1].k()], writes=[g2.k()])
    B_.emit("vector", lambda e: e.tensor_mul(
        m2b.ap[:], g1.ap[:], g2.ap[:]),
        reads=[g1.k(), g2.k()], writes=[m2b.k()])
    for a in lf:
        a.free()
    g1.free()
    g2.free()

    osel = [B_.arena.alloc([128, 2048], F32) for _ in range(2)]
    o2t = [B_.arena.alloc([128, 2048], F32) for _ in range(2)]
    o1t = [B_.arena.alloc([128, 2048], F32) for _ in range(2)]
    xn4_t = [B_.arena.alloc([128, 2048], BF16) for _ in range(2)]

    def evac_sel(fi, half, pa, reads):
        c = fi
        par = half
        hsl = slice(half * 2048, (half + 1) * 2048)
        sap, sk = G.vec("sbias_c", c)
        B_.dma(xn4_t[par].ap[:], xn4[c, :, hsl], reads=[f"xn4/{c}"],
               writes=[xn4_t[par].k()])
        B_.emit("vector", lambda e, par=par, sap=sap:
                e.tensor_scalar(osel[par].ap[:], xn4_t[par].ap[:], sap,
                                None, ALU.add),
                reads=[xn4_t[par].k(), sk], writes=[osel[par].k()])
        B_.emit("vector", lambda e, par=par, pa=pa:
                e.tensor_add(o1t[par].ap[:], pa.ap[:, 0:2048],
                             xn4_t[par].ap[:]),
                reads=reads + [xn4_t[par].k()], writes=[o1t[par].k()])
        B_.dma(o2t[par].ap[:], out2[c, :, hsl], reads=[f"out2/{c}"],
               writes=[o2t[par].k()])
        B_.emit("vector", lambda e, par=par, hsl=hsl: e.copy_predicated(
            osel[par].ap[:], m1b.ap[:, hsl], o1t[par].ap[:]),
            reads=[m1b.k(), o1t[par].k(), osel[par].k()],
            writes=[osel[par].k()])
        B_.emit("vector", lambda e, par=par, hsl=hsl: e.copy_predicated(
            osel[par].ap[:], m2b.ap[:, hsl], o2t[par].ap[:]),
            reads=[m2b.k(), o2t[par].k(), osel[par].k()],
            writes=[osel[par].k()])
        B_.dma(y4[c, :, hsl], osel[par].ap[:], reads=[osel[par].k()],
               writes=[f"y4/{c}"])
    proj_resident(B_, "w_lrup_s", DC, 1, lr1.ap, [lr1.k()], evac_sel)
    for a in osel + o2t + o1t + xn4_t:
        a.free()
    lr1.free()
    m1b.free()
    m2b.free()
    lg.free()

    ph_ln(B_, G, y4, resid_dram=x3, out_dram=x4, z_tag="y4",
          out_tag="x4", resid_tag="x3")



ALL_BF = list(BF_WEIGHTS.keys())
BLOCK_BF = {
    0: ["w_xproj_s", "w_dt_s"],
    1: ["w_gate_s", "w_up_s", "w_down_s"],
    2: ["wq_s", "wk_s", "wv_s", "wo_s"],
    3: ["iwq_s", "iwk_s", "iwv_s", "iwo_s", "wff1_s", "wff2_s",
        "w_router_s", "w_lrdown_s", "w_lrup_s"],
}


def build_program(blocks, fused_all=False):
    B_ = Bld()
    B_.mk_arena(200 * 1024)
    first, last = blocks[0], blocks[-1]
    bf_names = []
    for b in blocks:
        bf_names += BLOCK_BF[b]
    declare_weights(B_, bf_names)
    if first == 0:
        x0 = B_.din("x0", [TOK, D], F32)
    res_names = {0: "x1", 1: "x2", 2: "x3", 3: "x4"}
    for b in blocks:
        nm = res_names[b]
        if b == last and last != 3:
            B_.dout(nm, [DC, 128, TOK], F32)
        else:
            B_.dscr(nm, [DC, 128, TOK], F32)
    if first != 0:
        prev = res_names[first - 1]
        B_.din(prev, [DC, 128, TOK], F32)
    if last == 3:
        xout = B_.dout("xout", [TOK, D], F32)
    G = Globals(B_)
    ph_wcast(B_, bf_names)
    for b in blocks:
        if b == 0:
            emit_block0(B_, G, B_.dram["x0"])
        elif b == 1:
            emit_block1(B_, G)
        elif b == 2:
            emit_block2(B_, G)
        elif b == 3:
            emit_block3(B_, G)
    if last == 3:
        ph_exit(B_, G, B_.dram["x4"], xout, "x4")
    nc = B_.finalize(B_.outputs_done())
    return nc


_PROG_CACHE = {}


def _get_prog(key, blocks):
    if key not in _PROG_CACHE:
        _PROG_CACHE[key] = build_program(blocks)
    return _PROG_CACHE[key]


def kernel(**inputs):
    w = prep_weights(inputs)
    x_full = inputs["L0_attn_x"]
    launches = [[0, 1, 2, 3]]
    core_in = []
    for core in range(NCORES):
        xc = np.ascontiguousarray(
            x_full[core * B:(core + 1) * B].reshape(TOK, D))
        core_in.append(xc)
    state = [dict() for _ in range(NCORES)]
    res_names = {0: "x1", 1: "x2", 2: "x3", 3: "x4"}
    out = None
    for li, blocks in enumerate(launches):
        nc = _get_prog(tuple(blocks), blocks)
        in_maps = []
        for core in range(NCORES):
            m = dict(w)
            if blocks[0] == 0:
                m["x0"] = core_in[core]
            else:
                prev = res_names[blocks[0] - 1]
                m[prev] = state[core][prev]
            in_maps.append(m)
        res = run_bass_kernel_spmd(nc, in_maps, core_ids=list(range(NCORES)))
        for core in range(NCORES):
            r = res.results[core]
            for k, v in r.items():
                state[core][k] = np.asarray(v)
    out = np.empty((NCORES * B, S, D), dtype=np.float32)
    for core in range(NCORES):
        out[core * B:(core + 1) * B] = state[core]["xout"].reshape(B, S, D)
    return out


# version 04 (final)
# speedup vs baseline: 1.1779x; 1.0429x over previous
import contextlib
import numpy as np
import ml_dtypes
import concourse.bass as bass
import concourse.mybir as mybir
from concourse.bass_utils import run_bass_kernel_spmd

F32 = mybir.dt.float32
BF16 = mybir.dt.bfloat16
U8 = mybir.dt.uint8
U16 = mybir.dt.uint16
AF = mybir.ActivationFunctionType
ALU = mybir.AluOpType
AX = mybir.AxisListType

NCORES = 8
B = 2
S = 2048
TOK = B * S
D = 2048
DC = D // 128
N_SSM = 8
FF = 8192
FFC = FF // 128
NHEAD = 16
DH = 128
LN_EPS = 1e-5
RSQ_DH = float(1.0 / np.sqrt(128.0))

ENGINES = ("tensor", "vector", "scalar", "gpsimd", "sync")



class Buf:
    __slots__ = ("uid", "nsub", "war", "is_psum")
    _next = [0]

    def __init__(self, war=()):
        self.uid = str(Buf._next[0])
        Buf._next[0] += 1
        self.nsub = set()
        self.war = set(war)
        self.is_psum = False

    def k(self, i=None):
        key = self.uid if i is None else f"{self.uid}/{i}"
        self.nsub.add(key)
        return key


class _BufState:
    __slots__ = ("writer", "readers")

    def __init__(self):
        self.writer = None
        self.readers = []


class Arena:
    def __init__(self, bld, nbytes):
        self.bld = bld
        self.t = bld.stack.enter_context(
            bld.nc.sbuf_tensor("arena", [128, nbytes], U8))
        self.free_list = [(0, nbytes, set())]

    def alloc(self, shape, dtype, align=64, high=False):
        nbytes = int(np.prod(shape[1:])) * mybir.dt.size(dtype)
        nbytes = (nbytes + align - 1) & ~(align - 1)
        order = range(len(self.free_list))
        if high:
            order = reversed(list(order))
        for i in order:
            off, size, ghosts = self.free_list[i]
            if True:
                if high:
                    aoff = (off + size - nbytes) & ~(align - 1)
                    if aoff < off:
                        continue
                    pad_lo = aoff - off
                    pad_hi = (off + size) - (aoff + nbytes)
                else:
                    aoff = (off + align - 1) & ~(align - 1)
                    pad_lo = aoff - off
                    pad_hi = size - pad_lo - nbytes
                    if pad_hi < 0:
                        continue
                if size < nbytes + pad_lo + max(pad_hi, 0):
                    continue
                del self.free_list[i]
                if pad_lo:
                    self.free_list.insert(i, (off, pad_lo, set(ghosts)))
                if pad_hi:
                    self.free_list.insert(i + (1 if pad_lo else 0),
                                          (aoff + nbytes, pad_hi,
                                           set(ghosts)))
                ap = self.t[:, aoff:aoff + nbytes].bitcast(dtype)
                n = int(np.prod(shape[1:]))
                ap = ap[0:shape[0], :n]
                if len(shape) == 3:
                    ap = ap.rearrange("p (a b) -> p a b", a=shape[1])
                elif len(shape) == 4:
                    ap = ap.rearrange("p (a b c) -> p a b c", a=shape[1],
                                      b=shape[2])
                elif len(shape) != 2:
                    raise ValueError(shape)
                buf = Buf(war=ghosts)
                self.bld.bufreg[buf.uid] = buf
                return ArenaAlloc(self, ap, buf, aoff, nbytes)
        raise MemoryError(f"arena alloc {shape} {dtype} ({nbytes}B): "
                          f"free={[(o, s) for o, s, _ in self.free_list]}")


class ArenaAlloc:
    def __init__(self, arena, ap, buf, off, nbytes):
        self.arena = arena
        self.ap = ap
        self.buf = buf
        self.off = off
        self.nbytes = nbytes

    def k(self, i=None):
        return self.buf.k(i)

    def free(self):
        self.arena.free_list.append((self.off, self.nbytes,
                                     set(self.buf.nsub)))
        fl = sorted(self.arena.free_list)
        merged = []
        for off, size, gh in fl:
            if merged and merged[-1][0] + merged[-1][1] == off:
                merged[-1] = (merged[-1][0], merged[-1][1] + size,
                              merged[-1][2] | gh)
            else:
                merged.append((off, size, set(gh)))
        self.arena.free_list = merged


class PsumAlloc:
    def __init__(self, bld, ap, buf, banks):
        self.bld = bld
        self.ap = ap
        self.buf = buf
        self.banks = banks

    def k(self, i=None):
        return self.buf.k(i)

    def free(self):
        self.bld._psum_seq[0] += 1
        for bk in self.banks:
            self.bld._psum_free.append(bk)
            self.bld._psum_ghost[bk] = set(self.buf.nsub)
            self.bld._psum_age[bk] = self.bld._psum_seq[0]


class Bld:
    def __init__(self, fused=True):
        self.nc = bass.Bass("TRN2", target_bir_lowering=False)
        self.fused = fused
        self.stack = contextlib.ExitStack()
        self.ops = []
        self.cnt = {e: 0 for e in ENGINES}
        self.NDMA = 8
        for q in ("sync", "scalar"):
            for i in range(self.NDMA):
                self.cnt[f"dma_{q}{i}"] = 0
        self.dma_idx = {"sync": 0, "scalar": 0, "gpsimd": 0, "vector": 0}
        self.state = {}
        self.last_waited = {e: {} for e in ENGINES}
        self.dram = {}
        self.bufreg = {}
        self._psum_free = list(range(8))
        self._psum_ghost = {}
        self._psum_age = {i: 0 for i in range(8)}
        self._psum_seq = [0]
        self._uniq = [0]
        self._psum_all = self.stack.enter_context(
            self.nc.psum_tensor("psum_all", [128, 8 * 512], F32))
        self.arena = None

    def mk_arena(self, nbytes):
        self.arena = Arena(self, nbytes)

    def din(self, name, shape, dtype):
        ap = self.nc.dram_tensor(name, list(shape), dtype,
                                 kind="ExternalInput").ap()
        self.dram[name] = ap
        return ap

    def dout(self, name, shape, dtype):
        ap = self.nc.dram_tensor(name, list(shape), dtype,
                                 kind="ExternalOutput").ap()
        self.dram[name] = ap
        return ap

    def dscr(self, name, shape, dtype):
        ap = self.nc.dram_tensor(name, list(shape), dtype, kind="Internal").ap()
        self.dram[name] = ap
        return ap

    def _name(self, pfx):
        self._uniq[0] += 1
        return f"{pfx}_{self._uniq[0]}"

    def sb(self, shape, dtype, pfx="sb"):
        h = self.stack.enter_context(
            self.nc.sbuf_tensor(self._name(pfx), list(shape), dtype))
        b = Buf()
        self.bufreg[b.uid] = b
        return h, b

    def psum(self, nbank=1):
        fr = sorted(self._psum_free)
        runs = []
        for i in range(len(fr) - nbank + 1):
            if fr[i + nbank - 1] - fr[i] == nbank - 1:
                runs.append(fr[i:i + nbank])
        assert runs, f"psum alloc {nbank} failed; free={fr}"
        run = min(runs, key=lambda r: max(self._psum_age[b] for b in r))
        for bk in run:
            self._psum_free.remove(bk)
        ap = self._psum_all[:, run[0] * 512:(run[0] + nbank) * 512]
        war = set()
        for bk in run:
            war |= self._psum_ghost.pop(bk, set())
        buf = Buf(war=war)
        buf.is_psum = True
        self.bufreg[buf.uid] = buf
        return PsumAlloc(self, ap, buf, run)

    def _need(self, waits, ev):
        sem, val = ev
        if waits.get(sem, 0) < val:
            waits[sem] = val

    def emit(self, engine, fn, reads=(), writes=(), extra_waits=(), dma=False):
        waits = {}
        my_sem = engine
        for k in reads:
            st = self.state.get(k)
            if st and st.writer:
                self._need(waits, st.writer)
            if st is not None and not dma:
                buf = self.bufreg.get(k.split("/")[0])
                if buf is not None and buf.is_psum:
                    for r in st.readers:
                        if r[0] != my_sem:
                            self._need(waits, r)
        for k in writes:
            st = self.state.get(k)
            if st:
                for r in st.readers:
                    self._need(waits, r)
                if st.writer:
                    self._need(waits, st.writer)
            buf = self.bufreg.get(k.split("/")[0])
            if buf is not None and buf.war:
                for wk in buf.war:
                    wst = self.state.get(wk)
                    if wst:
                        for r in wst.readers:
                            self._need(waits, r)
                        if wst.writer:
                            self._need(waits, wst.writer)
                buf.war = set()
        for ev in extra_waits:
            if ev is not None:
                self._need(waits, ev)
        if dma:
            j = self.dma_idx[engine]
            self.dma_idx[engine] = j + 1
            ev_sem = f"dma_{engine}{j % self.NDMA}"
            if ev_sem not in self.cnt:
                self.cnt[ev_sem] = 0
            occ = j // self.NDMA
            if occ > 0:
                self._need(waits, (ev_sem, 16 * occ))
        else:
            ev_sem = engine
        lw = self.last_waited[engine]
        pruned = {}
        for sem, val in waits.items():
            if lw.get(sem, 0) >= val:
                continue
            pruned[sem] = val
            lw[sem] = val
        self.cnt[ev_sem] += 16 if dma else 1
        ev = (ev_sem, self.cnt[ev_sem])
        self.ops.append((engine, pruned, fn, (ev_sem, 16 if dma else 1)))
        for k in writes:
            st = self.state.setdefault(k, _BufState())
            st.writer = ev
            st.readers = []
        for k in reads:
            st = self.state.setdefault(k, _BufState())
            st.readers.append(ev)
        return ev

    def dma(self, out, in_, reads=(), writes=(), extra_waits=(), queue="sync"):
        return self.emit(queue, lambda e: e.dma_start(out=out, in_=in_),
                         reads=reads, writes=writes, extra_waits=extra_waits,
                         dma=True)

    def finalize(self, final_events=()):
        nc = self.nc
        sems = {}
        with contextlib.ExitStack() as es:
            for name in list(self.cnt.keys()):
                sems[name] = es.enter_context(nc.semaphore(f"sem_{name}"))
            blk = es.enter_context(nc.Block())
            per_engine = {e: [] for e in ENGINES}
            for (engine, waits, fn, inc) in self.ops:
                per_engine[engine].append((waits, fn, inc))

            def make(engine):
                def body(e):
                    for waits, fn, inc in per_engine[engine]:
                        for sem, val in waits.items():
                            e.wait_ge(sems[sem], val)
                        inst = fn(e)
                        sem, amt = inc
                        assert inst is not None, f"op on {engine} -> None"
                        inst.then_inc(sems[sem], amt)
                    if engine == "sync":
                        for sem, val in final_events:
                            e.wait_ge(sems[sem], val)
                return body

            for engine in ENGINES:
                if per_engine[engine] or engine == "sync":
                    getattr(blk, engine)(make(engine))
        self.stack.close()
        return nc

    def outputs_done(self):
        return [(s, v) for s, v in self.cnt.items()
                if s.startswith("dma_") and v > 0]



def _slabify(W, fs=128):
    K, F = W.shape
    KC = K // 128
    assert K % 128 == 0 and F % fs == 0
    return np.ascontiguousarray(
        W.reshape(KC, 128, F // fs, fs).transpose(2, 1, 0, 3))


def _perm24():
    i = np.arange(FF)
    g, j = i // 4, i % 4
    ip = j * (FF // 4) + g
    inv = np.empty(FF, dtype=np.int64)
    inv[ip] = i
    return inv


def _chunkvec(v):
    return np.ascontiguousarray(v.reshape(DC, 128).T.astype(np.float32))


def prep_weights(inp):
    P24 = _perm24()
    asc = np.ascontiguousarray
    w = {}
    xp = np.zeros((D, 384), dtype=np.float32)
    xp[:, :272] = inp["L0_attn_W_xproj"].T
    w["w_xproj_s"] = _slabify(xp)
    w["w_dt_s"] = _slabify(asc(inp["L0_attn_W_dt"].T.astype(np.float32)))
    w["b_dt_c"] = _chunkvec(inp["L0_attn_b_dt"])
    w["a_log_c"] = asc(inp["L0_attn_A_log"].reshape(DC, 128, N_SSM)
                       .transpose(1, 0, 2).reshape(128, DC * N_SSM)
                       .astype(np.float32))
    w["d_param_c"] = _chunkvec(inp["L0_attn_D_param"])
    w["w_gate_s"] = _slabify(asc(inp["L0_ffn_w_gate"][:, P24]))
    w["w_up_s"] = _slabify(asc(inp["L0_ffn_w_up"][:, P24]))
    w["w_down_s"] = _slabify(asc(inp["L0_ffn_w_down"][P24, :]))
    for nm, key in (("wq", "L1_attn_Wq"), ("wk", "L1_attn_Wk"),
                    ("wv", "L1_attn_Wv"), ("wo", "L1_attn_Wo")):
        w[nm + "_s"] = _slabify(asc(inp[key].T))
    w["bq_c"] = _chunkvec(inp["L1_attn_bq"])
    w["bk_c"] = _chunkvec(inp["L1_attn_bk"])
    w["bv_c"] = _chunkvec(inp["L1_attn_bv"])
    w["bo_c"] = _chunkvec(inp["L1_attn_bo"])
    w["sbias_c"] = _chunkvec(inp["L1_ffn_simple_bias"])
    for nm, key in (("iwq", "L1_ffn_wq"), ("iwk", "L1_ffn_wk"),
                    ("iwv", "L1_ffn_wv"), ("iwo", "L1_ffn_wo")):
        w[nm + "_s"] = _slabify(asc(inp[key].T))
    w["wff1_s"] = _slabify(asc(inp["L1_ffn_w_ff1"].T))
    w["wff2_s"] = _slabify(asc(inp["L1_ffn_w_ff2"].T))
    w["ln1s_c"] = _chunkvec(inp["L1_ffn_ln1_scale"])
    w["ln1b_c"] = _chunkvec(inp["L1_ffn_ln1_bias"])
    w["ln2s_c"] = _chunkvec(inp["L1_ffn_ln2_scale"])
    w["ln2b_c"] = _chunkvec(inp["L1_ffn_ln2_bias"])
    rt = np.zeros((D, 128), dtype=np.float32)
    rt[:, 0] = inp["L1_ffn_w_router"][0]
    rt[:, 32] = inp["L1_ffn_w_router"][1]
    rt[:, 64] = inp["L1_ffn_w_router"][2]
    w["w_router_s"] = _slabify(rt)
    lrd = np.zeros((D, 128), dtype=np.float32)
    lrd[:, :32] = inp["L1_ffn_w_down"].T
    w["w_lrdown_s"] = _slabify(lrd)
    lru = np.zeros((128, D), dtype=np.float32)
    lru[:32, :] = inp["L1_ffn_w_up"].T
    w["w_lrup_s"] = _slabify(lru)
    w["ident"] = np.eye(128, dtype=np.float32)
    sel = np.zeros((16, 16 * 128), dtype=np.float32)
    for n in range(16):
        sel[n, n * 128:(n + 1) * 128] = 1.0
    w["sel16"] = asc(sel)
    cm = (np.arange(3072)[None, :] >= (np.arange(128)[:, None] + 1920))
    w["causal_m"] = asc(cm.astype(ml_dtypes.bfloat16))
    return w


WEIGHT_SPECS = {
    "w_xproj_s": ([3, 128, 16, 128], F32),
    "w_dt_s": ([16, 128, 2, 128], F32),
    "b_dt_c": ([128, 16], F32),
    "a_log_c": ([128, 128], F32),
    "d_param_c": ([128, 16], F32),
    "w_gate_s": ([64, 128, 16, 128], F32),
    "w_up_s": ([64, 128, 16, 128], F32),
    "w_down_s": ([16, 128, 64, 128], F32),
    "wq_s": ([16, 128, 16, 128], F32),
    "wk_s": ([16, 128, 16, 128], F32),
    "wv_s": ([16, 128, 16, 128], F32),
    "wo_s": ([16, 128, 16, 128], F32),
    "bq_c": ([128, 16], F32),
    "bk_c": ([128, 16], F32),
    "bv_c": ([128, 16], F32),
    "bo_c": ([128, 16], F32),
    "sbias_c": ([128, 16], F32),
    "iwq_s": ([16, 128, 16, 128], F32),
    "iwk_s": ([16, 128, 16, 128], F32),
    "iwv_s": ([16, 128, 16, 128], F32),
    "iwo_s": ([16, 128, 16, 128], F32),
    "wff1_s": ([64, 128, 16, 128], F32),
    "wff2_s": ([16, 128, 64, 128], F32),
    "ln1s_c": ([128, 16], F32),
    "ln1b_c": ([128, 16], F32),
    "ln2s_c": ([128, 16], F32),
    "ln2b_c": ([128, 16], F32),
    "w_router_s": ([1, 128, 16, 128], F32),
    "w_lrdown_s": ([1, 128, 16, 128], F32),
    "w_lrup_s": ([16, 128, 1, 128], F32),
    "ident": ([128, 128], F32),
    "sel16": ([16, 2048], F32),
    "causal_m": ([128, 3072], BF16),
}

BF_WEIGHTS = {
    "w_xproj_s": (3, 16), "w_dt_s": (16, 2),
    "w_gate_s": (64, 16), "w_up_s": (64, 16), "w_down_s": (16, 64),
    "wq_s": (16, 16), "wk_s": (16, 16), "wv_s": (16, 16), "wo_s": (16, 16),
    "iwq_s": (16, 16), "iwk_s": (16, 16), "iwv_s": (16, 16), "iwo_s": (16, 16),
    "wff1_s": (64, 16), "wff2_s": (16, 64),
    "w_router_s": (1, 16), "w_lrdown_s": (1, 16), "w_lrup_s": (16, 1),
}


def ph_wcast(B_, names):
    NB = 4
    stg = [B_.arena.alloc([128, 4096], F32, high=True) for _ in range(NB)]
    ob = [B_.arena.alloc([128, 4096], BF16, high=True) for _ in range(NB)]
    i = 0
    for nm in names:
        nslab, KCW = BF_WEIGHTS[nm]
        src = B_.dram[nm]
        dst = B_.dram["bf_" + nm]
        cols = KCW * 128
        for fi in range(nslab):
            for s0 in range(0, cols, 4096):
                w = min(4096, cols - s0)
                sv = src[fi].rearrange("p a b -> p (a b)")[:, s0:s0 + w]
                dv = dst[fi].rearrange("p a b -> p (a b)")[:, s0:s0 + w]
                par = i % NB
                eng = "vector" if i % 4 < 3 else "gpsimd"
                i += 1
                B_.dma(stg[par].ap[:, :w], sv, writes=[stg[par].k()],
                       queue="scalar")
                B_.emit(eng, lambda e, w=w, par=par: e.tensor_copy(
                    ob[par].ap[:, :w], stg[par].ap[:, :w]),
                    reads=[stg[par].k()], writes=[ob[par].k()])
                B_.dma(dv, ob[par].ap[:, :w], reads=[ob[par].k()],
                       writes=[f"bfw_{nm}/{fi}"], queue="scalar")
    for a in stg + ob:
        a.free()


def declare_weights(B_, bf_names=()):
    for nm, (shape, dtype) in WEIGHT_SPECS.items():
        B_.din(nm, shape, dtype)
    for nm in bf_names:
        shape = WEIGHT_SPECS[nm][0]
        B_.dscr("bf_" + nm, shape, BF16)



class Globals:
    def __init__(self, B_):
        self.ones_bf, self.b_ones = B_.sb([128, 128], BF16, "ones")
        B_.emit("vector", lambda e: e.memset(self.ones_bf[:], 1.0),
                writes=[self.b_ones.k()])
        self.ident, self.b_ident = B_.sb([128, 128], F32, "ident")
        B_.dma(self.ident[:], B_.dram["ident"], writes=[self.b_ident.k()])
        self.identb, self.b_identb = B_.sb([128, 128], BF16, "identb")
        B_.emit("vector", lambda e: e.tensor_copy(self.identb[:], self.ident[:]),
                reads=[self.b_ident.k()], writes=[self.b_identb.k()])
        self.vecs = {}
        for nm in ("b_dt_c", "d_param_c", "bq_c", "bk_c", "bv_c", "bo_c",
                   "sbias_c", "ln1s_c", "ln1b_c", "ln2s_c", "ln2b_c"):
            h, b = B_.sb([128, 16], F32, nm)
            B_.dma(h[:], B_.dram[nm], writes=[b.k()])
            self.vecs[nm] = (h, b)
        self.A, self.b_A = B_.sb([128, 128], F32, "A")
        B_.dma(self.A[:], B_.dram["a_log_c"], writes=[self.b_A.k()])
        B_.emit("scalar", lambda e: e.activation(self.A[:], self.A[:], AF.Exp),
                reads=[self.b_A.k()], writes=[self.b_A.k()])
        B_.emit("vector", lambda e: e.tensor_scalar(
            self.A[:], self.A[:], -1.0, None, ALU.mult),
            reads=[self.b_A.k()], writes=[self.b_A.k()])
        self.sel16, self.b_sel16 = B_.sb([16, 16 * 128], BF16, "sel16")
        self.cpool, self.b_cpool = B_.sb([128, 8], F32, "cpool")
        self._cmap = {}
        self._B = B_

    def cbias(self, value):
        if value not in self._cmap:
            col = len(self._cmap)
            assert col < 8
            self._B.emit("vector",
                         lambda e, col=col, value=value: e.memset(
                             self.cpool[:, col:col + 1], value),
                         writes=[self.b_cpool.k(col)])
            self._cmap[value] = col
        col = self._cmap[value]
        return self.cpool[:, col:col + 1], self.b_cpool.k(col)

    def load_sel16(self, B_):
        tmp = B_.arena.alloc([16, 2048], F32)
        B_.dma(tmp.ap[:], B_.dram["sel16"], writes=[tmp.k()])
        B_.emit("vector", lambda e: e.tensor_copy(self.sel16[:], tmp.ap[:]),
                reads=[tmp.k()], writes=[self.b_sel16.k()])
        tmp.free()

    def vec(self, nm, c):
        h, b = self.vecs[nm]
        return h[:, c:c + 1], b.k()


def load_xn(B_, xn_dram, tag):
    al = B_.arena.alloc([128, DC, TOK], BF16)
    for c in range(DC):
        B_.dma(al.ap[:, c], xn_dram[c], reads=[f"{tag}/{c}"],
               writes=[al.k(c)])
    return al


def proj_resident(B_, wname, nslab, KCW, rhs_ap, rhs_keys, evac, order=None,
                  slab_extra_reads=None):
    wdram = B_.dram["bf_" + wname]
    if order is None:
        order = list(range(nslab))
    wb = [B_.arena.alloc([128, KCW, 128], BF16) for _ in range(2)]
    for idx, fi in enumerate(order):
        w = wb[idx % 2]
        B_.dma(w.ap[:], wdram[fi],
               reads=[f"bfw_{wname}/{fi}"], writes=[w.k()], queue="scalar")
        for half in range(2):
            pa = B_.psum(4)

            def mmfn(e, w=w, pa=pa, half=half):
                last = None
                for kc in range(KCW):
                    for t4 in range(4):
                        last = e.matmul(
                            pa.ap[:, t4 * 512:(t4 + 1) * 512], w.ap[:, kc],
                            rhs_ap[:, kc, half * 2048 + t4 * 512:
                                   half * 2048 + (t4 + 1) * 512],
                            start=(kc == 0), stop=(kc == KCW - 1))
                return last
            B_.emit("tensor", mmfn,
                    reads=[w.k()] + list(rhs_keys) +
                          (slab_extra_reads(fi) if slab_extra_reads else []),
                    writes=[pa.k()])
            evac(fi, half, pa, [pa.k()])
            pa.free()
    for w in wb:
        w.free()



def ph_ln(B_, G, z_dram, xn_dram=None, z_is_bf16=False,
          scale_nm=None, bias_nm=None, resid_dram=None, out_dram=None,
          out_tag=None, z_tag=None, resid_tag=None):
    htok = TOK // 2
    stats = B_.arena.alloc([128, 2, TOK], F32)
    zstg = [B_.arena.alloc([128, 2048], BF16 if z_is_bf16 else F32)
            for _ in range(2)]
    zb = None if z_is_bf16 else [B_.arena.alloc([128, 2048], BF16)
                                 for _ in range(2)]
    z2 = [B_.arena.alloc([128, 2048], BF16) for _ in range(2)]
    for half in range(2):
        pa_s = B_.psum(4)
        pa_q = B_.psum(4)
        for c in range(DC):
            par = c % 2
            stg = zstg[par]
            B_.dma(stg.ap[:], z_dram[c, :, half * htok:(half + 1) * htok],
                   reads=[f"{z_tag}/{c}"] if z_tag else [],
                   writes=[stg.k()])
            if z_is_bf16:
                zb_ap, zb_k = stg.ap, stg.k()
            else:
                B_.emit("vector", lambda e, stg=stg, par=par: e.tensor_copy(
                    zb[par].ap[:], stg.ap[:]),
                    reads=[stg.k()], writes=[zb[par].k()])
                zb_ap, zb_k = zb[par].ap, zb[par].k()
            B_.emit("scalar", lambda e, stg=stg, par=par: e.activation(
                z2[par].ap[:], stg.ap[:], AF.Square),
                reads=[stg.k()], writes=[z2[par].k()])

            def mk_mm(src_ap, pa, c):
                def mms(e):
                    last = None
                    for t4 in range(4):
                        last = e.matmul(
                            pa.ap[:, t4 * 512:(t4 + 1) * 512], G.ones_bf[:],
                            src_ap[:, t4 * 512:(t4 + 1) * 512],
                            start=(c == 0), stop=(c == DC - 1))
                    return last
                return mms
            B_.emit("tensor", mk_mm(zb_ap, pa_s, c),
                    reads=[zb_k, G.b_ones.k()],
                    writes=[pa_s.k()] if c in (0, DC - 1) else [])
            B_.emit("tensor", mk_mm(z2[par].ap, pa_q, c),
                    reads=[z2[par].k(), G.b_ones.k()],
                    writes=[pa_q.k()] if c in (0, DC - 1) else [])
        hsl = slice(half * htok, (half + 1) * htok)
        m_b = B_.arena.alloc([128, 2048], F32)
        msq = B_.arena.alloc([128, 2048], F32)
        vpe = B_.arena.alloc([128, 2048], F32)
        B_.emit("vector", lambda e, m_b=m_b, pa=pa_s: e.tensor_scalar(
            m_b.ap[:], pa.ap[:, 0:2048], 1.0 / D, None, ALU.mult),
            reads=[pa_s.k()], writes=[m_b.k()])
        B_.emit("vector", lambda e, m_b=m_b, msq=msq: e.tensor_mul(
            msq.ap[:], m_b.ap[:], m_b.ap[:]),
            reads=[m_b.k()], writes=[msq.k()])
        B_.emit("vector", lambda e, vpe=vpe, msq=msq, pa=pa_q:
                e.scalar_tensor_tensor(vpe.ap[:], pa.ap[:, 0:2048], 1.0 / D,
                                       msq.ap[:], ALU.mult, ALU.subtract),
                reads=[pa_q.k(), msq.k()], writes=[vpe.k()])
        eps_ap, eps_k = G.cbias(LN_EPS)
        B_.emit("scalar", lambda e, vpe=vpe, eps_ap=eps_ap: e.activation(
            vpe.ap[:], vpe.ap[:], AF.Ln, bias=eps_ap),
            reads=[vpe.k(), eps_k], writes=[vpe.k()])
        B_.emit("scalar", lambda e, vpe=vpe, hsl=hsl: e.activation(
            stats.ap[:, 0, hsl], vpe.ap[:], AF.Exp, scale=-0.5),
            reads=[vpe.k()], writes=[stats.k(f"r{half}")])
        B_.emit("vector", lambda e, m_b=m_b, hsl=hsl: e.tensor_mul(
            stats.ap[:, 1, hsl], m_b.ap[:], stats.ap[:, 0, hsl]),
            reads=[m_b.k(), stats.k(f"r{half}")],
            writes=[stats.k(f"m{half}")])
        pa_s.free()
        pa_q.free()
        m_b.free()
        msq.free()
        vpe.free()
    for a in zstg + (zb or []) + z2:
        a.free()

    stat_reads = [stats.k("r0"), stats.k("r1"), stats.k("m0"), stats.k("m1")]
    zfull = [B_.arena.alloc([128, TOK], BF16 if z_is_bf16 else F32)
             for _ in range(2)]
    t1 = [B_.arena.alloc([128, TOK], F32) for _ in range(2)]
    xn_t = [B_.arena.alloc([128, TOK], BF16) for _ in range(2)] \
        if xn_dram is not None else None
    xm = [B_.arena.alloc([128, TOK], F32) for _ in range(2)] \
        if scale_nm is not None else None
    xs = [B_.arena.alloc([128, TOK], F32) for _ in range(2)] \
        if resid_dram is not None else None
    xo = [B_.arena.alloc([128, TOK], F32) for _ in range(2)] \
        if out_dram is not None else None
    for c in range(DC):
        par = c % 2
        stg = zfull[par]
        B_.dma(stg.ap[:], z_dram[c], reads=[f"{z_tag}/{c}"] if z_tag else [],
               writes=[stg.k()])
        eng = "vector"
        eng2 = "gpsimd"
        B_.emit(eng, lambda e, stg=stg, par=par: e.tensor_mul(
            t1[par].ap[:], stg.ap[:], stats.ap[:, 0]),
            reads=[stg.k()] + stat_reads, writes=[t1[par].k()])
        if xn_dram is not None:
            if scale_nm is None:
                B_.emit(eng2, lambda e, par=par: e.tensor_sub(
                    xn_t[par].ap[:], t1[par].ap[:], stats.ap[:, 1]),
                    reads=[t1[par].k()] + stat_reads, writes=[xn_t[par].k()])
            else:
                B_.emit(eng2, lambda e, par=par: e.tensor_sub(
                    xm[par].ap[:], t1[par].ap[:], stats.ap[:, 1]),
                    reads=[t1[par].k()] + stat_reads, writes=[xm[par].k()])
                sap, sk = G.vec(scale_nm, c)
                bap, bk = G.vec(bias_nm, c)
                B_.emit("vector", lambda e, par=par, sap=sap, bap=bap:
                        e.tensor_scalar(xn_t[par].ap[:], xm[par].ap[:], sap,
                                        bap, ALU.mult, ALU.add),
                        reads=[xm[par].k(), sk, bk], writes=[xn_t[par].k()])
            B_.dma(xn_dram[c], xn_t[par].ap[:], reads=[xn_t[par].k()],
                   writes=[f"{out_tag}/{c}"])
        else:
            B_.dma(xs[par].ap[:], resid_dram[c],
                   reads=[f"{resid_tag}/{c}"] if resid_tag else [],
                   writes=[xs[par].k()])
            B_.emit(eng, lambda e, par=par: e.tensor_sub(
                xo[par].ap[:], t1[par].ap[:], stats.ap[:, 1]),
                reads=[t1[par].k()] + stat_reads, writes=[xo[par].k()])
            B_.emit(eng, lambda e, par=par: e.tensor_add(
                xo[par].ap[:], xo[par].ap[:], xs[par].ap[:]),
                reads=[xo[par].k(), xs[par].k()], writes=[xo[par].k()])
            B_.dma(out_dram[c], xo[par].ap[:], reads=[xo[par].k()],
                   writes=[f"{out_tag}/{c}"])
    for a in zfull + t1 + (xn_t or []) + (xm or []) + (xs or []) + (xo or []):
        a.free()
    stats.free()



def ph_entry(B_, G, x0, xres):
    GRP = 4
    NG = (TOK // 128) // GRP
    tin = B_.arena.alloc([128, GRP, D], F32)
    tout = B_.arena.alloc([128, DC, GRP * 128], F32)
    for g in range(NG):
        for r in range(GRP):
            rc = g * GRP + r
            B_.dma(tin.ap[:, r], x0[rc * 128:(rc + 1) * 128, :],
                   writes=[tin.k(r)])
        for c in range(DC):
            pa = B_.psum(1)

            def tfn(e, c=c, pa=pa):
                last = None
                for r in range(GRP):
                    last = e.transpose(pa.ap[:, r * 128:(r + 1) * 128],
                                       tin.ap[:, r, c * 128:(c + 1) * 128],
                                       G.ident[:])
                return last
            B_.emit("tensor", tfn,
                    reads=[tin.k(r) for r in range(GRP)] + [G.b_ident.k()],
                    writes=[pa.k()])
            B_.emit("vector", lambda e, c=c, pa=pa: e.tensor_copy(
                tout.ap[:, c], pa.ap[:, 0:GRP * 128]),
                reads=[pa.k()], writes=[tout.k(c)])
            pa.free()
        for c in range(DC):
            B_.dma(xres[c, :, g * GRP * 128:(g + 1) * GRP * 128],
                   tout.ap[:, c], reads=[tout.k(c)], writes=[f"xres0/{c}"])
    tin.free()
    tout.free()


def ph_exit(B_, G, xres, xout, z_tag):
    GRP = 4
    NG = DC // GRP
    tin = B_.arena.alloc([128, GRP, TOK], F32)
    tout = B_.arena.alloc([128, TOK // 128, GRP * 128], F32)
    for g in range(NG):
        for r in range(GRP):
            c = g * GRP + r
            B_.dma(tin.ap[:, r], xres[c], reads=[f"{z_tag}/{c}"],
                   writes=[tin.k(r)])
        for rc in range(TOK // 128):
            pa = B_.psum(1)

            def tfn(e, rc=rc, pa=pa):
                last = None
                for r in range(GRP):
                    last = e.transpose(pa.ap[:, r * 128:(r + 1) * 128],
                                       tin.ap[:, r, rc * 128:(rc + 1) * 128],
                                       G.ident[:])
                return last
            B_.emit("tensor", tfn,
                    reads=[tin.k(r) for r in range(GRP)] + [G.b_ident.k()],
                    writes=[pa.k()])
            B_.emit("vector", lambda e, rc=rc, pa=pa: e.tensor_copy(
                tout.ap[:, rc], pa.ap[:, 0:GRP * 128]),
                reads=[pa.k()], writes=[tout.k(rc)])
            pa.free()
        for rc in range(TOK // 128):
            B_.dma(xout[rc * 128:(rc + 1) * 128,
                        g * GRP * 128:(g + 1) * GRP * 128],
                   tout.ap[:, rc], reads=[tout.k(rc)], writes=["xout_done"])
    tin.free()
    tout.free()



def ph_l0_mixer(B_, G, xn_dram, y_dram, xn_tag):
    G.load_sel16(B_)
    sub = MIXER_SUB[0]
    xn = load_xn(B_, xn_dram, xn_tag)
    xn_keys = [xn.k(c) for c in range(DC)]
    dtr = B_.arena.alloc([128, 2, TOK], BF16)
    bcsb = B_.arena.alloc([16, TOK], BF16)

    def evac_xp(fi, half, pa, reads):
        hsl = slice(half * 2048, (half + 1) * 2048)
        if fi < 2:
            B_.emit("vector", lambda e, fi=fi, hsl=hsl, pa=pa: e.tensor_copy(
                dtr.ap[:, fi, hsl], pa.ap[:, 0:2048]),
                reads=reads, writes=[dtr.k(fi)])
        else:
            B_.emit("vector", lambda e, hsl=hsl, pa=pa: e.tensor_copy(
                bcsb.ap[:, hsl], pa.ap[0:16, 0:2048]),
                reads=reads, writes=[bcsb.k(half)])
    if sub != 24:
        proj_resident(B_, "w_xproj_s", 3, DC, xn.ap, xn_keys, evac_xp)
    if sub <= 1:
        zt = B_.arena.alloc([128, TOK], F32)
        for c in range(DC):
            if c < 2:
                B_.emit("vector", lambda e, c=c: e.tensor_copy(
                    zt.ap[:], dtr.ap[:, c]), reads=[dtr.k(c)], writes=[zt.k()])
            else:
                B_.emit("vector", lambda e: e.memset(zt.ap[:], 0.0),
                        writes=[zt.k()])
            B_.dma(y_dram[c], zt.ap[:], reads=[zt.k()], writes=[f"l0_y/{c}"])
        zt.free()
        xn.free()
        dtr.free()
        bcsb.free()
        return

    delta_dram = B_.dscr("l0_delta", [DC, 128, TOK], BF16)
    u_dram = B_.dscr("l0_u", [DC, 128, TOK], BF16)
    dl_t = B_.arena.alloc([128, TOK], BF16)
    u_t = B_.arena.alloc([128, TOK], BF16)
    ab_t = B_.arena.alloc([128, 2048], F32)
    lg_t = B_.arena.alloc([128, 2048], F32)
    mz_t = B_.arena.alloc([128, 2048], F32)

    def evac_dt(fi, half, pa, reads):
        c = fi
        hsl = slice(half * 2048, (half + 1) * 2048)
        bap, bk = G.vec("b_dt_c", c)
        if MIXER_SUB[0] in (21, 24):
            B_.emit("vector", lambda e, pa=pa, hsl=hsl: e.tensor_copy(
                dl_t.ap[:, hsl], pa.ap[:, 0:2048]),
                reads=reads, writes=[dl_t.k()])
            if half == 1:
                B_.dma(delta_dram[c], dl_t.ap[:], reads=[dl_t.k()],
                       writes=[f"l0_delta/{c}"])
                B_.dma(u_dram[c], dl_t.ap[:], reads=[dl_t.k()],
                       writes=[f"l0_u/{c}"])
            return
        if MIXER_SUB[0] == 22:
            B_.emit("scalar", lambda e, pa=pa, bap=bap: e.activation(
                ab_t.ap[:], pa.ap[:, 0:2048], AF.Abs, bias=bap),
                reads=reads + [bk], writes=[ab_t.k()])
            B_.emit("scalar", lambda e: e.activation(
                ab_t.ap[:], ab_t.ap[:], AF.Exp, scale=-1.0),
                reads=[ab_t.k()], writes=[ab_t.k()])
            B_.emit("scalar", lambda e: e.activation(
                lg_t.ap[:], ab_t.ap[:], AF.Ln, bias=1.0),
                reads=[ab_t.k()], writes=[lg_t.k()])
            B_.emit("vector", lambda e, pa=pa, bap=bap: e.tensor_scalar(
                mz_t.ap[:], pa.ap[:, 0:2048], bap, 0.0, ALU.add, ALU.max),
                reads=reads + [bk], writes=[mz_t.k()])
            B_.emit("vector", lambda e, hsl=hsl: e.tensor_add(
                dl_t.ap[:, hsl], mz_t.ap[:], lg_t.ap[:]),
                reads=[mz_t.k(), lg_t.k()], writes=[dl_t.k()])
            if half == 1:
                B_.dma(delta_dram[c], dl_t.ap[:], reads=[dl_t.k()],
                       writes=[f"l0_delta/{c}"])
                B_.dma(u_dram[c], dl_t.ap[:], reads=[dl_t.k()],
                       writes=[f"l0_u/{c}"])
            return
        ev_abs = B_.emit("scalar", lambda e, pa=pa, bap=bap: e.activation(
            ab_t.ap[:], pa.ap[:, 0:2048], AF.Abs, bias=bap),
            reads=reads + [bk], writes=[ab_t.k()])
        B_.emit("scalar", lambda e: e.activation(
            ab_t.ap[:], ab_t.ap[:], AF.Exp, scale=-1.0),
            reads=[ab_t.k()], writes=[ab_t.k()])
        B_.emit("scalar", lambda e: e.activation(
            lg_t.ap[:], ab_t.ap[:], AF.Ln, bias=1.0),
            reads=[ab_t.k()], writes=[lg_t.k()])
        B_.emit("vector", lambda e, pa=pa, bap=bap: e.tensor_scalar(
            mz_t.ap[:], pa.ap[:, 0:2048], bap, 0.0, ALU.add, ALU.max),
            reads=reads + [bk], writes=[mz_t.k()], extra_waits=[ev_abs])
        B_.emit("vector", lambda e, hsl=hsl: e.tensor_add(
            dl_t.ap[:, hsl], mz_t.ap[:], lg_t.ap[:]),
            reads=[mz_t.k(), lg_t.k()], writes=[dl_t.k()])
        B_.emit("vector", lambda e, hsl=hsl, c=c: e.tensor_mul(
            u_t.ap[:, hsl], dl_t.ap[:, hsl], xn.ap[:, c, hsl]),
            reads=[dl_t.k(), xn.k(c)], writes=[u_t.k()])
        if half == 1:
            B_.dma(delta_dram[c], dl_t.ap[:], reads=[dl_t.k()],
                   writes=[f"l0_delta/{c}"])
            B_.dma(u_dram[c], u_t.ap[:], reads=[u_t.k()],
                   writes=[f"l0_u/{c}"])
    if MIXER_SUB[0] in (23, 24):
        proj_resident(B_, "w_dt_s", DC, 2, xn.ap, [xn.k(0), xn.k(1)], evac_dt)
    else:
        proj_resident(B_, "w_dt_s", DC, 2, dtr.ap, [dtr.k(0), dtr.k(1)], evac_dt)
    for a in (dl_t, u_t, ab_t, lg_t, mz_t):
        a.free()
    xn.free()
    dtr.free()
    if sub <= 2:
        zt = B_.arena.alloc([128, TOK], F32)
        zb2 = B_.arena.alloc([128, TOK], BF16)
        for c in range(DC):
            B_.dma(zb2.ap[:], delta_dram[c], reads=[f"l0_delta/{c}"],
                   writes=[zb2.k()])
            B_.emit("vector", lambda e: e.tensor_copy(zt.ap[:], zb2.ap[:]),
                    reads=[zb2.k()], writes=[zt.k()])
            B_.dma(y_dram[c], zt.ap[:], reads=[zt.k()], writes=[f"l0_y/{c}"])
        zt.free()
        zb2.free()
        bcsb.free()
        return

    dsl = B_.arena.alloc([128, TOK], BF16)
    usl = B_.arena.alloc([128, TOK], BF16)
    xsl = B_.arena.alloc([128, TOK], BF16)
    ysl = B_.arena.alloc([128, TOK], F32)
    dA = B_.arena.alloc([128, 2048], F32)
    dBu = B_.arena.alloc([128, 2048], F32)
    hsc = B_.arena.alloc([128, 2048], F32)
    tmul = B_.arena.alloc([128, 2048], F32)
    for sl in range(DC):
        if sl >= SCAN_SLABS[0]:
            B_.emit("vector", lambda e: e.memset(ysl.ap[:], 0.0),
                    writes=[ysl.k()])
            B_.dma(y_dram[sl], ysl.ap[:], reads=[ysl.k()],
                   writes=[f"l0_y/{sl}"])
            continue
        B_.dma(dsl.ap[:], delta_dram[sl], reads=[f"l0_delta/{sl}"],
               writes=[dsl.k()])
        B_.dma(usl.ap[:], u_dram[sl], reads=[f"l0_u/{sl}"], writes=[usl.k()])
        B_.dma(xsl.ap[:], xn_dram[sl], reads=[f"{xn_tag}/{sl}"],
               writes=[xsl.k()])
        dap, dk = G.vec("d_param_c", sl)
        B_.emit("vector", lambda e, dap=dap: e.tensor_scalar(
            ysl.ap[:], xsl.ap[:], dap, None, ALU.mult),
            reads=[xsl.k(), dk], writes=[ysl.k()])
        for b in range(B):
            bsl = slice(b * 2048, (b + 1) * 2048)
            for n in range(N_SSM):
                paB = B_.psum(4)
                paC = B_.psum(4)

                def bcfn(e, b=b, n=n, paB=paB, paC=paC):
                    last = None
                    for q in range(4):
                        qs = slice(b * 2048 + q * 512,
                                   b * 2048 + (q + 1) * 512)
                        last = e.matmul(paB.ap[:, q * 512:(q + 1) * 512],
                                        G.sel16[:, n * 128:(n + 1) * 128],
                                        bcsb.ap[:, qs], start=True, stop=True)
                        last = e.matmul(paC.ap[:, q * 512:(q + 1) * 512],
                                        G.sel16[:, (8 + n) * 128:(9 + n) * 128],
                                        bcsb.ap[:, qs], start=True, stop=True)
                    return last
                B_.emit("tensor", bcfn,
                        reads=[bcsb.k(0), bcsb.k(1), G.b_sel16.k()],
                        writes=[paB.k(), paC.k()])
                B_.emit("scalar", lambda e, sl=sl, n=n, bsl=bsl: e.activation(
                    dA.ap[:], dsl.ap[:, bsl], AF.Exp,
                    scale=G.A[:, sl * 8 + n:sl * 8 + n + 1]),
                    reads=[dsl.k(), G.b_A.k()], writes=[dA.k()])
                B_.emit("vector", lambda e, bsl=bsl, paB=paB: e.tensor_mul(
                    dBu.ap[:], usl.ap[:, bsl], paB.ap[:, 0:2048]),
                    reads=[usl.k(), paB.k()], writes=[dBu.k()])
                B_.emit("vector", lambda e: e.tensor_tensor_scan(
                    hsc.ap[:], dA.ap[:], dBu.ap[:], 0.0, ALU.mult, ALU.add),
                    reads=[dA.k(), dBu.k()], writes=[hsc.k()])
                B_.emit("vector", lambda e, paC=paC: e.tensor_mul(
                    tmul.ap[:], hsc.ap[:], paC.ap[:, 0:2048]),
                    reads=[hsc.k(), paC.k()], writes=[tmul.k()])
                B_.emit("gpsimd", lambda e, bsl=bsl: e.tensor_add(
                    ysl.ap[:, bsl], ysl.ap[:, bsl], tmul.ap[:]),
                    reads=[tmul.k(), ysl.k()], writes=[ysl.k()])
                paB.free()
                paC.free()
        B_.dma(y_dram[sl], ysl.ap[:], reads=[ysl.k()], writes=[f"l0_y/{sl}"])
    for a in (dsl, usl, xsl, ysl, dA, dBu, hsc, tmul, bcsb):
        a.free()



def ph_ffn24(B_, G, xn_dram, y_dram, xn_tag):
    xn = load_xn(B_, xn_dram, xn_tag)
    xn_keys = [xn.k(c) for c in range(DC)]
    hsp_dram = B_.dscr("l0_hsp", [FFC, 128, TOK], BF16)
    wgd = B_.dram["bf_w_gate_s"]
    wud = B_.dram["bf_w_up_s"]
    hbuf = [B_.arena.alloc([128, 2048], BF16) for _ in range(4)]
    absb = [B_.arena.alloc([128, 2048], BF16) for _ in range(4)]
    q_t = B_.arena.alloc([128, 2048], BF16)
    s_t = B_.arena.alloc([128, 2048], BF16)
    msk = B_.arena.alloc([128, 2048], BF16)
    wgb = [B_.arena.alloc([128, DC, 128], BF16) for _ in range(2)]
    wub = [B_.arena.alloc([128, DC, 128], BF16) for _ in range(2)]

    def one_mm(wap, pa, rhs_off):
        def mmfn(e):
            last = None
            for kc in range(DC):
                for t4 in range(4):
                    last = e.matmul(
                        pa.ap[:, t4 * 512:(t4 + 1) * 512], wap[:, kc],
                        xn.ap[:, kc, rhs_off + t4 * 512:
                              rhs_off + (t4 + 1) * 512],
                        start=(kc == 0), stop=(kc == DC - 1))
            return last
        return mmfn

    for gc in range(16):
        for half in range(2):
            hsl = slice(half * 2048, (half + 1) * 2048)
            for j in range(4):
                cp = 16 * j + gc
                par = j % 2
                wg, wu = wgb[par], wub[par]
                B_.dma(wg.ap[:], wgd[cp], reads=[f"bfw_w_gate_s/{cp}"],
                       writes=[wg.k()], queue="scalar")
                B_.dma(wu.ap[:], wud[cp], reads=[f"bfw_w_up_s/{cp}"],
                       writes=[wu.k()], queue="scalar")
                pa_g = B_.psum(4)
                B_.emit("tensor", one_mm(wg.ap, pa_g, half * 2048),
                        reads=[wg.k()] + xn_keys, writes=[pa_g.k()])
                B_.emit("scalar", lambda e, pa=pa_g, j=j:
                        e.activation(hbuf[j].ap[:], pa.ap[:, 0:2048],
                                     AF.Silu),
                        reads=[pa_g.k()], writes=[hbuf[j].k()])
                pa_g.free()
                pa_u = B_.psum(4)
                B_.emit("tensor", one_mm(wu.ap, pa_u, half * 2048),
                        reads=[wu.k()] + xn_keys, writes=[pa_u.k()])
                B_.emit("vector", lambda e, pa=pa_u, j=j:
                        e.tensor_mul(hbuf[j].ap[:], hbuf[j].ap[:],
                                     pa.ap[:, 0:2048]),
                        reads=[pa_u.k(), hbuf[j].k()], writes=[hbuf[j].k()])
                pa_u.free()
            for j in range(4):
                B_.emit("vector", lambda e, j=j: e.tensor_scalar(
                    absb[j].ap[:].bitcast(U16), hbuf[j].ap[:].bitcast(U16),
                    0x7FFF, None, ALU.bitwise_and),
                    reads=[hbuf[j].k()], writes=[absb[j].k()])
            B_.emit("vector", lambda e: e.tensor_tensor(
                q_t.ap[:], absb[0].ap[:], absb[1].ap[:], ALU.min),
                reads=[absb[0].k(), absb[1].k()], writes=[q_t.k()])
            B_.emit("vector", lambda e: e.tensor_tensor(
                s_t.ap[:], absb[2].ap[:], absb[3].ap[:], ALU.min),
                reads=[absb[2].k(), absb[3].k()], writes=[s_t.k()])
            B_.emit("vector", lambda e: e.tensor_max(
                absb[0].ap[:], absb[0].ap[:], absb[1].ap[:]),
                reads=[absb[0].k(), absb[1].k()], writes=[absb[0].k()])
            B_.emit("vector", lambda e: e.tensor_max(
                absb[2].ap[:], absb[2].ap[:], absb[3].ap[:]),
                reads=[absb[2].k(), absb[3].k()], writes=[absb[2].k()])
            B_.emit("vector", lambda e: e.tensor_tensor(
                absb[0].ap[:], absb[0].ap[:], absb[2].ap[:], ALU.min),
                reads=[absb[0].k(), absb[2].k()], writes=[absb[0].k()])
            B_.emit("vector", lambda e: e.tensor_max(
                q_t.ap[:], q_t.ap[:], s_t.ap[:]),
                reads=[q_t.k(), s_t.k()], writes=[q_t.k()])
            B_.emit("vector", lambda e: e.tensor_max(
                q_t.ap[:], absb[0].ap[:], q_t.ap[:]),
                reads=[absb[0].k(), q_t.k()], writes=[q_t.k()])
            for j in range(4):
                cp = 16 * j + gc
                B_.emit("vector", lambda e, j=j: e.tensor_scalar(
                    msk.ap[:].bitcast(U16), hbuf[j].ap[:].bitcast(U16),
                    0x7FFF, None, ALU.bitwise_and),
                    reads=[hbuf[j].k()], writes=[msk.k()])
                B_.emit("vector", lambda e: e.tensor_tensor(
                    msk.ap[:], msk.ap[:], q_t.ap[:], ALU.is_ge),
                    reads=[msk.k(), q_t.k()], writes=[msk.k()])
                B_.emit("gpsimd", lambda e, j=j: e.tensor_mul(
                    hbuf[j].ap[:], hbuf[j].ap[:], msk.ap[:]),
                    reads=[msk.k(), hbuf[j].k()], writes=[hbuf[j].k()])
                B_.dma(hsp_dram[cp, :, hsl], hbuf[j].ap[:],
                       reads=[hbuf[j].k()], writes=[f"l0_hsp/{cp}"])
    xn.free()
    for a in hbuf + absb + [q_t, s_t, msk] + wgb + wub:
        a.free()

    wdd = B_.dram["bf_w_down_s"]
    wdh = [B_.arena.alloc([128, FFC, 128], BF16) for _ in range(8)]
    hsp_t = [B_.arena.alloc([128, 8, 512], BF16) for _ in range(2)]
    yev = [B_.arena.alloc([128, 512], F32) for _ in range(2)]
    for dh2 in range(2):
        for i in range(8):
            dc = dh2 * 8 + i
            B_.dma(wdh[i].ap[:], wdd[dc], reads=[f"bfw_w_down_s/{dc}"],
                   writes=[wdh[i].k()], queue="scalar")
        for ttn in range(8):
            pa = B_.psum(8)
            for kg in range(8):
                par = kg % 2
                ht = hsp_t[par]
                B_.dma(ht.ap[:],
                       hsp_dram[kg * 8:(kg + 1) * 8, :,
                                ttn * 512:(ttn + 1) * 512]
                       .rearrange("a p t -> p a t"),
                       reads=[f"l0_hsp/{kg * 8 + i2}" for i2 in range(8)],
                       writes=[ht.k()])

                def mmfn(e, kg=kg, ht=ht, pa=pa):
                    last = None
                    for i2 in range(8):
                        kc = kg * 8 + i2
                        for i in range(8):
                            last = e.matmul(pa.ap[:, i * 512:(i + 1) * 512],
                                            wdh[i].ap[:, kc], ht.ap[:, i2],
                                            start=(kc == 0),
                                            stop=(kc == FFC - 1))
                    return last
                B_.emit("tensor", mmfn,
                        reads=[ht.k()] + [wdh[i].k() for i in range(8)],
                        writes=[pa.k()] if kg in (0, 7) else [])
            for i in range(8):
                dc = dh2 * 8 + i
                par = i % 2
                B_.emit("vector", lambda e, i=i, pa=pa, par=par: e.tensor_copy(
                    yev[par].ap[:], pa.ap[:, i * 512:(i + 1) * 512]),
                    reads=[pa.k()], writes=[yev[par].k()])
                B_.dma(y_dram[dc, :, ttn * 512:(ttn + 1) * 512],
                       yev[par].ap[:], reads=[yev[par].k()],
                       writes=[f"l0ffn_y/{dc}"])
            pa.free()
    for a in wdh + hsp_t + yev:
        a.free()



def ph_mha(B_, G, xn_dram, y_dram, pfx, wnames, bias_names=None, causal=False,
           inner_resid=False, xn_tag="xn", y_tag=None, causal_sb=None):
    xn = load_xn(B_, xn_dram, xn_tag)
    xn_keys = [xn.k(c) for c in range(DC)]
    qd = B_.dscr(pfx + "_q", [DC, 128, TOK], BF16)
    kd = B_.dscr(pfx + "_k", [DC, 128, TOK], BF16)
    vd = B_.dscr(pfx + "_v", [DC, 128, TOK], BF16)
    negm = B_.arena.alloc([128, 96], F32)
    sq_t = B_.arena.alloc([128, 2048], BF16)
    red_t = B_.arena.alloc([128, 1], F32)
    out_t = [B_.arena.alloc([128, 2048], BF16) for _ in range(2)]
    B_.emit("vector", lambda e: e.memset(negm.ap[:], 0.0),
            writes=[negm.k(i) for i in range(96)])

    wq, wk, wv, wo = wnames
    bq = bk_ = bv = bo = None
    if bias_names:
        bq, bk_, bv, bo = bias_names

    def evac_qkv(which, out_dram, bias_nm, base_col):
        def evac(fi, half, pa, reads):
            c = fi
            par = half
            hsl = slice(half * 2048, (half + 1) * 2048)
            ot = out_t[par]
            if bias_nm is not None:
                bap, bk2 = G.vec(bias_nm, c)
                B_.emit("scalar", lambda e, pa=pa, bap=bap, ot=ot:
                        e.activation(ot.ap[:], pa.ap[:, 0:2048], AF.Identity,
                                     bias=bap),
                        reads=reads + [bk2], writes=[ot.k()])
            else:
                B_.emit("scalar", lambda e, pa=pa, ot=ot: e.activation(
                    ot.ap[:], pa.ap[:, 0:2048], AF.Copy),
                    reads=reads, writes=[ot.k()])
            B_.dma(out_dram[c, :, hsl], ot.ap[:], reads=[ot.k()],
                   writes=[f"{pfx}_{which}/{c}"])
            if base_col >= 0:
                B_.emit("scalar", lambda e, ot=ot: e.activation(
                    sq_t.ap[:], ot.ap[:], AF.Square),
                    reads=[ot.k()], writes=[sq_t.k()])
                pn = B_.psum(4)

                def nfn(e, pn=pn):
                    last = None
                    for t4 in range(4):
                        last = e.matmul(pn.ap[:, t4 * 512:(t4 + 1) * 512],
                                        G.ones_bf[:],
                                        sq_t.ap[:, t4 * 512:(t4 + 1) * 512],
                                        start=True, stop=True)
                    return last
                B_.emit("tensor", nfn, reads=[sq_t.k(), G.b_ones.k()],
                        writes=[pn.k()])
                col = base_col + half * 16 + c
                B_.emit("vector", lambda e, pn=pn: e.reduce_max(
                    red_t.ap[:], pn.ap[:, 0:2048], axis=AX.X),
                    reads=[pn.k()], writes=[red_t.k()])
                B_.emit("vector", lambda e, col=col: e.tensor_max(
                    negm.ap[:, col:col + 1], negm.ap[:, col:col + 1],
                    red_t.ap[:]),
                    reads=[red_t.k(), negm.k(col)], writes=[negm.k(col)])
                pn.free()
        return evac

    proj_resident(B_, wq, DC, DC, xn.ap, xn_keys, evac_qkv("q", qd, bq, 0))
    proj_resident(B_, wk, DC, DC, xn.ap, xn_keys, evac_qkv("k", kd, bk_, 32))
    proj_resident(B_, wv, DC, DC, xn.ap, xn_keys, evac_qkv("v", vd, bv, -1))
    sq_t.free()
    red_t.free()
    for a in out_t:
        a.free()
    xn.free()

    for b in range(B):
        for h in range(NHEAD):
            i = b * 16 + h
            cq, ck, co = i, 32 + i, 64 + i
            B_.emit("vector", lambda e, cq=cq, ck=ck, co=co: e.tensor_mul(
                negm.ap[:, co:co + 1], negm.ap[:, cq:cq + 1],
                negm.ap[:, ck:ck + 1]),
                reads=[negm.k(cq), negm.k(ck)], writes=[negm.k(co)])
            t30_ap, t30_k = G.cbias(1e-30)
            B_.emit("scalar", lambda e, co=co, t30_ap=t30_ap: e.activation(
                negm.ap[:, co:co + 1], negm.ap[:, co:co + 1], AF.Ln,
                bias=t30_ap),
                reads=[negm.k(co), t30_k], writes=[negm.k(co)])
            B_.emit("scalar", lambda e, co=co: e.activation(
                negm.ap[:, co:co + 1], negm.ap[:, co:co + 1], AF.Exp,
                scale=0.5),
                reads=[negm.k(co)], writes=[negm.k(co)])
            B_.emit("vector", lambda e, co=co: e.tensor_scalar(
                negm.ap[:, co:co + 1], negm.ap[:, co:co + 1], -RSQ_DH, None,
                ALU.mult),
                reads=[negm.k(co)], writes=[negm.k(co)])

    ctx = B_.arena.alloc([128, DC, TOK], BF16)
    qh_t = [B_.arena.alloc([128, 2048], BF16) for _ in range(2)]
    kh_t = [B_.arena.alloc([128, 2048], BF16) for _ in range(2)]
    vh_t = [B_.arena.alloc([128, 2048], BF16) for _ in range(2)]
    vtm = [B_.arena.alloc([128, 16, 128], BF16) for _ in range(2)]
    pt = [B_.arena.alloc([128, 1024], BF16) for _ in range(2)]
    rec_t = B_.arena.alloc([128, 1024], F32)
    for bh in range(B * NHEAD):
        b, h = bh // NHEAD, bh % NHEAD
        hp = bh % 2
        bsl = slice(b * 2048, (b + 1) * 2048)
        co = 64 + b * 16 + h
        qh, kh, vh, vt = qh_t[hp], kh_t[hp], vh_t[hp], vtm[hp]
        B_.dma(qh.ap[:], qd[h, :, bsl], reads=[f"{pfx}_q/{h}"],
               writes=[qh.k()])
        B_.dma(kh.ap[:], kd[h, :, bsl], reads=[f"{pfx}_k/{h}"],
               writes=[kh.k()])
        B_.dma(vh.ap[:], vd[h, :, bsl], reads=[f"{pfx}_v/{h}"],
               writes=[vh.k()])
        pv = B_.psum(2)

        def tv(e, pv=pv, vh=vh):
            last = None
            for kc in range(16):
                last = e.transpose(
                    pv.ap.bitcast(BF16)[:, kc * 128:(kc + 1) * 128],
                    vh.ap[:, kc * 128:(kc + 1) * 128], G.identb[:])
            return last
        B_.emit("tensor", tv, reads=[vh.k(), G.b_identb.k()],
                writes=[pv.k()])
        B_.emit("vector", lambda e, pv=pv, vt=vt: e.tensor_copy(
            vt.ap[:], pv.ap.bitcast(BF16)[:, 0:2048]),
            reads=[pv.k()], writes=[vt.k()])
        pv.free()
        for qh2 in range(2):
            Q0 = qh2 * 1024
            pa_rs = B_.psum(2)
            pa_av = B_.psum(2)
            kcl = list(range(16)) if not causal else \
                list(range((Q0 + 1024 + 127) // 128))
            pst = [B_.psum(2), B_.psum(2)]
            nk = len(kcl)

            def emit_st(ki):
                kc = kcl[ki]
                par = ki % 2
                pa = pst[par]

                def stfn(e, kc=kc, pa=pa, qh2=qh2, qh=qh, kh=kh):
                    last = None
                    for qq in range(2):
                        last = e.matmul(
                            pa.ap[:, qq * 512:(qq + 1) * 512],
                            kh.ap[:, kc * 128:(kc + 1) * 128],
                            qh.ap[:, qh2 * 1024 + qq * 512:
                                  qh2 * 1024 + (qq + 1) * 512],
                            start=True, stop=True)
                    return last
                B_.emit("tensor", stfn, reads=[kh.k(), qh.k()],
                        writes=[pa.k()])
                B_.emit("scalar", lambda e, pa=pa, par=par, co=co:
                        e.activation(pt[par].ap[:], pa.ap[:, 0:1024],
                                     AF.Exp, bias=negm.ap[:, co:co + 1],
                                     scale=RSQ_DH),
                        reads=[pa.k(), negm.k(co)],
                        writes=[pt[par].k()])
                K0 = kc * 128
                if causal and K0 + 127 > Q0:
                    off = 1920 - (K0 - Q0)
                    B_.emit("vector", lambda e, par=par, off=off:
                            e.tensor_mul(pt[par].ap[:], pt[par].ap[:],
                                         causal_sb.ap[:, off:off + 1024]),
                            reads=[pt[par].k(), causal_sb.k()],
                            writes=[pt[par].k()])

            def emit_rs_av(ki):
                kc = kcl[ki]
                par = ki % 2
                first = (ki == 0)
                last_k = (ki == nk - 1)

                def rsfn(e, par=par, pa=pa_rs, first=first, last_k=last_k):
                    last = None
                    for qq in range(2):
                        last = e.matmul(pa.ap[:, qq * 512:(qq + 1) * 512],
                                        G.ones_bf[:],
                                        pt[par].ap[:, qq * 512:(qq + 1) * 512],
                                        start=first, stop=last_k)
                    return last
                B_.emit("tensor", rsfn, reads=[pt[par].k(), G.b_ones.k()],
                        writes=[pa_rs.k()] if (first or last_k) else [])

                def avfn(e, par=par, pa=pa_av, kc=kc, first=first,
                         last_k=last_k, vt=vt):
                    last = None
                    for qq in range(2):
                        last = e.matmul(pa.ap[:, qq * 512:(qq + 1) * 512],
                                        vt.ap[:, kc],
                                        pt[par].ap[:, qq * 512:(qq + 1) * 512],
                                        start=first, stop=last_k)
                    return last
                B_.emit("tensor", avfn, reads=[pt[par].k(), vt.k()],
                        writes=[pa_av.k()] if (first or last_k) else [])

            emit_st(0)
            for ki in range(nk):
                if ki + 1 < nk:
                    emit_st(ki + 1)
                emit_rs_av(ki)
            B_.emit("vector", lambda e, pa=pa_rs: e.reciprocal(
                rec_t.ap[:], pa.ap[:, 0:1024]),
                reads=[pa_rs.k()], writes=[rec_t.k()])
            B_.emit("vector", lambda e, pa=pa_av, h=h, b=b, Q0=Q0:
                    e.tensor_mul(
                        ctx.ap[:, h, b * 2048 + Q0:b * 2048 + Q0 + 1024],
                        pa.ap[:, 0:1024], rec_t.ap[:]),
                    reads=[pa_av.k(), rec_t.k()], writes=[ctx.k(h)])
            pa_rs.free()
            pa_av.free()
            pst[0].free()
            pst[1].free()
    for a in qh_t + kh_t + vh_t + vtm + pt + [rec_t]:
        a.free()
    negm.free()

    yev2 = [B_.arena.alloc([128, 2048], F32) for _ in range(2)]
    rin = [B_.arena.alloc([128, 2048], BF16) for _ in range(2)] \
        if inner_resid else None
    ctx_keys = [ctx.k(h) for h in range(DC)]

    def evac_o(fi, half, pa, reads):
        c = fi
        par = half
        hsl = slice(half * 2048, (half + 1) * 2048)
        ye = yev2[par]
        if inner_resid:
            B_.dma(rin[par].ap[:], xn_dram[c, :, hsl],
                   reads=[f"{xn_tag}/{c}"], writes=[rin[par].k()])
            B_.emit("vector", lambda e, pa=pa, par=par, ye=ye: e.tensor_add(
                ye.ap[:], pa.ap[:, 0:2048], rin[par].ap[:]),
                reads=reads + [rin[par].k()], writes=[ye.k()])
        elif bo is not None:
            bap, bk2 = G.vec(bo, c)
            B_.emit("scalar", lambda e, pa=pa, bap=bap, ye=ye: e.activation(
                ye.ap[:], pa.ap[:, 0:2048], AF.Identity, bias=bap),
                reads=reads + [bk2], writes=[ye.k()])
        else:
            B_.emit("vector", lambda e, pa=pa, ye=ye: e.tensor_copy(
                ye.ap[:], pa.ap[:, 0:2048]),
                reads=reads, writes=[ye.k()])
        B_.dma(y_dram[c, :, hsl], ye.ap[:], reads=[ye.k()],
               writes=[f"{y_tag}/{c}"])
    proj_resident(B_, wo, DC, DC, ctx.ap, ctx_keys, evac_o)
    for a in yev2 + (rin or []):
        a.free()
    ctx.free()



STOP_AFTER = [99]
MIXER_SUB = [99]
SCAN_SLABS = [99]


def emit_block0(B_, G, x0):
    xres0 = B_.dscr("xres0", [DC, 128, TOK], F32)
    xn1 = B_.dscr("xn1", [DC, 128, TOK], BF16)
    y1 = B_.dscr("y1", [DC, 128, TOK], F32)
    x1 = B_.dram["x1"]
    ph_entry(B_, G, x0, xres0)
    if STOP_AFTER[0] <= 1:
        for c in range(DC):
            st = B_.arena.alloc([128, TOK], F32)
            B_.dma(st.ap[:], xres0[c], reads=[f"xres0/{c}"], writes=[st.k()])
            B_.dma(x1[c], st.ap[:], reads=[st.k()], writes=[f"x1/{c}"])
            st.free()
        return
    ph_ln(B_, G, xres0, xn_dram=xn1, z_tag="xres0", out_tag="xn1")
    if STOP_AFTER[0] <= 2:
        cv = B_.arena.alloc([128, TOK], BF16)
        cf = B_.arena.alloc([128, TOK], F32)
        for c in range(DC):
            B_.dma(cv.ap[:], xn1[c], reads=[f"xn1/{c}"], writes=[cv.k()])
            B_.emit("vector", lambda e: e.tensor_copy(cf.ap[:], cv.ap[:]),
                    reads=[cv.k()], writes=[cf.k()])
            B_.dma(x1[c], cf.ap[:], reads=[cf.k()], writes=[f"x1/{c}"])
        cv.free()
        cf.free()
        return
    ph_l0_mixer(B_, G, xn1, y1, "xn1")
    if STOP_AFTER[0] <= 3:
        for c in range(DC):
            st = B_.arena.alloc([128, TOK], F32)
            B_.dma(st.ap[:], y1[c], reads=[f"l0_y/{c}"], writes=[st.k()])
            B_.dma(x1[c], st.ap[:], reads=[st.k()], writes=[f"x1/{c}"])
            st.free()
        return
    ph_ln(B_, G, y1, resid_dram=xres0, out_dram=x1, z_tag="l0_y",
          out_tag="x1", resid_tag="xres0")


def emit_block1(B_, G):
    x1 = B_.dram["x1"]
    xn2 = B_.dscr("xn2", [DC, 128, TOK], BF16)
    y2 = B_.dscr("y2", [DC, 128, TOK], F32)
    x2 = B_.dram["x2"]
    ph_ln(B_, G, x1, xn_dram=xn2, z_tag="x1", out_tag="xn2")
    ph_ffn24(B_, G, xn2, y2, "xn2")
    ph_ln(B_, G, y2, resid_dram=x1, out_dram=x2, z_tag="l0ffn_y",
          out_tag="x2", resid_tag="x1")


def emit_block2(B_, G):
    x2 = B_.dram["x2"]
    xn3 = B_.dscr("xn3", [DC, 128, TOK], BF16)
    y3 = B_.dscr("y3", [DC, 128, TOK], F32)
    x3 = B_.dram["x3"]
    ph_ln(B_, G, x2, xn_dram=xn3, z_tag="x2", out_tag="xn3")
    ph_mha(B_, G, xn3, y3, "mha3", ("wq_s", "wk_s", "wv_s", "wo_s"),
           bias_names=("bq_c", "bk_c", "bv_c", "bo_c"), causal=False,
           xn_tag="xn3", y_tag="mha3_y")
    ph_ln(B_, G, y3, resid_dram=x2, out_dram=x3, z_tag="mha3_y",
          out_tag="x3", resid_tag="x2")


def emit_block3(B_, G):
    x3 = B_.dram["x3"]
    xn4 = B_.dscr("xn4", [DC, 128, TOK], BF16)
    x4 = B_.dram["x4"]
    ph_ln(B_, G, x3, xn_dram=xn4, z_tag="x3", out_tag="xn4")

    causal_sb = B_.arena.alloc([128, 3072], BF16)
    B_.dma(causal_sb.ap[:], B_.dram["causal_m"], writes=[causal_sb.k()])

    ixn1 = B_.dscr("ixn1", [DC, 128, TOK], BF16)
    x2a = B_.dscr("x2a", [DC, 128, TOK], BF16)
    x2a_f = B_.dscr("x2a_f", [DC, 128, TOK], F32)
    ixn2 = B_.dscr("ixn2", [DC, 128, TOK], BF16)
    hg = B_.dscr("hg", [FFC, 128, TOK], BF16)
    out2 = B_.dscr("out2", [DC, 128, TOK], F32)

    ph_ln(B_, G, xn4, xn_dram=ixn1, z_is_bf16=True, scale_nm="ln1s_c",
          bias_nm="ln1b_c", z_tag="xn4", out_tag="ixn1")
    ph_mha(B_, G, ixn1, x2a_f, "imha", ("iwq_s", "iwk_s", "iwv_s", "iwo_s"),
           bias_names=None, causal=True, inner_resid=False,
           xn_tag="ixn1", y_tag="ictx_y", causal_sb=causal_sb)
    ra = B_.arena.alloc([128, TOK], F32)
    rb = B_.arena.alloc([128, TOK], BF16)
    for c in range(DC):
        B_.dma(ra.ap[:], x2a_f[c], reads=[f"ictx_y/{c}"], writes=[ra.k()])
        B_.dma(rb.ap[:], xn4[c], reads=[f"xn4/{c}"], writes=[rb.k()])
        B_.emit("vector", lambda e: e.tensor_add(ra.ap[:], ra.ap[:], rb.ap[:]),
                reads=[ra.k(), rb.k()], writes=[ra.k()])
        B_.dma(x2a_f[c], ra.ap[:], reads=[ra.k()], writes=[f"x2a/{c}"])
    ra.free()
    rb.free()
    ph_ln(B_, G, x2a_f, xn_dram=ixn2, scale_nm="ln2s_c", bias_nm="ln2b_c",
          z_tag="x2a", out_tag="ixn2")

    ixn2_sb = load_xn(B_, ixn2, "ixn2")
    ixn2_keys = [ixn2_sb.k(c) for c in range(DC)]
    gel = [B_.arena.alloc([128, 2048], BF16) for _ in range(2)]

    def evac_ff1(fi, half, pa, reads):
        par = half
        hsl = slice(half * 2048, (half + 1) * 2048)
        B_.emit("scalar", lambda e, pa=pa, par=par: e.activation(
            gel[par].ap[:], pa.ap[:, 0:2048], AF.Gelu),
            reads=reads, writes=[gel[par].k()])
        B_.dma(hg[fi, :, hsl], gel[par].ap[:], reads=[gel[par].k()],
               writes=[f"hg/{fi}"])
    proj_resident(B_, "wff1_s", FFC, DC, ixn2_sb.ap, ixn2_keys, evac_ff1)
    for a in gel:
        a.free()
    ixn2_sb.free()

    wfd = B_.dram["bf_wff2_s"]
    wfh = [B_.arena.alloc([128, FFC, 128], BF16) for _ in range(8)]
    hg_t = [B_.arena.alloc([128, 8, 512], BF16) for _ in range(2)]
    o2ev = [B_.arena.alloc([128, 512], F32) for _ in range(2)]
    x2a_t = [B_.arena.alloc([128, 512], F32) for _ in range(2)]
    for dh2 in range(2):
        for i in range(8):
            dc = dh2 * 8 + i
            B_.dma(wfh[i].ap[:], wfd[dc], reads=[f"bfw_wff2_s/{dc}"],
                   writes=[wfh[i].k()], queue="scalar")
        for ttn in range(8):
            pa = B_.psum(8)
            for kg in range(8):
                par = kg % 2
                ht = hg_t[par]
                B_.dma(ht.ap[:],
                       hg[kg * 8:(kg + 1) * 8, :, ttn * 512:(ttn + 1) * 512]
                       .rearrange("a p t -> p a t"),
                       reads=[f"hg/{kg * 8 + i2}" for i2 in range(8)],
                       writes=[ht.k()])

                def mmfn(e, kg=kg, ht=ht, pa=pa):
                    last = None
                    for i2 in range(8):
                        kc = kg * 8 + i2
                        for i in range(8):
                            last = e.matmul(pa.ap[:, i * 512:(i + 1) * 512],
                                            wfh[i].ap[:, kc], ht.ap[:, i2],
                                            start=(kc == 0),
                                            stop=(kc == FFC - 1))
                    return last
                B_.emit("tensor", mmfn,
                        reads=[ht.k()] + [wfh[i].k() for i in range(8)],
                        writes=[pa.k()] if kg in (0, 7) else [])
            for i in range(8):
                dc = dh2 * 8 + i
                par = i % 2
                B_.dma(x2a_t[par].ap[:],
                       x2a_f[dc, :, ttn * 512:(ttn + 1) * 512],
                       reads=[f"x2a/{dc}"], writes=[x2a_t[par].k()])
                B_.emit("vector", lambda e, i=i, pa=pa, par=par: e.tensor_add(
                    o2ev[par].ap[:], pa.ap[:, i * 512:(i + 1) * 512],
                    x2a_t[par].ap[:]),
                    reads=[pa.k(), x2a_t[par].k()], writes=[o2ev[par].k()])
                B_.dma(out2[dc, :, ttn * 512:(ttn + 1) * 512],
                       o2ev[par].ap[:], reads=[o2ev[par].k()],
                       writes=[f"out2/{dc}"])
            pa.free()
    for a in wfh + hg_t + o2ev + x2a_t:
        a.free()
    causal_sb.free()

    xn4_sb = load_xn(B_, xn4, "xn4")
    xn4_keys = [xn4_sb.k(c) for c in range(DC)]
    y4 = B_.dscr("y4", [DC, 128, TOK], F32)
    lg = B_.arena.alloc([128, TOK], BF16)

    def evac_rt(fi, half, pa, reads):
        hsl = slice(half * 2048, (half + 1) * 2048)
        def cpf(e, pa=pa, hsl=hsl):
            last = None
            for prow in (0, 32, 64):
                last = e.tensor_copy(lg.ap[prow:prow + 1, hsl],
                                     pa.ap[prow:prow + 1, 0:2048])
            return last
        B_.emit("vector", cpf, reads=reads, writes=[lg.k()])
    proj_resident(B_, "w_router_s", 1, DC, xn4_sb.ap, xn4_keys, evac_rt)
    lr1 = B_.arena.alloc([128, 1, TOK], BF16)

    def evac_lr1(fi, half, pa, reads):
        hsl = slice(half * 2048, (half + 1) * 2048)
        B_.emit("vector", lambda e, pa=pa, hsl=hsl: e.tensor_copy(
            lr1.ap[:, 0, hsl], pa.ap[:, 0:2048]),
            reads=reads, writes=[lr1.k()])
    proj_resident(B_, "w_lrdown_s", 1, DC, xn4_sb.ap, xn4_keys, evac_lr1)
    xn4_sb.free()

    lf = [B_.arena.alloc([128, TOK], F32) for _ in range(3)]
    for r in range(3):
        base = 32 * r
        for half in range(2):
            pb = B_.psum(4)

            def bfn(e, base=base, half=half, pb=pb):
                last = None
                for t4 in range(4):
                    c0 = half * 2048 + t4 * 512
                    last = e.matmul(pb.ap[:, t4 * 512:(t4 + 1) * 512],
                                    G.ones_bf[base:base + 1, :],
                                    lg.ap[base:base + 1, c0:c0 + 512],
                                    start=True, stop=True)
                return last
            B_.emit("tensor", bfn, reads=[lg.k(), G.b_ones.k()],
                    writes=[pb.k()])
            B_.emit("vector", lambda e, pb=pb, r=r, half=half: e.tensor_copy(
                lf[r].ap[:, half * 2048:(half + 1) * 2048], pb.ap[:, 0:2048]),
                reads=[pb.k()], writes=[lf[r].k()])
            pb.free()

    m1b = B_.arena.alloc([128, TOK], U8)
    m2b = B_.arena.alloc([128, TOK], U8)
    g1 = B_.arena.alloc([128, TOK], BF16)
    g2 = B_.arena.alloc([128, TOK], BF16)
    B_.emit("vector", lambda e: e.tensor_tensor(
        m1b.ap[:], lf[1].ap[:], lf[0].ap[:], ALU.is_gt),
        reads=[lf[1].k(), lf[0].k()], writes=[m1b.k()])
    B_.emit("vector", lambda e: e.tensor_tensor(
        g1.ap[:], lf[2].ap[:], lf[0].ap[:], ALU.is_gt),
        reads=[lf[2].k(), lf[0].k()], writes=[g1.k()])
    B_.emit("vector", lambda e: e.tensor_tensor(
        g2.ap[:], lf[2].ap[:], lf[1].ap[:], ALU.is_gt),
        reads=[lf[2].k(), lf[1].k()], writes=[g2.k()])
    B_.emit("vector", lambda e: e.tensor_mul(
        m2b.ap[:], g1.ap[:], g2.ap[:]),
        reads=[g1.k(), g2.k()], writes=[m2b.k()])
    for a in lf:
        a.free()
    g1.free()
    g2.free()

    osel = [B_.arena.alloc([128, 2048], F32) for _ in range(2)]
    o2t = [B_.arena.alloc([128, 2048], F32) for _ in range(2)]
    o1t = [B_.arena.alloc([128, 2048], F32) for _ in range(2)]
    xn4_t = [B_.arena.alloc([128, 2048], BF16) for _ in range(2)]

    def evac_sel(fi, half, pa, reads):
        c = fi
        par = half
        hsl = slice(half * 2048, (half + 1) * 2048)
        sap, sk = G.vec("sbias_c", c)
        B_.dma(xn4_t[par].ap[:], xn4[c, :, hsl], reads=[f"xn4/{c}"],
               writes=[xn4_t[par].k()])
        B_.emit("vector", lambda e, par=par, sap=sap:
                e.tensor_scalar(osel[par].ap[:], xn4_t[par].ap[:], sap,
                                None, ALU.add),
                reads=[xn4_t[par].k(), sk], writes=[osel[par].k()])
        B_.emit("vector", lambda e, par=par, pa=pa:
                e.tensor_add(o1t[par].ap[:], pa.ap[:, 0:2048],
                             xn4_t[par].ap[:]),
                reads=reads + [xn4_t[par].k()], writes=[o1t[par].k()])
        B_.dma(o2t[par].ap[:], out2[c, :, hsl], reads=[f"out2/{c}"],
               writes=[o2t[par].k()])
        B_.emit("vector", lambda e, par=par, hsl=hsl: e.copy_predicated(
            osel[par].ap[:], m1b.ap[:, hsl], o1t[par].ap[:]),
            reads=[m1b.k(), o1t[par].k(), osel[par].k()],
            writes=[osel[par].k()])
        B_.emit("vector", lambda e, par=par, hsl=hsl: e.copy_predicated(
            osel[par].ap[:], m2b.ap[:, hsl], o2t[par].ap[:]),
            reads=[m2b.k(), o2t[par].k(), osel[par].k()],
            writes=[osel[par].k()])
        B_.dma(y4[c, :, hsl], osel[par].ap[:], reads=[osel[par].k()],
               writes=[f"y4/{c}"])
    proj_resident(B_, "w_lrup_s", DC, 1, lr1.ap, [lr1.k()], evac_sel)
    for a in osel + o2t + o1t + xn4_t:
        a.free()
    lr1.free()
    m1b.free()
    m2b.free()
    lg.free()

    ph_ln(B_, G, y4, resid_dram=x3, out_dram=x4, z_tag="y4",
          out_tag="x4", resid_tag="x3")



ALL_BF = list(BF_WEIGHTS.keys())
BLOCK_BF = {
    0: ["w_xproj_s", "w_dt_s"],
    1: ["w_gate_s", "w_up_s", "w_down_s"],
    2: ["wq_s", "wk_s", "wv_s", "wo_s"],
    3: ["iwq_s", "iwk_s", "iwv_s", "iwo_s", "wff1_s", "wff2_s",
        "w_router_s", "w_lrdown_s", "w_lrup_s"],
}


def build_program(blocks, fused_all=False):
    B_ = Bld()
    B_.mk_arena(200 * 1024)
    first, last = blocks[0], blocks[-1]
    bf_names = []
    for b in blocks:
        bf_names += BLOCK_BF[b]
    declare_weights(B_, bf_names)
    if first == 0:
        x0 = B_.din("x0", [TOK, D], F32)
    res_names = {0: "x1", 1: "x2", 2: "x3", 3: "x4"}
    for b in blocks:
        nm = res_names[b]
        if b == last and last != 3:
            B_.dout(nm, [DC, 128, TOK], F32)
        else:
            B_.dscr(nm, [DC, 128, TOK], F32)
    if first != 0:
        prev = res_names[first - 1]
        B_.din(prev, [DC, 128, TOK], F32)
    if last == 3:
        xout = B_.dout("xout", [TOK, D], F32)
    G = Globals(B_)
    ph_wcast(B_, bf_names)
    for b in blocks:
        if b == 0:
            emit_block0(B_, G, B_.dram["x0"])
        elif b == 1:
            emit_block1(B_, G)
        elif b == 2:
            emit_block2(B_, G)
        elif b == 3:
            emit_block3(B_, G)
    if last == 3:
        ph_exit(B_, G, B_.dram["x4"], xout, "x4")
    nc = B_.finalize(B_.outputs_done())
    return nc


_PROG_CACHE = {}


def _get_prog(key, blocks):
    if key not in _PROG_CACHE:
        _PROG_CACHE[key] = build_program(blocks)
    return _PROG_CACHE[key]


def kernel(**inputs):
    w = prep_weights(inputs)
    x_full = inputs["L0_attn_x"]
    launches = [[0, 1, 2, 3]]
    core_in = []
    for core in range(NCORES):
        xc = np.ascontiguousarray(
            x_full[core * B:(core + 1) * B].reshape(TOK, D))
        core_in.append(xc)
    state = [dict() for _ in range(NCORES)]
    res_names = {0: "x1", 1: "x2", 2: "x3", 3: "x4"}
    out = None
    for li, blocks in enumerate(launches):
        nc = _get_prog(tuple(blocks), blocks)
        in_maps = []
        for core in range(NCORES):
            m = dict(w)
            if blocks[0] == 0:
                m["x0"] = core_in[core]
            else:
                prev = res_names[blocks[0] - 1]
                m[prev] = state[core][prev]
            in_maps.append(m)
        res = run_bass_kernel_spmd(nc, in_maps, core_ids=list(range(NCORES)))
        for core in range(NCORES):
            r = res.results[core]
            for k, v in r.items():
                state[core][k] = np.asarray(v)
    out = np.empty((NCORES * B, S, D), dtype=np.float32)
    for core in range(NCORES):
        out[core * B:(core + 1) * B] = state[core]["xout"].reshape(B, S, D)
    return out
